# version 01
# baseline (speedup 1.0000x reference)
import contextlib
import math
import numpy as np
import ml_dtypes
import concourse.bass as bass
import concourse.mybir as mybir
from concourse.bass_utils import run_bass_kernel_spmd

F32 = mybir.dt.float32
BF16 = mybir.dt.bfloat16
U32 = mybir.dt.uint32
AF = mybir.ActivationFunctionType
ALU = mybir.AluOpType
AX = mybir.AxisListType

NCORES = 8
B, SEQ, DM = 4, 4096, 1024
T = 2048
NT = T // 128
NC = DM // 128
NH, DH = 16, 64
NG = 409
NE, HI = 8, 4096
CAP = 4096
EPS = 1e-5
SCALE_G = DH ** -0.5

BF = ml_dtypes.bfloat16

DEBUG = False

_QMAP = dict(pe="tensor", dve="vector", act="scalar", pool="gpsimd", sp="sync")


class Pr:

    def __init__(self, nc, es):
        self.nc = nc
        self.es = es
        self.q = {k: [] for k in _QMAP}
        self.sems = {}
        self.cnt = {}
        self.bufs = {}

    def sem(self, name):
        if name not in self.sems:
            self.sems[name] = self.es.enter_context(self.nc.semaphore(name))
            self.cnt[name] = 0
        return self.sems[name]

    def sbuf(self, name, shape, dtype):
        t = self.es.enter_context(self.nc.sbuf_tensor(name, shape, dtype))
        self.bufs[name] = t
        return t

    def psum(self, name, shape, dtype):
        t = self.es.enter_context(self.nc.psum_tensor(name, shape, dtype))
        self.bufs[name] = t
        return t

    def on(self, qn, fn, inc=None):
        if inc is not None:
            s, n = inc if isinstance(inc, tuple) else (inc, 1)
            h = self.sem(s)
            self.cnt[s] += n

            def wrapped(e, fn=fn, h=h, n=n):
                r = fn(e)
                r.then_inc(h, n)
            self.q[qn].append(wrapped)
        else:
            self.q[qn].append(fn)

    def wait(self, qn, s, minus=0):
        h = self.sem(s)
        v = self.cnt[s] - minus
        self.q[qn].append(lambda e, h=h, v=v: e.wait_ge(h, v))

    def wait_at(self, qn, s, v):
        h = self.sem(s)
        self.q[qn].append(lambda e, h=h, v=v: e.wait_ge(h, v))

    def val(self, s):
        self.sem(s)
        return self.cnt[s]

    def finalize(self):
        with self.nc.Block() as b:
            for qn, attr in _QMAP.items():
                ops = self.q[qn]
                if not ops:
                    continue

                def runner(e, ops=ops):
                    for f in ops:
                        f(e)
                getattr(b, attr)(runner)



def load_w_chunks(P, name, src_ap, dout, sem="dmap"):
    wb = P.sbuf(name, [128, NC * dout], BF16)
    P.on("pool", lambda e, wb=wb, src=src_ap, dout=dout: e.dma_start(
        out=wb[:, :].rearrange("p (c j) -> p c j", c=NC),
        in_=src.rearrange("(c p) j -> p c j", p=128)), inc=(sem, 16))
    return wb


def emit_ln(P, src_col, dst_col, x_sb, o32_sb, o16_sb, st, pfx,
            w_b=None, b_b=None, n_tiles=NT, ncols=DM, sem_suffix="", inc_last=None):
    sq = P.bufs[pfx + "_sq"]
    dvesem = "dve" + sem_suffix
    actsem = "act" + sem_suffix
    for i in range(n_tiles):
        xs = x_sb[:, src_col + i * ncols: src_col + (i + 1) * ncols]
        P.on("dve", lambda e, xs=xs, st=st, i=i: e.reduce_sum(
            st[:, 4 * i:4 * i + 1], xs, axis=AX.X))
        P.on("dve", lambda e, xs=xs, sq=sq: e.tensor_tensor(sq[:, :ncols], xs, xs, ALU.mult))
        last = (i == n_tiles - 1)
        P.on("dve", lambda e, sq=sq, st=st, i=i: e.reduce_sum(
            st[:, 4 * i + 1:4 * i + 2], sq[:, :ncols], axis=AX.X),
            inc=dvesem if last else None)
    P.wait("dve", dvesem)
    sv = st.ap().rearrange("p (t f) -> p t f", f=4)

    def stats1(e, sv=sv):
        e.tensor_scalar(sv[:, :, 2:3], sv[:, :, 0:1], 1.0 / ncols, None, ALU.mult)
        return e.tensor_scalar(sv[:, :, 1:2], sv[:, :, 1:2], 1.0 / ncols, None, ALU.mult)
    P.on("dve", stats1, inc=dvesem)
    P.wait("dve", dvesem)

    def stats2(e, sv=sv):
        e.tensor_tensor(sv[:, :, 3:4], sv[:, :, 2:3], sv[:, :, 2:3], ALU.mult)
        return e.engine_nop()
    P.on("dve", stats2, inc=dvesem)
    P.wait("dve", dvesem)

    def stats3(e, sv=sv):
        e.tensor_tensor(sv[:, :, 3:4], sv[:, :, 1:2], sv[:, :, 3:4], ALU.subtract)
        return e.engine_nop()
    P.on("dve", stats3, inc=dvesem)
    P.wait("dve", dvesem)
    P.on("dve", lambda e, sv=sv: e.tensor_scalar(sv[:, :, 3:4], sv[:, :, 3:4], EPS, None, ALU.add),
         inc=dvesem)
    P.wait("act", dvesem)
    act_pow(P, sv[:, :, 3:4], sv[:, :, 3:4], -0.5, actsem)
    P.wait("dve", actsem)
    ops = []
    for i in range(n_tiles):
        xs = x_sb[:, src_col + i * ncols: src_col + (i + 1) * ncols]
        m_ = st[:, 4 * i + 2:4 * i + 3]
        r_ = st[:, 4 * i + 3:4 * i + 4]
        outs = []
        if o32_sb is not None:
            outs.append(o32_sb[:, dst_col + i * ncols: dst_col + (i + 1) * ncols])
        if o16_sb is not None:
            outs.append(o16_sb[:, dst_col + i * ncols: dst_col + (i + 1) * ncols])
        first = outs[0]
        ops.append(lambda e, xs=xs, m_=m_, r_=r_, o=first: e.tensor_scalar(
            o, xs, m_, r_, ALU.subtract, ALU.mult))
        if w_b is not None:
            ops.append(lambda e, o=first, w=w_b: e.tensor_tensor(o, o, w[:, :ncols], ALU.mult))
        if b_b is not None:
            ops.append(lambda e, o=first, bb=b_b: e.tensor_tensor(o, o, bb[:, :ncols], ALU.add))
        for o2 in outs[1:]:
            ops.append(lambda e, o=first, o2=o2: e.tensor_copy(o2, o))
    for j, op in enumerate(ops):
        P.on("dve", op, inc=inc_last if j == len(ops) - 1 else None)


def act_pow(P, dst, src, power, sem, rows=None):
    s = slice(0, rows) if rows is not None else slice(None)
    P.on("act", lambda e: e.activation(dst[s] if rows is not None else dst, 
                                       src[s] if rows is not None else src, AF.Ln), inc=sem)
    P.wait("act", sem)
    P.on("act", lambda e: e.activation(dst[s] if rows is not None else dst,
                                       dst[s] if rows is not None else dst, AF.Exp,
                                       scale=float(power)), inc=sem)


def emit_bcast_row(P, dst_sb, src_row_ap, ones1, ps, width, dve_sem="dve", pe_sem="pe"):
    nchunks = (width + 511) // 512
    for ci in range(nchunks):
        w = min(512, width - ci * 512)
        P.on("pe", lambda e, ps=ps, o=ones1, s=src_row_ap, ci=ci, w=w: e.matmul(
            ps[:, 0:w], o[0:1, 0:128], s[0:1, ci * 512:ci * 512 + w],
            start=True, stop=True), inc=pe_sem)
        P.wait("dve", pe_sem)
        P.on("dve", lambda e, ps=ps, d=dst_sb, ci=ci, w=w: e.tensor_copy(
            d[:, ci * 512:ci * 512 + w], ps[:, 0:w]), inc=dve_sem)
        P.wait("pe", dve_sem)


def transpose_tm_to_fm(P, src_fn, dst_sb, dst_col_fn, n_trow, n_tcol, pt0, pt1,
                       rows=128, pe_sem="tp_pe", dve_sem="tp_dve"):
    idt = P.bufs["id_bf"]
    n = 0
    for i in range(n_trow):
        for c in range(n_tcol):
            pp = pt0 if n % 2 == 0 else pt1
            if n >= 2:
                P.wait_at("pe", dve_sem, P.val(dve_sem) - 1)
            P.on("pe", lambda e, pp=pp, s=src_fn(i, c), idt=idt, rows=rows: e.transpose(
                pp[:, 0:rows], s, idt[0:rows, 0:rows]), inc=pe_sem)
            P.wait("dve", pe_sem)
            col = dst_col_fn(i, c)
            P.on("dve", lambda e, pp=pp, d=dst_sb, col=col, rows=rows: e.tensor_copy(
                d[:, col:col + rows], pp[:, 0:rows]), inc=dve_sem)
            n += 1
    P.wait("pe", dve_sem)



def build_A():
    nc = bass.Bass("TRN2", target_bir_lowering=False)
    es = contextlib.ExitStack()
    with es:
        P = Pr(nc, es)
        x = nc.dram_tensor("x", [T, DM], F32, kind="ExternalInput")
        wq = nc.dram_tensor("wq", [DM, DM], F32, kind="ExternalInput")
        wk = nc.dram_tensor("wk", [DM, DM], F32, kind="ExternalInput")
        wv = nc.dram_tensor("wv", [DM, DM], F32, kind="ExternalInput")
        ra = nc.dram_tensor("ra", [T, DM], F32, kind="ExternalInput")
        rb = nc.dram_tensor("rb", [T, DM], F32, kind="ExternalInput")
        qo = nc.dram_tensor("qo", [T, DM], BF16, kind="ExternalOutput")
        ko = nc.dram_tensor("ko", [T, DM], BF16, kind="ExternalOutput")
        vo = nc.dram_tensor("vo", [T, DM], BF16, kind="ExternalOutput")

        x_sb = P.sbuf("x_sb", [128, NT * DM], F32)
        xn16 = P.sbuf("xn16", [128, NT * DM], BF16)
        xfm = P.sbuf("xfm", [128, NC * T], BF16)
        st = P.sbuf("st", [128, 4 * NT], F32)
        P.sbuf("ln_sq", [128, DM], F32)
        id_bf = P.sbuf("id_bf", [128, 128], BF16)
        wqb = load_w_chunks(P, "wqb", wq.ap(), DM)
        wkb = load_w_chunks(P, "wkb", wk.ap(), DM)
        wvb = load_w_chunks(P, "wvb", wv.ap(), DM)
        rat = P.sbuf("rat", [128, 2 * DM], F32)
        rbt = P.sbuf("rbt", [128, 2 * DM], F32)
        otile = P.sbuf("otile", [128, 3 * DM], BF16)
        rsc = P.sbuf("rsc", [128, DM], F32)
        idram = nc.dram_tensor("id_dram", [128, 128], BF16)

        psA = P.psum("psA", [128, 512], F32)
        psB = P.psum("psB", [128, 512], F32)
        psC = P.psum("psC", [128, 512], F32)
        psD = P.psum("psD", [128, 512], F32)
        pt0 = P.psum("pt0", [128, 1024], BF16)
        pt1 = P.psum("pt1", [128, 1024], BF16)

        iot1 = P.sbuf("iot1", [128, 128], F32)
        iot2 = P.sbuf("iot2", [128, 1], F32)
        P.on("pool", lambda e: e.iota(iot1[:, :], [[1, 128]], base=0,
                                      channel_multiplier=0, allow_small_or_imprecise_dtypes=True))
        P.on("pool", lambda e: e.iota(iot2[:, :], [[1, 1]], base=0,
                                      channel_multiplier=1, allow_small_or_imprecise_dtypes=True),
             inc="pool")
        P.wait("dve", "pool")
        P.on("dve", lambda e: e.tensor_scalar(id_bf[:, :], iot1[:, :], iot2[:, 0:1],
                                              None, ALU.is_equal), inc="dve")

        P.on("sp", lambda e: e.dma_start(
            out=x_sb[:, :].rearrange("p (t j) -> p t j", t=NT),
            in_=x.ap().rearrange("(t p) j -> p t j", p=128)), inc=("dma", 16))
        P.wait("dve", "dma")
        P.wait("act", "dma")

        emit_ln(P, 0, 0, x_sb, None, xn16, st, "ln", inc_last="dve")
        P.wait("pe", "dve")

        transpose_tm_to_fm(P, lambda i, c: xn16[:, i * DM + c * 128: i * DM + (c + 1) * 128],
                           xfm, lambda i, c: c * T + i * 128, NT, NC, pt0, pt1)

        P.wait("pool", "dmap")
        P.on("pool", lambda e: e.engine_nop(), inc="pool")
        P.wait("pe", "pool")
        P.wait("sp", "dve")

        names = [("q", wqb, qo, True), ("k", wkb, ko, True), ("v", wvb, vo, False)]
        pspair = [(psA, psB), (psC, psD)]
        evac_cnt = 0
        for i in range(NT):
            rslot = i % 2
            P.on("sp", lambda e, i=i, rslot=rslot: e.dma_start(
                out=rat[:, rslot * DM:(rslot + 1) * DM], in_=ra[i * 128:(i + 1) * 128, :]),
                inc=("dma", 16))
            P.on("sp", lambda e, i=i, rslot=rslot: e.dma_start(
                out=rbt[:, rslot * DM:(rslot + 1) * DM], in_=rb[i * 128:(i + 1) * 128, :]),
                inc=("dma", 16))
            for tname, wb, outd, do_rope in names:
                psu = pspair[evac_cnt % 2]
                if evac_cnt >= 2:
                    P.wait_at("pe", "evac", P.val("evac") - 1)
                for half in range(2):
                    ps = psu[half]
                    for k in range(NC):
                        P.on("pe", lambda e, ps=ps, xfm=xfm, wb=wb, i=i, k=k, half=half: e.matmul(
                            ps[:, :], xfm[:, k * T + i * 128: k * T + (i + 1) * 128],
                            wb[:, k * DM + half * 512: k * DM + half * 512 + 512],
                            start=(k == 0), stop=(k == NC - 1)),
                            inc="pe" if k == NC - 1 else None)
                P.wait("dve", "pe")
                P.wait("dve", "dma")
                if do_rope:
                    oslot = {"q": 0, "k": 1}[tname]
                    emit_rope_evac(P, psu, rat, rbt, rslot, rsc, otile, oslot, inc_last="evac")
                else:
                    P.on("dve", lambda e, ps=psu[0]: e.tensor_copy(otile[:, 2 * DM: 2 * DM + 512], ps[:, :]))
                    P.on("dve", lambda e, ps=psu[1]: e.tensor_copy(otile[:, 2 * DM + 512: 3 * DM], ps[:, :]),
                         inc="evac")
                P.wait("sp", "evac")
                oslot = {"q": 0, "k": 1, "v": 2}[tname]
                P.on("sp", lambda e, outd=outd, i=i, oslot=oslot: e.dma_start(
                    out=outd[i * 128:(i + 1) * 128, :], in_=otile[:, oslot * DM:(oslot + 1) * DM]),
                    inc=("dma", 16))
                evac_cnt += 1
            if i >= 1:
                pass
            if i < NT - 1:
                P.wait("sp", "evac")

        P.wait("sp", "dma")
        P.finalize()
    return nc


def emit_rope_evac(P, psu, rat, rbt, rslot, rsc, otile, oslot, inc_last=None):
    for half in range(2):
        ps = psu[half]
        acol = rslot * DM + half * 512
        out0 = oslot * DM + half * 512
        psv = ps.ap().rearrange("p (h two c) -> p h two c", two=2, c=32)
        rsv = rsc[:, 0:512].rearrange("p (h two c) -> p h two c", two=2, c=32)
        tmp = rsc[:, 512:1024].rearrange("p (h two c) -> p h two c", two=2, c=32)
        rbv = rbt[:, acol:acol + 512].rearrange("p (h two c) -> p h two c", two=2, c=32)
        rav = rat[:, acol:acol + 512].rearrange("p (h two c) -> p h two c", two=2, c=32)
        ov = otile[:, out0:out0 + 512].rearrange("p (h two c) -> p h two c", two=2, c=32)

        P.on("dve", lambda e, psv=psv, rav=rav, rsv=rsv: e.tensor_tensor(
            rsv[:, :, :, :], psv[:, :, :, :], rav[:, :, :, :], ALU.mult))
        P.on("dve", lambda e, psv=psv, rbv=rbv, tmp=tmp: e.tensor_tensor(
            tmp[:, :, 0:1, :], psv[:, :, 1:2, :], rbv[:, :, 0:1, :], ALU.mult))
        P.on("dve", lambda e, psv=psv, rbv=rbv, tmp=tmp: e.tensor_tensor(
            tmp[:, :, 1:2, :], psv[:, :, 0:1, :], rbv[:, :, 1:2, :], ALU.mult))
        P.on("dve", lambda e, rsv=rsv, tmp=tmp, ov=ov: e.tensor_tensor(
            ov[:, :, :, :], rsv[:, :, :, :], tmp[:, :, :, :], ALU.add),
            inc=inc_last if half == 1 else None)




def emit_ln2x(P, i, x_res, xn16t, scratch, st, gb_b, bb_b, inc_last):
    xs = x_res[:, i * DM:(i + 1) * DM]
    P.on("dve", lambda e: e.reduce_sum(st[:, 0:1], xs, axis=AX.X))
    P.on("dve", lambda e: e.tensor_tensor(scratch[:, :], xs, xs, ALU.mult))
    P.on("dve", lambda e: e.reduce_sum(st[:, 1:2], scratch[:, :], axis=AX.X), inc="l2a")
    P.wait("dve", "l2a")

    def stats1(e):
        e.tensor_scalar(st[:, 2:3], st[:, 0:1], 1.0 / DM, None, ALU.mult)
        return e.tensor_scalar(st[:, 1:2], st[:, 1:2], 1.0 / DM, None, ALU.mult)
    P.on("dve", stats1, inc="l2a")
    P.wait("dve", "l2a")
    P.on("dve", lambda e: e.tensor_tensor(st[:, 3:4], st[:, 2:3], st[:, 2:3], ALU.mult), inc="l2a")
    P.wait("dve", "l2a")
    P.on("dve", lambda e: e.tensor_tensor(st[:, 3:4], st[:, 1:2], st[:, 3:4], ALU.subtract), inc="l2a")
    P.wait("dve", "l2a")
    P.on("dve", lambda e: e.tensor_scalar(st[:, 4:5], st[:, 3:4], EPS, None, ALU.add), inc="l2a")
    P.wait("act", "l2a")
    P.on("act", lambda e: e.activation(st[:, 6:7], st[:, 4:5], AF.Ln), inc="l2b")
    P.wait("act", "l2b")
    P.on("act", lambda e: e.activation(st[:, 5:6], st[:, 6:7], AF.Exp, scale=-0.5))
    P.on("act", lambda e: e.activation(st[:, 6:7], st[:, 6:7], AF.Exp, scale=-1.0), inc="l2b")
    P.wait("dve", "l2b")
    P.on("dve", lambda e: e.tensor_tensor(st[:, 6:7], st[:, 3:4], st[:, 6:7], ALU.mult), inc="l2c")
    P.wait("dve", "l2c")
    P.on("dve", lambda e: e.tensor_scalar(st[:, 6:7], st[:, 6:7], EPS, None, ALU.add), inc="l2c")
    P.wait("act", "l2c")
    act_pow(P, st[:, 7:8], st[:, 6:7], -0.5, "l2cc")
    P.wait("dve", "l2cc")
    P.on("dve", lambda e: e.tensor_scalar(st[:, 7:8], st[:, 5:6], st[:, 7:8], None, ALU.mult),
         inc="l2d")
    P.wait("dve", "l2d")
    P.on("dve", lambda e: e.tensor_scalar(scratch[:, :], xs, st[:, 2:3], st[:, 5:6],
                                          ALU.subtract, ALU.mult))
    P.on("dve", lambda e: e.tensor_scalar(xn16t[:, :], xs, st[:, 2:3], st[:, 7:8],
                                          ALU.subtract, ALU.mult))
    P.on("dve", lambda e: e.tensor_tensor(xs, xs, scratch[:, :], ALU.add))
    P.on("dve", lambda e: e.tensor_tensor(xn16t[:, :], xn16t[:, :], gb_b[:, :], ALU.mult))
    P.on("dve", lambda e: e.tensor_tensor(xn16t[:, :], xn16t[:, :], bb_b[:, :], ALU.add),
         inc=inc_last)



def build_BC():
    nc = bass.Bass("TRN2", target_bir_lowering=False)
    es = contextlib.ExitStack()
    with es:
        P = Pr(nc, es)
        x = nc.dram_tensor("x", [T, DM], F32, kind="ExternalInput")
        qin = nc.dram_tensor("qin", [T, DM], BF16, kind="ExternalInput")
        kseq = nc.dram_tensor("kseq", [SEQ, DM], BF16, kind="ExternalInput")
        vseq = nc.dram_tensor("vseq", [SEQ, DM], BF16, kind="ExternalInput")
        wout = nc.dram_tensor("wout", [DM, DM], F32, kind="ExternalInput")
        flagc = nc.dram_tensor("flagc", [128, 1], F32, kind="ExternalInput")
        gam = nc.dram_tensor("gam", [1, DM], F32, kind="ExternalInput")
        bet = nc.dram_tensor("bet", [1, DM], F32, kind="ExternalInput")
        rw = nc.dram_tensor("rw", [NE, DM], F32, kind="ExternalInput")
        w1 = nc.dram_tensor("w1", [NE, DM, HI], F32, kind="ExternalInput")
        w2 = nc.dram_tensor("w2", [NE, HI, DM], F32, kind="ExternalInput")
        x2o = nc.dram_tensor("x2o", [T, DM], F32, kind="ExternalOutput")
        if DEBUG:
            x1o = nc.dram_tensor("x1o", [T, DM], F32, kind="ExternalOutput")
            attno = nc.dram_tensor("attno", [128, NC * T], BF16, kind="ExternalOutput")
            kvo = nc.dram_tensor("kvo", [128, 512], BF16, kind="ExternalOutput")
            csbo = nc.dram_tensor("csbo", [128, NT * NE], F32, kind="ExternalOutput")
            xno = nc.dram_tensor("xno", [128, NC * T], BF16, kind="ExternalOutput")

        x_res = P.sbuf("x_res", [128, NT * DM], F32)
        fmbuf = P.sbuf("fmbuf", [128, NC * T], BF16)
        lobuf = P.sbuf("lobuf", [128, NC * T], BF16)
        ubuf = P.sbuf("ubuf", [128, 32768], BF16)
        kch = ubuf[:, 0:4096]
        pch = ubuf[:, 4096:8192]
        pstrip = ubuf[:, 8192:12288]
        vch = ubuf[:, 12288:16384]
        OKG, OGL, OVG, OPG, OPGT = 16384, 19656, 22928, 27024, 27440
        kg_fm = ubuf[:, OKG:OKG + NC * NG]
        gl_fm = ubuf[:, OGL:OGL + NC * NG]
        vg_tm = ubuf[:, OVG:OVG + 4 * DM]
        pg = ubuf[:, OPG:OPG + NG]
        pgt = ubuf[:, OPGT:OPGT + 4 * NG]
        w1b = ubuf[:, 0:16384]
        w2b = ubuf[:, 16384:32768]
        hbuf = lobuf[:, 0:16 * 512]

        kv_sb = ubuf[:, 29184:29696]
        woutb = ubuf[:, 0:NC * DM]
        st = P.sbuf("st", [128, 8], F32)
        st2 = P.sbuf("st2", [128, 16], F32)
        id_bf = P.sbuf("id_bf", [128, 128], BF16)
        flag_sb = P.sbuf("flag_sb", [128, 2], F32)
        gb_b = P.sbuf("gb_b", [128, DM], BF16)
        bb_b = P.sbuf("bb_b", [128, DM], BF16)
        grow = P.sbuf("grow", [1, DM], F32)
        rw_sb = P.sbuf("rw_sb", [8, DM], BF16)
        rwT = P.sbuf("rwT", [128, NC * NE], BF16)
        csb = P.sbuf("csb", [128, NT * NE], F32)
        gsc = P.sbuf("gsc", [128, 4 * NE], F32)
        ytmp = P.sbuf("ytmp", [128, DM], F32)
        P.bufs["ln_sq"] = ytmp
        xn16t = lobuf[:, 15 * DM:16 * DM]
        ones1_32 = P.sbuf("ones1_32", [1, 128], F32)

        psA = P.psum("psA", [128, 512], F32)
        psB = P.psum("psB", [128, 512], F32)
        psC = P.psum("psC", [128, 512], F32)
        psD = P.psum("psD", [128, 512], F32)
        pt0 = P.psum("pt0", [128, 1024], BF16)
        pt1 = P.psum("pt1", [128, 1024], BF16)
        psE = P.psum("psE", [128, 512], F32)
        psF = P.psum("psF", [128, 512], F32)

        emit_identity(P, id_bf, None, scratch=lobuf)

        P.on("sp", lambda e: e.dma_start(
            out=x_res[:, :].rearrange("p (t j) -> p t j", t=NT),
            in_=x.ap().rearrange("(t p) j -> p t j", p=128)), inc=("dma", 16))
        for c in range(NC):
            P.on("sp", lambda e, c=c: e.dma_start(
                out=fmbuf[:, c * T:(c + 1) * T], in_=qin[:, c * 128:(c + 1) * 128],
                transpose=True), inc=("dma", 16))
        P.on("sp", lambda e: e.dma_start(out=flag_sb[:, 0:1], in_=flagc[:, :]), inc=("dma", 16))
        P.on("sp", lambda e: e.dma_start(out=grow[0:1, 0:DM], in_=gam[:, :]), inc=("dma", 16))
        P.on("pool", lambda e: e.dma_start(out=rw_sb[:, :], in_=rw[:, :]), inc=("dmap", 16))
        P.wait("dve", "dma")
        P.on("dve", lambda e: e.tensor_scalar(
            flag_sb[:, 1:2], flag_sb[:, 0:1], -1.0, 1.0, ALU.mult, ALU.add), inc="dve")
        P.on("dve", lambda e: e.memset(ones1_32[:, :], 1.0), inc="dve")
        P.wait("pe", "dve")

        emit_bcast_row(P, gb_b, grow[0:1, 0:DM], ones1_32, psA, DM, "gb_dve", "gb_pe")
        P.wait("sp", "gb_dve")
        P.on("sp", lambda e: e.dma_start(out=grow[0:1, 0:DM], in_=bet[:, :]), inc=("dma", 16))
        P.wait("pe", "dma")
        emit_bcast_row(P, bb_b, grow[0:1, 0:DM], ones1_32, psA, DM, "gb_dve", "gb_pe")

        P.wait("pe", "dmap")
        for c in range(NC):
            P.on("pe", lambda e, c=c: e.transpose(
                pt0[0:128, 0:8],
                rw_sb[0:8, c * 128:(c + 1) * 128], id_bf[0:8, 0:8]), inc="rt_pe")
            P.wait("dve", "rt_pe")
            P.on("dve", lambda e, c=c: e.tensor_copy(
                rwT[:, c * NE:(c + 1) * NE], pt0[0:128, 0:8]), inc="rt_dve")
            P.wait_at("pe", "rt_dve", P.val("rt_dve"))

        for c in range(NC):
            P.on("sp", lambda e, c=c: e.dma_start(
                out=kch, in_=kseq[:, c * 128:(c + 1) * 128], transpose=True),
                inc=("dma", 16))
            P.on("sp", lambda e, c=c: e.dma_start(
                out=vch.rearrange("p (t j) -> p t j", t=32),
                in_=vseq.ap().rearrange("(t p) j -> p t j", p=128)[:, :, c * 128:(c + 1) * 128]),
                inc=("dma", 16))
            P.wait("dve", "dma")
            P.wait("dve", "lexp")
            P.on("dve", lambda e, c=c: e.tensor_copy(kg_fm[:, c * NG:(c + 1) * NG], kch[:, 0:NG]),
                 inc="kgc")
            P.on("dve", lambda e: e.tensor_scalar(kch[:, 0:NG], kch[:, 0:NG], -10000.0,
                                                  None, ALU.add))
            P.on("dve", lambda e: e.reduce_max(st2[:, 8:9], kch[:, :], axis=AX.X), inc="lst")
            P.wait("dve", "lst")
            P.on("dve", lambda e: e.tensor_scalar_mul(st2[:, 9:10], st2[:, 8:9], -1.0), inc="lst")
            P.wait("act", "lst")
            P.wait("act", "ptp")
            P.on("act", lambda e: e.activation(
                pch, kch[:, :], AF.Exp, bias=st2[:, 9:10], scale=1.0,
                accum_out=st2[:, 10:11]), inc="lexp")
            P.wait("act", "lexp")
            act_pow(P, st2[:, 11:12], st2[:, 10:11], -1.0, "lrc")
            P.wait("dve", "lrc")
            P.on("dve", lambda e: e.tensor_scalar_mul(pch, pch, st2[:, 11:12]), inc="lnorm")
            for kt in range(4):
                rows = 128 if kt < 3 else NG - 384
                P.on("dve", lambda e, c=c, kt=kt, rows=rows: e.tensor_copy(
                    vg_tm[0:rows, kt * DM + c * 128: kt * DM + (c + 1) * 128],
                    vch[0:rows, kt * 128:(kt + 1) * 128]), inc="vgc")
            P.wait("pe", "lnorm")
            transpose_tm_to_fm(P, lambda i, cc: pch[:, i * 128:(i + 1) * 128],
                               pstrip, lambda i, cc: i * 128, 32, 1, pt0, pt1,
                               pe_sem="ptp", dve_sem="ptd")
            for hh in range(2):
                base = 64 * hh
                kvp = psE if hh == 0 else psF
                for t_ in range(32):
                    P.on("pe", lambda e, kvp=kvp, hh=hh, t_=t_, base=base: e.matmul(
                        kvp[base:base + 64, 0:64],
                        vch[:, t_ * 128 + hh * 64: t_ * 128 + hh * 64 + 64],
                        pstrip[:, t_ * 128 + hh * 64: t_ * 128 + hh * 64 + 64],
                        start=(t_ == 0), stop=(t_ == 31), tile_position=(0, base)),
                        inc="kvpe" if t_ == 31 else None)
            P.wait("dve", "kvpe")
            for hh in range(2):
                h = 2 * c + hh
                kvp = psE if hh == 0 else psF
                base = 64 * hh
                P.on("dve", lambda e, kvp=kvp, h=h, base=base: e.tensor_copy(
                    kv_sb[base:base + 64, (h // 2) * 64:(h // 2) * 64 + 64],
                    kvp[base:base + 64, 0:64]), inc="kvd")
            P.wait("pe", "kvd")
            P.wait("sp", "kvd")

        P.wait("pe", "dma")
        nn = 0
        for h in range(NH):
            base = 64 * (h % 2)
            for qc in range(4):
                pp = (psA, psB, psC, psD)[nn % 4]
                if nn >= 4:
                    P.wait_at("pe", "lod", P.val("lod") - 3)
                P.on("pe", lambda e, pp=pp, h=h, qc=qc, base=base: e.matmul(
                    pp[base:base + 64, 0:512],
                    kv_sb[base:base + 64, (h // 2) * 64:(h // 2) * 64 + 64],
                    fmbuf[base:base + 64, (h // 2) * T + qc * 512:(h // 2) * T + (qc + 1) * 512],
                    start=True, stop=True, tile_position=(base, base)), inc="lope")
                P.wait("dve", "lope")
                P.on("dve", lambda e, pp=pp, h=h, qc=qc, base=base: e.tensor_copy(
                    lobuf[base:base + 64, (h // 2) * T + qc * 512:(h // 2) * T + (qc + 1) * 512],
                    pp[base:base + 64, 0:512]), inc="lod")
                nn += 1
        P.wait("pe", "lod")

        P.wait("pe", "vgc")
        P.wait("pe", "kgc")
        QT = [(0, 128), (128, 128), (256, 128), (384, NG - 384)]
        for h in range(NH):
            base = 64 * (h % 2)
            for qt, (q0, qn) in enumerate(QT):
                P.wait("pe", "gexp")
                P.on("pe", lambda e, h=h, q0=q0, qn=qn, base=base: e.matmul(
                    psE[0:qn, 0:NG],
                    fmbuf[base:base + 64, (h // 2) * T + q0:(h // 2) * T + q0 + qn],
                    kg_fm[base:base + 64, (h // 2) * NG:(h // 2 + 1) * NG],
                    start=True, stop=True, tile_position=(base, 0)), inc="gsc_pe")
                P.wait("dve", "gsc_pe")
                P.on("dve", lambda e, qn=qn: e.reduce_max(st2[0:qn, 12:13], psE[0:qn, 0:NG], axis=AX.X),
                     inc="gst")
                P.wait("dve", "gst")
                P.on("dve", lambda e, qn=qn: e.tensor_scalar_mul(
                    st2[0:qn, 13:14], st2[0:qn, 12:13], -SCALE_G), inc="gst")
                P.wait("act", "gst")
                P.wait("act", "gtp")
                P.on("act", lambda e, qn=qn: e.activation(
                    pg[0:qn, :], psE[0:qn, 0:NG], AF.Exp, bias=st2[0:qn, 13:14],
                    scale=SCALE_G, accum_out=st2[0:qn, 14:15]), inc="gexp")
                P.wait("act", "gexp")
                act_pow(P, st2[:, 15:16], st2[:, 14:15], -1.0, "grc")
                P.wait("dve", "grc")
                P.on("dve", lambda e, qn=qn: e.tensor_scalar_mul(
                    pg[0:qn, :], pg[0:qn, :], st2[0:qn, 15:16]), inc="gnorm")
                P.wait("pe", "gnorm")
                for kt, (k0, kn) in enumerate(QT):
                    ppt = pt0 if kt % 2 == 0 else pt1
                    if kt >= 2:
                        P.wait_at("pe", "gtd", P.val("gtd") - 1)
                    P.on("pe", lambda e, ppt=ppt, qn=qn, k0=k0, kn=kn: e.transpose(
                        ppt[0:kn, 0:qn], pg[0:qn, k0:k0 + kn], id_bf[0:qn, 0:qn]), inc="gtp")
                    P.wait("dve", "gtp")
                    P.on("dve", lambda e, ppt=ppt, kt=kt, q0=q0, qn=qn, kn=kn: e.tensor_copy(
                        pgt[0:kn, kt * NG + q0:kt * NG + q0 + qn], ppt[0:kn, 0:qn]), inc="gtd")
                P.wait("pe", "gtd")
            for kt, (k0, kn) in enumerate(QT):
                P.on("pe", lambda e, h=h, kt=kt, k0=k0, kn=kn, base=base: e.matmul(
                    psF[base:base + 64, 0:NG],
                    vg_tm[0:kn, kt * DM + (h // 2) * 128 + (h % 2) * 64:
                          kt * DM + (h // 2) * 128 + (h % 2) * 64 + 64],
                    pgt[0:kn, kt * NG:(kt + 1) * NG],
                    start=(kt == 0), stop=(kt == 3), tile_position=(0, base)),
                    inc="gav" if kt == 3 else None)
            P.wait("dve", "gav")
            P.on("dve", lambda e, h=h, base=base: e.tensor_copy(
                gl_fm[base:base + 64, (h // 2) * NG:(h // 2 + 1) * NG],
                psF[base:base + 64, 0:NG]), inc="gld")
            P.wait("pe", "gld")

        for c in range(NC):
            P.on("dve", lambda e, c=c: e.tensor_scalar(
                gl_fm[:, c * NG:(c + 1) * NG], gl_fm[:, c * NG:(c + 1) * NG],
                flag_sb[:, 0:1], None, ALU.mult))
            P.on("dve", lambda e, c=c: e.tensor_scalar(
                lobuf[:, c * T:c * T + NG], lobuf[:, c * T:c * T + NG],
                flag_sb[:, 1:2], None, ALU.mult))
            P.on("dve", lambda e, c=c: e.tensor_tensor(
                lobuf[:, c * T:c * T + NG], lobuf[:, c * T:c * T + NG],
                gl_fm[:, c * NG:(c + 1) * NG], ALU.add), inc="blend")

        P.wait("pool", "kvd")
        P.wait("pool", "ptd")
        P.on("pool", lambda e: e.dma_start(
            out=woutb.rearrange("p (c j) -> p c j", c=NC),
            in_=wout.ap().rearrange("(c p) j -> p c j", p=128)), inc=("dmap", 16))
        P.wait("pool", "dmap")
        P.on("pool", lambda e: e.engine_nop(), inc="pool")
        P.wait("pe", "pool")
        P.wait("pe", "blend")
        nn = 0
        for i in range(NT):
            for half in range(2):
                pp = (psA, psB, psC, psD)[nn % 4]
                if nn >= 4:
                    P.wait_at("pe", "wod", P.val("wod") - 3)
                for k in range(NC):
                    P.on("pe", lambda e, pp=pp, i=i, k=k, half=half: e.matmul(
                        pp[:, :], lobuf[:, k * T + i * 128:k * T + (i + 1) * 128],
                        woutb[:, k * DM + half * 512:k * DM + half * 512 + 512],
                        start=(k == 0), stop=(k == NC - 1)),
                        inc="wop" if k == NC - 1 else None)
                P.wait("dve", "wop")
                P.on("dve", lambda e, pp=pp, i=i, half=half: e.tensor_tensor(
                    x_res[:, i * DM + half * 512:i * DM + half * 512 + 512],
                    x_res[:, i * DM + half * 512:i * DM + half * 512 + 512],
                    pp[:, :], ALU.add), inc="wod")
                nn += 1

        if DEBUG:
            P.wait("sp", "wod")
            P.on("sp", lambda e: e.dma_start(
                out=x1o.ap().rearrange("(t p) j -> p t j", p=128),
                in_=x_res[:, :].rearrange("p (t j) -> p t j", t=NT)), inc=("dma", 16))
            P.on("sp", lambda e: e.dma_start(out=attno.ap(), in_=lobuf[:, :]), inc=("dma", 16))
            P.on("sp", lambda e: e.dma_start(out=kvo.ap(), in_=kv_sb), inc=("dma", 16))
            P.wait("dve", "dma")
            P.wait("act", "dma")

        emit_moe_l0(P, x_res, fmbuf, hbuf, csb, gsc, st, st2, gb_b, bb_b,
                    rwT, w1, w2, w1b, w2b, ytmp, xn16t,
                    (psA, psB, psC, psD, psE, psF), (pt0, pt1))

        if DEBUG:
            P.wait("sp", "m_gate")
            P.on("sp", lambda e: e.dma_start(out=csbo.ap(), in_=csb[:, :]), inc=("dma", 16))
            P.on("sp", lambda e: e.dma_start(out=xno.ap(), in_=fmbuf[:, :]), inc=("dma", 16))
        P.wait("sp", "moe_done")
        P.on("sp", lambda e: e.dma_start(
            out=x2o.ap().rearrange("(t p) j -> p t j", p=128),
            in_=x_res[:, :].rearrange("p (t j) -> p t j", t=NT)), inc=("dma", 16))
        P.wait("sp", "dma")
        P.finalize()
    return nc


def emit_identity(P, id_bf, id_32, scratch=None):
    if scratch is None:
        iot1 = P.sbuf("iot1", [128, 128], BF16)
        iot2 = P.sbuf("iot2", [128, 1], F32)
    else:
        iot1 = scratch[:, 0:128]
        iot2 = P.sbuf("iot2", [128, 1], F32)
    P.on("pool", lambda e: e.iota(iot1[:, :], [[1, 128]], base=0,
                                  channel_multiplier=0, allow_small_or_imprecise_dtypes=True))
    P.on("pool", lambda e: e.iota(iot2[:, :], [[1, 1]], base=0,
                                  channel_multiplier=1, allow_small_or_imprecise_dtypes=True),
         inc="pool")
    P.wait("dve", "pool")
    P.on("dve", lambda e: e.tensor_scalar(id_bf[:, :], iot1[:, :], iot2[:, 0:1],
                                          None, ALU.is_equal), inc="dve")
    if id_32 is not None:
        P.on("dve", lambda e: e.tensor_scalar(id_32[:, :], iot1[:, :], iot2[:, 0:1],
                                              None, ALU.is_equal), inc="dve")


def emit_moe_l0(P, x_res, xn_fm, hbuf, csb, gsc, st, st2, gb_b, bb_b,
                rwT, w1, w2, w1b, w2b, ytmp, xn16t, ps6, pts):
    psA, psB, psC, psD, psE, psF = ps6
    pt0, pt1 = pts

    for i in range(NT):
        P.wait("dve", "m_tp")
        emit_ln2x(P, i, x_res, xn16t, ytmp, st, gb_b, bb_b, "m_lnt")
        P.wait("pe", "m_lnt")
        transpose_tm_to_fm(P, lambda ii, c: xn16t[:, c * 128:(c + 1) * 128],
                           xn_fm, lambda ii, c, i=i: c * T + i * 128, 1, NC, pt0, pt1,
                           pe_sem="m_tp2", dve_sem="m_tp")

    for i in range(NT):
        P.wait("pe", "m_gexp")
        P.on("pe", lambda e, i=i: [
            e.matmul(psE[:, 0:NE], xn_fm[:, k * T + i * 128:k * T + (i + 1) * 128],
                     rwT[:, k * NE:(k + 1) * NE], start=(k == 0), stop=(k == NC - 1))
            for k in range(NC)][-1], inc="m_gpe")
        P.wait("dve", "m_gpe")
        P.on("dve", lambda e: e.reduce_max(st2[:, 0:1], psE[:, 0:NE], axis=AX.X), inc="m_gst")
        P.wait("dve", "m_gst")
        P.on("dve", lambda e: e.tensor_scalar_mul(st2[:, 1:2], st2[:, 0:1], -1.0 / 1.5), inc="m_gst")
        P.wait("act", "m_gst")
        P.wait("act", "m_gate")
        P.on("act", lambda e: e.activation(
            gsc[:, 0:NE], psE[:, 0:NE], AF.Exp, bias=st2[:, 1:2], scale=1.0 / 1.5,
            accum_out=st2[:, 2:3]), inc="m_gexp")
        P.wait("act", "m_gexp")
        act_pow(P, st2[:, 3:4], st2[:, 2:3], -1.0, "m_grc")
        P.wait("dve", "m_grc")
        P.on("dve", lambda e: e.tensor_scalar_mul(gsc[:, 0:NE], gsc[:, 0:NE], st2[:, 3:4]), inc="m_g2")
        P.wait("dve", "m_g2")
        P.on("dve", lambda e: e.reduce_max(st2[:, 4:5], gsc[:, 0:NE], axis=AX.X), inc="m_g3")
        P.wait("dve", "m_g3")
        P.on("dve", lambda e: e.tensor_scalar(
            gsc[:, NE:2 * NE], gsc[:, 0:NE], st2[:, 4:5], None, ALU.is_equal), inc="m_g3")
        P.wait("dve", "m_g3")
        P.on("dve", lambda e: e.tensor_tensor(
            gsc[:, 2 * NE:3 * NE], gsc[:, 0:NE], gsc[:, NE:2 * NE], ALU.mult), inc="m_g3")
        P.wait("dve", "m_g3")
        P.on("dve", lambda e: e.tensor_tensor(
            gsc[:, 2 * NE:3 * NE], gsc[:, 0:NE], gsc[:, 2 * NE:3 * NE], ALU.subtract), inc="m_g3")
        P.wait("dve", "m_g3")
        P.on("dve", lambda e: e.reduce_max(st2[:, 5:6], gsc[:, 2 * NE:3 * NE], axis=AX.X), inc="m_g4")
        P.wait("dve", "m_g4")
        P.on("dve", lambda e: e.tensor_scalar(
            gsc[:, 3 * NE:4 * NE], gsc[:, 2 * NE:3 * NE], st2[:, 5:6], None, ALU.is_equal), inc="m_g4")
        P.wait("dve", "m_g4")
        P.on("dve", lambda e: e.tensor_tensor(
            gsc[:, NE:2 * NE], gsc[:, NE:2 * NE], gsc[:, 3 * NE:4 * NE], ALU.add), inc="m_g4")
        P.wait("dve", "m_g4")
        P.on("dve", lambda e, i=i: e.tensor_tensor(
            csb[:, i * NE:(i + 1) * NE], gsc[:, 0:NE], gsc[:, NE:2 * NE], ALU.mult),
            inc="m_gate")
    P.wait("pe", "m_gate")

    hcnt = 0
    P.wait("pool", "wop")
    for e_ in range(NE):
        for half in range(2):
            P.wait("pool", "m_hpe")
            P.on("pool", lambda e, e_=e_, half=half: e.dma_start(
                out=w1b.rearrange("p (c j) -> p c j", c=NC),
                in_=w1.ap()[e_].rearrange("(c p) j -> p c j", p=128)
                [:, :, half * 2048:(half + 1) * 2048]), inc=("dmap", 16))
            P.wait("pool", "m_ype")
            P.on("pool", lambda e, e_=e_, half=half: e.dma_start(
                out=w2b.rearrange("p (c j) -> p c j", c=16),
                in_=w2.ap()[e_].rearrange("(c p) j -> p c j", p=128)
                [:, half * 16:(half + 1) * 16, :]), inc=("dmap", 16))
            P.wait("pe", "dmap")
            for tc in range(4):
                for hc in range(16):
                    pp = psA if hcnt % 2 == 0 else psB
                    if hcnt >= 2:
                        P.wait_at("pe", "m_hact", P.val("m_hact") - 1)
                    for k in range(NC):
                        P.on("pe", lambda e, pp=pp, tc=tc, hc=hc, k=k: e.matmul(
                            pp[:, :], w1b[:, k * 2048 + hc * 128:k * 2048 + (hc + 1) * 128],
                            xn_fm[:, k * T + tc * 512:k * T + tc * 512 + 512],
                            start=(k == 0), stop=(k == NC - 1)),
                            inc="m_hpe" if k == NC - 1 else None)
                    P.wait("act", "m_hpe")
                    P.on("act", lambda e, pp=pp, hc=hc: e.activation(
                        hbuf[:, hc * 512:(hc + 1) * 512], pp[:, :], AF.Silu), inc="m_hact")
                    hcnt += 1
                P.wait("pe", "m_hact")
                for ts in range(4):
                    pp0, pp1 = (psC, psD) if ts % 2 == 0 else (psE, psF)
                    P.wait_at("pe", "m_yact", max(0, P.val("m_yact") - 1))
                    for k in range(16):
                        P.on("pe", lambda e, pp0=pp0, ts=ts, k=k: e.matmul(
                            pp0[:, :], hbuf[:, k * 512 + ts * 128:k * 512 + ts * 128 + 128],
                            w2b[:, k * DM:k * DM + 512],
                            start=(k == 0), stop=(k == 15)))
                        P.on("pe", lambda e, pp1=pp1, ts=ts, k=k: e.matmul(
                            pp1[:, :], hbuf[:, k * 512 + ts * 128:k * 512 + ts * 128 + 128],
                            w2b[:, k * DM + 512:(k + 1) * DM],
                            start=(k == 0), stop=(k == 15)),
                            inc="m_ype" if k == 15 else None)
                    P.wait("act", "m_ype")
                    t_g = tc * 4 + ts
                    P.wait("act", "m_yd")
                    P.on("act", lambda e, pp0=pp0, t_g=t_g, e_=e_: e.activation(
                        ytmp[:, 0:512], pp0[:, :], AF.Copy,
                        scale=csb[:, t_g * NE + e_:t_g * NE + e_ + 1]))
                    P.on("act", lambda e, pp1=pp1, t_g=t_g, e_=e_: e.activation(
                        ytmp[:, 512:DM], pp1[:, :], AF.Copy,
                        scale=csb[:, t_g * NE + e_:t_g * NE + e_ + 1]), inc="m_yact")
                    P.wait("dve", "m_yact")
                    P.on("dve", lambda e, t_g=t_g: e.tensor_tensor(
                        x_res[:, t_g * DM:(t_g + 1) * DM], x_res[:, t_g * DM:(t_g + 1) * DM],
                        ytmp[:, :], ALU.add), inc="m_yd")
    P.wait("dve", "m_yd")
    P.on("dve", lambda e: e.engine_nop(), inc="moe_done")



def build_D():
    nc = bass.Bass("TRN2", target_bir_lowering=False)
    es = contextlib.ExitStack()
    with es:
        P = Pr(nc, es)
        x2 = nc.dram_tensor("x2", [SEQ, DM], F32, kind="ExternalInput")
        ln1w = nc.dram_tensor("ln1w", [1, DM], F32, kind="ExternalInput")
        ln1b = nc.dram_tensor("ln1b", [1, DM], F32, kind="ExternalInput")
        ln2w = nc.dram_tensor("ln2w", [1, DM], F32, kind="ExternalInput")
        ln2b = nc.dram_tensor("ln2b", [1, DM], F32, kind="ExternalInput")
        wqk = nc.dram_tensor("wqk", [DM, DM], F32, kind="ExternalInput")
        wv = nc.dram_tensor("wv", [DM, DM], F32, kind="ExternalInput")
        wo = nc.dram_tensor("wo", [DM, DM], F32, kind="ExternalInput")
        wqkc = nc.dram_tensor("wqkc", [DM, DM], F32, kind="ExternalInput")
        taur = nc.dram_tensor("taur", [1, NH], F32, kind="ExternalInput")
        g1r = nc.dram_tensor("g1r", [1, DM], F32, kind="ExternalInput")
        xmido = nc.dram_tensor("xmido", [T, DM], F32, kind="ExternalOutput")
        dotso = nc.dram_tensor("dotso", [128, NC * 128], F32, kind="ExternalOutput")
        q2co = nc.dram_tensor("q2co", [1, DM], F32, kind="ExternalOutput")

        qk_fm = P.sbuf("qk_fm", [128, NC * SEQ], BF16)
        v65 = P.sbuf("v65", [128, 32 * NH * 65], BF16)
        wbuf = P.sbuf("wbuf", [128, 2 * NC * DM], BF16)
        wqkb = wbuf[:, 0:NC * DM]
        wvb = wbuf[:, NC * DM:2 * NC * DM]
        attn_fm = wbuf
        ptb = P.sbuf("ptb", [128, 2 * 1024], BF16)
        xtile = P.sbuf("xtile", [128, DM], F32)
        t1t = P.sbuf("t1t", [128, DM], BF16)
        t1f = P.sbuf("t1f", [128, DM], BF16)
        qktt = t1t
        sqt = P.sbuf("sqt", [128, DM], F32)
        st = P.sbuf("st", [128, 8], F32)
        q2tm = P.sbuf("q2tm", [128, 32 * NH], F32)
        biasall = P.sbuf("biasall", [128, 32 * NH], F32)
        taub = P.sbuf("taub", [128, NH], F32)
        taub8 = P.sbuf("taub8", [128, NH], F32)
        mb = P.sbuf("mb", [128, NH], F32)
        m16 = P.sbuf("m16", [16, 2], F32)
        mrow = P.sbuf("mrow", [1, NH], F32)
        lnwb = P.sbuf("lnwb", [128, 4 * DM], BF16)
        g1b = P.sbuf("g1b", [128, DM], BF16)
        rssb = P.sbuf("rssb", [1, 2048], BF16)
        rsbb = P.sbuf("rsbb", [128, 2048], BF16)
        ones1_32 = P.sbuf("ones1_32", [1, 128], F32)
        ones1b = P.sbuf("ones1b", [1, 128], BF16)
        onescol = P.sbuf("onescol", [128, 1], BF16)
        grow = P.sbuf("grow", [1, DM], BF16)
        id_bf = P.sbuf("id_bf", [128, 128], BF16)
        id_32 = P.sbuf("id_32", [128, 128], F32)
        dots_sb = rsbb.bitcast(F32)
        q2crow = ptb[0:1, 0:2048].bitcast(F32)
        tmpf = P.sbuf("tmpf", [128, DM], F32)
        P.bufs["ln_sq"] = sqt

        psA = P.psum("psA", [128, 512], F32)
        psB = P.psum("psB", [128, 512], F32)
        psC = P.psum("psC", [128, 512], F32)
        psD = P.psum("psD", [128, 512], F32)
        psE = P.psum("psE", [128, 512], F32)
        psF = P.psum("psF", [128, 512], F32)
        pt0 = P.psum("pt0", [128, 1024], BF16)
        pt1 = P.psum("pt1", [128, 1024], BF16)

        emit_identity(P, id_bf, id_32, scratch=qk_fm)

        P.on("pool", lambda e: e.dma_start(
            out=wqkb.rearrange("p (c j) -> p c j", c=NC),
            in_=wqk.ap().rearrange("(c p) j -> p c j", p=128)), inc=("dmap", 16))
        P.on("pool", lambda e: e.dma_start(
            out=wvb.rearrange("p (c j) -> p c j", c=NC),
            in_=wv.ap().rearrange("(c p) j -> p c j", p=128)), inc=("dmap", 16))

        P.on("sp", lambda e: e.dma_start(out=mrow[0:1, :], in_=taur[:, :]), inc=("dma", 16))
        P.on("dve", lambda e: e.memset(ones1_32[:, :], 1.0))
        P.on("dve", lambda e: e.memset(ones1b[:, :], 1.0))
        P.on("dve", lambda e: e.memset(onescol[:, :], 1.0), inc="dve")
        P.wait("pe", "dve")
        P.wait("pe", "dma")
        P.on("pe", lambda e: e.matmul(psA[:, 0:NH], ones1_32[0:1, :], mrow[0:1, :],
                                      start=True, stop=True), inc="tb_pe")
        P.wait("dve", "tb_pe")
        P.on("dve", lambda e: e.tensor_scalar_mul(taub[:, :], psA[:, 0:NH], 0.25))
        P.on("dve", lambda e: e.tensor_scalar_mul(taub8[:, :], psA[:, 0:NH], -0.125), inc="tb_dve")
        P.wait("pe", "tb_dve")
        for idx, src_t in enumerate([ln1w, ln1b, ln2w, ln2b, g1r]):
            P.wait("pool", "gb_dve")
            P.on("pool", lambda e, s=src_t: e.dma_start(out=grow[0:1, :], in_=s[:, :]),
                 inc=("dmap", 16))
            P.wait("pe", "dmap")
            dst = lnwb[:, idx * DM:(idx + 1) * DM] if idx < 4 else g1b
            emit_bcast_row(P, dst, grow[0:1, :], ones1b, psA, DM, "gb_dve", "gb_pe")
        P.on("dve", lambda e: e.memset(
            v65.ap().rearrange("p (t h c) -> p t h c", t=32, h=NH)[:, :, :, 64:65], 1.0),
            inc="v1s")
        P.on("dve", lambda e: e.memset(m16[:, :], -1e30), inc="v1s")

        P.wait("pe", "dmap")
        for j in range(32):
            P.on("sp", lambda e, j=j: e.dma_start(
                out=xtile[:, :], in_=x2[j * 128:(j + 1) * 128, :]), inc=("dma", 16))
            P.wait("dve", "dma")
            emit_ln(P, 0, 0, xtile, None, t1t, st, "ln", n_tiles=1, sem_suffix="d1")
            P.on("dve", lambda e: e.tensor_tensor(t1t[:, :], t1t[:, :], lnwb[:, 0:DM], ALU.mult))
            P.on("dve", lambda e: e.tensor_tensor(t1t[:, :], t1t[:, :], lnwb[:, DM:2 * DM], ALU.add),
                 inc="p1t1")
            P.wait("pe", "p1t1")
            transpose_tm_to_fm(P, lambda ii, c: t1t[:, c * 128:(c + 1) * 128],
                               t1f, lambda ii, c: c * 128, 1, NC, pt0, pt1,
                               pe_sem="p1tp", dve_sem="p1td")
            P.wait("pe", "p1ve")
            for half in range(2):
                pp = psC if half == 0 else psD
                for k in range(NC):
                    P.on("pe", lambda e, pp=pp, k=k, half=half: e.matmul(
                        pp[:, :], t1f[:, k * 128:(k + 1) * 128],
                        wvb[:, k * DM + half * 512:k * DM + half * 512 + 512],
                        start=(k == 0), stop=(k == NC - 1)),
                        inc="p1v" if k == NC - 1 else None)
            P.wait("dve", "p1v")
            P.wait("dve", "v1s")
            for half in range(2):
                pp = psC if half == 0 else psD
                src = pp.ap().rearrange("p (h c) -> p h c", h=8)
                dst = v65[:, j * NH * 65 + half * 520:j * NH * 65 + (half + 1) * 520].rearrange(
                    "p (h c) -> p h c", c=65)[:, :, 0:64]
                P.on("dve", lambda e, src=src, dst=dst: e.tensor_copy(dst, src),
                     inc="p1ve" if half == 1 else None)
            P.wait("pe", "p1sq")
            for half in range(2):
                pp = psA if half == 0 else psB
                for k in range(NC):
                    P.on("pe", lambda e, pp=pp, k=k, half=half: e.matmul(
                        pp[:, :], t1f[:, k * 128:(k + 1) * 128],
                        wqkb[:, k * DM + half * 512:k * DM + half * 512 + 512],
                        start=(k == 0), stop=(k == NC - 1)),
                        inc="p1qk" if k == NC - 1 else None)
            P.wait("dve", "p1qk")
            P.on("dve", lambda e: e.tensor_copy(qktt[:, 0:512], psA[:, :]))
            P.on("dve", lambda e: e.tensor_copy(qktt[:, 512:1024], psB[:, :]), inc="p1qe")
            P.wait("dve", "p1qe")
            P.on("dve", lambda e: e.tensor_tensor(sqt[:, :], qktt[:, :], qktt[:, :], ALU.mult),
                 inc="p1sq")
            P.wait("dve", "p1sq")
            for h in range(NH):
                P.on("dve", lambda e, j=j, h=h: e.reduce_sum(
                    q2tm[:, j * NH + h:j * NH + h + 1], sqt[:, h * 64:(h + 1) * 64], axis=AX.X),
                    inc="p1q2" if h == NH - 1 else None)
            transpose_tm_to_fm(P, lambda ii, c: qktt[:, c * 128:(c + 1) * 128],
                               qk_fm, lambda ii, c, j=j: c * SEQ + j * 128, 1, NC, pt0, pt1,
                               pe_sem="p1tp", dve_sem="p1td")
            P.wait("sp", "p1sq")

        P.wait("pe", "p1q2")
        for j in range(32):
            pp = psE if j % 2 == 0 else psF
            if j >= 2:
                P.wait_at("pe", "q2td", P.val("q2td") - 1)
            P.on("pe", lambda e, j=j, pp=pp: e.transpose(
                pp[0:16, 0:128], q2tm[:, j * NH:(j + 1) * NH], id_32[:, :]), inc="q2tp")
            P.wait("dve", "q2tp")
            P.on("dve", lambda e, pp=pp: e.reduce_max(m16[0:16, 1:2], pp[0:16, 0:128], axis=AX.X),
                 inc="q2td")
            P.wait("dve", "q2td")
            P.on("dve", lambda e: e.tensor_scalar(m16[0:16, 0:1], m16[0:16, 1:2],
                                                  m16[0:16, 0:1], None, ALU.max), inc="q2td")
            P.wait("dve", "q2td")
        P.wait("pe", "q2td")
        P.on("pe", lambda e: e.transpose(psE[0:1, 0:16], m16[0:16, 0:1], id_32[0:16, 0:16]),
             inc="q2mt")
        P.wait("dve", "q2mt")
        P.on("dve", lambda e: e.tensor_copy(mrow[0:1, 0:NH], psE[0:1, 0:16]), inc="q2mr")
        P.wait("pe", "q2mr")
        P.on("pe", lambda e: e.matmul(psF[:, 0:NH], ones1_32[0:1, :], mrow[0:1, 0:NH],
                                      start=True, stop=True), inc="q2mb")
        P.wait("dve", "q2mb")
        P.on("dve", lambda e: e.tensor_copy(mb[:, 0:NH], psF[:, 0:NH]), inc="q2mb2")
        P.wait("dve", "q2mb2")
        for j in range(32):
            P.on("dve", lambda e, j=j: e.tensor_tensor(
                biasall[:, j * NH:(j + 1) * NH], q2tm[:, j * NH:(j + 1) * NH], mb[:, 0:NH],
                ALU.add), inc="ba1" if j == 31 else None)
        P.wait("dve", "ba1")
        P.wait("pool", "ba1")
        for j in range(32):
            P.on("pool", lambda e, j=j: e.tensor_tensor(
                biasall[:, j * NH:(j + 1) * NH], biasall[:, j * NH:(j + 1) * NH], taub8[:, :],
                ALU.mult), inc="ba2" if j == 31 else None)
        P.wait("act", "ba2")
        P.wait("act", "p1ve")
        P.wait("act", "p1td")

        for h in range(NH):
            base = 64 * (h % 2)
            P.wait("pe", "tnorm")
            for qh in range(2):
                P.wait("pe", "tev")
                P.wait("dve", "trc")
                dcnt = 0
                for jt in range(32):
                    dps = (psA, psB) if dcnt % 2 == 0 else (psC, psD)
                    if dcnt >= 2:
                        P.wait_at("pe", "texp", P.val("texp") - 1)
                    for qq in range(2):
                        P.on("pe", lambda e, dps=dps, h=h, base=base, jt=jt, qh=qh, qq=qq: e.matmul(
                            dps[qq][:, :],
                            qk_fm[base:base + 64,
                                  (h // 2) * SEQ + jt * 128:(h // 2) * SEQ + (jt + 1) * 128],
                            qk_fm[base:base + 64,
                                  (h // 2) * SEQ + qh * 1024 + qq * 512:
                                  (h // 2) * SEQ + qh * 1024 + (qq + 1) * 512],
                            start=True, stop=True, tile_position=(base, 0)),
                            inc="tdot" if qq == 1 else None)
                    P.wait("act", "tdot")
                    pb = ptb[:, (dcnt % 2) * 1024:(dcnt % 2 + 1) * 1024]
                    P.on("act", lambda e, dps=dps, pb=pb, h=h, jt=jt: e.activation(
                        pb[:, 0:512], dps[0][:, :], AF.Exp,
                        bias=biasall[:, jt * NH + h:jt * NH + h + 1], scale=taub[:, h:h + 1]))
                    P.on("act", lambda e, dps=dps, pb=pb, h=h, jt=jt: e.activation(
                        pb[:, 512:1024], dps[1][:, :], AF.Exp,
                        bias=biasall[:, jt * NH + h:jt * NH + h + 1], scale=taub[:, h:h + 1]),
                        inc="texp")
                    P.wait("pe", "texp")
                    for qq in range(2):
                        pv = psE if qq == 0 else psF
                        P.on("pe", lambda e, pv=pv, pb=pb, h=h, jt=jt, qq=qq: e.matmul(
                            pv[0:65, :],
                            v65[:, jt * NH * 65 + h * 65:jt * NH * 65 + (h + 1) * 65],
                            pb[:, qq * 512:(qq + 1) * 512],
                            start=(jt == 0), stop=(jt == 31)),
                            inc="tpv" if (jt == 31 and qq == 1) else None)
                    dcnt += 1
                P.wait("dve", "tpv")
                P.on("dve", lambda e, h=h, qh=qh, base=base: e.tensor_copy(
                    attn_fm[base:base + 64,
                            (h // 2) * T + qh * 1024:(h // 2) * T + qh * 1024 + 512],
                    psE[0:64, :]))
                P.on("dve", lambda e, h=h, qh=qh, base=base: e.tensor_copy(
                    attn_fm[base:base + 64,
                            (h // 2) * T + qh * 1024 + 512:(h // 2) * T + (qh + 1) * 1024],
                    psF[0:64, :]))
                P.on("dve", lambda e, qh=qh: e.tensor_copy(
                    rssb[0:1, qh * 1024:qh * 1024 + 512], psE[64:65, :]))
                P.on("dve", lambda e, qh=qh: e.tensor_copy(
                    rssb[0:1, qh * 1024 + 512:(qh + 1) * 1024], psF[64:65, :]), inc="tev")
            P.wait("act", "tev")
            act_pow(P, rssb[0:1, :], rssb[0:1, :], -1.0, "trc")
            P.wait("pe", "trc")
            for qq in range(4):
                pp = (psA, psB, psC, psD)[qq]
                P.on("pe", lambda e, pp=pp, qq=qq: e.matmul(
                    pp[:, :], ones1b[0:1, :], rssb[0:1, qq * 512:(qq + 1) * 512],
                    start=True, stop=True), inc="tnb")
            P.wait("dve", "tnb")
            for qq in range(4):
                pp = (psA, psB, psC, psD)[qq]
                P.on("dve", lambda e, pp=pp, qq=qq: e.tensor_copy(
                    rsbb[:, qq * 512:(qq + 1) * 512], pp[:, :]), inc="tnc" if qq == 3 else None)
            P.wait("dve", "tnc")
            for qq in range(4):
                P.on("dve", lambda e, h=h, base=base, qq=qq: e.tensor_tensor(
                    attn_fm[base:base + 64, (h // 2) * T + qq * 512:(h // 2) * T + (qq + 1) * 512],
                    attn_fm[base:base + 64, (h // 2) * T + qq * 512:(h // 2) * T + (qq + 1) * 512],
                    rsbb[base:base + 64, qq * 512:(qq + 1) * 512], ALU.mult),
                    inc="tnorm" if qq == 3 else None)

        wob = v65[:, 0:NC * DM]
        t2fm = v65[:, NC * DM:NC * DM + NC * T]
        P.wait("pool", "tnorm")
        P.on("pool", lambda e: e.dma_start(
            out=wob.rearrange("p (c j) -> p c j", c=NC),
            in_=wo.ap().rearrange("(c p) j -> p c j", p=128)), inc=("dmap", 16))
        P.wait("pe", "dmap")
        for i in range(NT):
            P.on("sp", lambda e, i=i: e.dma_start(
                out=xtile[:, :], in_=x2[i * 128:(i + 1) * 128, :]), inc=("dma", 16))
            P.wait("pe", "wog")
            for half in range(2):
                pp = psE if half == 0 else psF
                for k in range(NC):
                    P.on("pe", lambda e, pp=pp, i=i, k=k, half=half: e.matmul(
                        pp[:, :], attn_fm[:, k * T + i * 128:k * T + (i + 1) * 128],
                        wob[:, k * DM + half * 512:k * DM + half * 512 + 512],
                        start=(k == 0), stop=(k == NC - 1)),
                        inc="wop" if k == NC - 1 else None)
            P.wait("dve", "dma")
            emit_ln(P, 0, 0, xtile, tmpf, None, st, "ln", n_tiles=1, sem_suffix="d2")
            P.wait("dve", "wop")
            P.on("dve", lambda e: e.tensor_tensor(sqt[:, 0:512], psE[:, :], g1b[:, 0:512], ALU.mult))
            P.on("dve", lambda e: e.tensor_tensor(sqt[:, 512:1024], psF[:, :], g1b[:, 512:1024],
                                                  ALU.mult), inc="wog")
            P.wait("dve", "wog")
            P.on("dve", lambda e: e.tensor_tensor(tmpf[:, :], tmpf[:, :], sqt[:, :], ALU.add),
                 inc="xmid")
            P.wait("sp", "xmid")
            P.on("sp", lambda e, i=i: e.dma_start(
                out=xmido[i * 128:(i + 1) * 128, :], in_=tmpf[:, :]), inc=("dma", 16))
            emit_ln(P, 0, 0, tmpf, None, t1t, st, "ln", n_tiles=1, sem_suffix="d3")
            P.on("dve", lambda e: e.tensor_tensor(t1t[:, :], t1t[:, :], lnwb[:, 2 * DM:3 * DM],
                                                  ALU.mult))
            P.on("dve", lambda e: e.tensor_tensor(t1t[:, :], t1t[:, :], lnwb[:, 3 * DM:4 * DM],
                                                  ALU.add), inc="p3t2")
            P.wait("pe", "p3t2")
            transpose_tm_to_fm(P, lambda ii, c: t1t[:, c * 128:(c + 1) * 128],
                               t2fm, lambda ii, c, i=i: c * T + i * 128, 1, NC, pt0, pt1,
                               pe_sem="p3tp", dve_sem="p3td")
            P.wait("sp", "p3td")

        wqkcb = qk_fm[:, 0:NC * DM]
        qkctm = qk_fm[:, NC * DM:NC * DM + NT * DM]
        P.wait("pool", "p3td")
        P.on("pool", lambda e: e.dma_start(
            out=wqkcb.rearrange("p (c j) -> p c j", c=NC),
            in_=wqkc.ap().rearrange("(c p) j -> p c j", p=128)), inc=("dmap", 16))
        P.wait("pe", "dmap")
        nn = 0
        for i in range(NT):
            for half in range(2):
                pp = psA if nn % 2 == 0 else psB
                if nn >= 2:
                    P.wait_at("pe", "p4e", P.val("p4e") - 1)
                for k in range(NC):
                    P.on("pe", lambda e, pp=pp, i=i, k=k, half=half: e.matmul(
                        pp[:, :], t2fm[:, k * T + i * 128:k * T + (i + 1) * 128],
                        wqkcb[:, k * DM + half * 512:k * DM + half * 512 + 512],
                        start=(k == 0), stop=(k == NC - 1)),
                        inc="p4pe" if k == NC - 1 else None)
                P.wait("dve", "p4pe")
                P.on("dve", lambda e, pp=pp, i=i, half=half: e.tensor_copy(
                    qkctm[:, i * DM + half * 512:i * DM + half * 512 + 512], pp[:, :]),
                    inc="p4e")
                nn += 1
        P.wait("dve", "p4e")
        sq16 = t1f
        for i in range(NT):
            if i > 0:
                P.wait("dve", "p4q2m")
            P.on("dve", lambda e, i=i: e.tensor_tensor(
                sq16[:, :], qkctm[:, i * DM:(i + 1) * DM], qkctm[:, i * DM:(i + 1) * DM],
                ALU.mult), inc="p4sq")
            P.wait("pe", "p4sq")
            for half in range(2):
                pp = psC if half == 0 else psD
                P.on("pe", lambda e, pp=pp, i=i, half=half: e.matmul(
                    pp[0:1, :], onescol[:, 0:1], sq16[:, half * 512:(half + 1) * 512],
                    start=(i == 0), stop=(i == NT - 1)),
                    inc="p4q2m" if half == 1 else None)
        P.wait("dve", "p4q2m")
        P.on("dve", lambda e: e.tensor_copy(q2crow[0:1, 0:512], psC[0:1, :]))
        P.on("dve", lambda e: e.tensor_copy(q2crow[0:1, 512:1024], psD[0:1, :]), inc="p4q2e")
        for g in range(NC):
            pp = psE if g % 2 == 0 else psF
            if g >= 2:
                P.wait_at("pe", "p4de", P.val("p4de") - 1)
            for i in range(NT):
                P.on("pe", lambda e, pp=pp, g=g, i=i: e.matmul(
                    pp[:, 0:128], qkctm[:, i * DM + g * 128:i * DM + (g + 1) * 128],
                    qkctm[:, i * DM + g * 128:i * DM + (g + 1) * 128],
                    start=(i == 0), stop=(i == NT - 1)),
                    inc="p4d" if i == NT - 1 else None)
            P.wait("dve", "p4d")
            P.on("dve", lambda e, pp=pp, g=g: e.tensor_copy(
                dots_sb[:, g * 128:(g + 1) * 128], pp[:, 0:128]), inc="p4de")
        P.wait("sp", "p4de")
        P.wait("sp", "p4q2e")
        P.on("sp", lambda e: e.dma_start(out=dotso.ap(), in_=dots_sb[:, :]), inc=("dma", 16))
        P.on("sp", lambda e: e.dma_start(out=q2co.ap(), in_=q2crow[0:1, :]), inc=("dma", 16))
        P.wait("sp", "dma")
        P.finalize()
    return nc



def build_E():
    nc = bass.Bass("TRN2", target_bir_lowering=False)
    es = contextlib.ExitStack()
    with es:
        P = Pr(nc, es)
        xmid = nc.dram_tensor("xmid", [T, DM], F32, kind="ExternalInput")
        x2own = nc.dram_tensor("x2own", [T, DM], F32, kind="ExternalInput")
        dotsa = nc.dram_tensor("dotsa", [128, NC * 128], F32, kind="ExternalInput")
        dotsb = nc.dram_tensor("dotsb", [128, NC * 128], F32, kind="ExternalInput")
        q2a = nc.dram_tensor("q2a", [1, DM], F32, kind="ExternalInput")
        q2b = nc.dram_tensor("q2b", [1, DM], F32, kind="ExternalInput")
        ln2w = nc.dram_tensor("ln2w", [1, DM], F32, kind="ExternalInput")
        ln2b = nc.dram_tensor("ln2b", [1, DM], F32, kind="ExternalInput")
        wvc = nc.dram_tensor("wvc", [DM, DM], F32, kind="ExternalInput")
        woc = nc.dram_tensor("woc", [DM, DM], F32, kind="ExternalInput")
        taucr = nc.dram_tensor("taucr", [1, 8], F32, kind="ExternalInput")
        g2r = nc.dram_tensor("g2r", [1, DM], F32, kind="ExternalInput")
        wg = nc.dram_tensor("wg", [DM, NE], F32, kind="ExternalInput")
        x3o = nc.dram_tensor("x3o", [T, DM], F32, kind="ExternalOutput")
        h1o = nc.dram_tensor("h1o", [T, NE], F32, kind="ExternalOutput")
        h2o = nc.dram_tensor("h2o", [T, NE], F32, kind="ExternalOutput")
        w12o = nc.dram_tensor("w12o", [T, 2], F32, kind="ExternalOutput")
        cnto = nc.dram_tensor("cnto", [NE, 2], F32, kind="ExternalOutput")
        dbounce = nc.dram_tensor("dbounce", [DM], F32)
        if DEBUG:
            ddot = nc.dram_tensor("ddot", [128, DM], F32, kind="ExternalOutput")
            dq2c = nc.dram_tensor("dq2c", [128, NC], F32, kind="ExternalOutput")
            dnb = nc.dram_tensor("dnb", [128, DM], BF16, kind="ExternalOutput")
            dvc = nc.dram_tensor("dvc", [128, NC * T], BF16, kind="ExternalOutput")
            dcd = nc.dram_tensor("dcd", [128, NC * T], BF16, kind="ExternalOutput")
            dt2 = nc.dram_tensor("dt2", [128, NC * T], BF16, kind="ExternalOutput")
            dlg = nc.dram_tensor("dlg", [128, NC * 128], F32, kind="ExternalOutput")
            dst2 = nc.dram_tensor("dst2", [128, 48], F32, kind="ExternalOutput")

        t2fm = P.sbuf("t2fm", [128, NC * T], BF16)
        vcfm = P.sbuf("vcfm", [128, NC * T], BF16)
        cdfm = P.sbuf("cdfm", [128, NC * T], BF16)
        wvcb = P.sbuf("wvcb", [128, NC * DM], BF16)
        wocb = P.sbuf("wocb", [128, NC * DM], BF16)
        wgb = P.sbuf("wgb", [128, NC * NE], BF16)
        dot = P.sbuf("dot", [128, DM], F32)
        nb = P.sbuf("nb", [128, DM], BF16)
        q2colh = P.sbuf("q2colh", [128, NC], F32)
        q2row = P.sbuf("q2row", [1, DM], F32)
        taucb = P.sbuf("taucb", [128, NC], F32)
        lnwb = P.sbuf("lnwb", [128, 2 * DM], BF16)
        g2b = P.sbuf("g2b", [128, DM], BF16)
        grow = P.sbuf("grow", [1, DM], BF16)
        xtile = P.sbuf("xtile", [128, DM], F32)
        x3t = P.sbuf("x3t", [128, DM], F32)
        t1t = P.sbuf("t1t", [128, DM], BF16)
        t1f = P.sbuf("t1f", [128, DM], BF16)
        sqt = P.sbuf("sqt", [128, DM], F32)
        st = P.sbuf("st", [128, 8], F32)
        st2 = P.sbuf("st2", [128, 16], F32)
        lg = P.sbuf("lg", [128, 128], F32)
        pc = P.sbuf("pc", [128, 128], BF16)
        pnt = P.sbuf("pnt", [128, 128], BF16)
        gsc = P.sbuf("gsc", [128, 4 * NE], F32)
        h1sb = P.sbuf("h1sb", [128, NT * NE], F32)
        h2sb = P.sbuf("h2sb", [128, NT * NE], F32)
        w12sb = P.sbuf("w12sb", [128, NT * 2], F32)
        cntsb = P.sbuf("cntsb", [NE, 2], F32)
        onescol32 = P.sbuf("onescol32", [128, 1], F32)
        ones1b = P.sbuf("ones1b", [1, 128], BF16)
        ones1_32 = P.sbuf("ones1_32", [1, 128], F32)
        id_bf = P.sbuf("id_bf", [128, 128], BF16)
        P.bufs["ln_sq"] = sqt

        psA = P.psum("psA", [128, 512], F32)
        psB = P.psum("psB", [128, 512], F32)
        psC = P.psum("psC", [128, 512], F32)
        psD = P.psum("psD", [128, 512], F32)
        psE = P.psum("psE", [128, 512], F32)
        psF = P.psum("psF", [128, 512], F32)
        pt0 = P.psum("pt0", [128, 1024], BF16)
        pt1 = P.psum("pt1", [128, 1024], BF16)

        emit_identity(P, id_bf, None, scratch=t2fm)

        P.on("pool", lambda e: e.dma_start(
            out=wvcb.rearrange("p (c j) -> p c j", c=NC),
            in_=wvc.ap().rearrange("(c p) j -> p c j", p=128)), inc=("dmap", 16))
        P.on("pool", lambda e: e.dma_start(
            out=wocb.rearrange("p (c j) -> p c j", c=NC),
            in_=woc.ap().rearrange("(c p) j -> p c j", p=128)), inc=("dmap", 16))
        P.on("pool", lambda e: e.dma_start(
            out=wgb.rearrange("p (c j) -> p c j", c=NC),
            in_=wg.ap().rearrange("(c p) j -> p c j", p=128)), inc=("dmap", 16))
        P.on("sp", lambda e: e.dma_start(out=dot[:, :].rearrange("p (c j) -> p c j", c=NC) if False
                                         else dot[:, :], in_=dotsa[:, :]), inc=("dma", 16))
        P.on("sp", lambda e: e.dma_start(out=sqt[:, :], in_=dotsb[:, :]), inc=("dma", 16))
        P.on("sp", lambda e: e.dma_start(out=q2row[0:1, :], in_=q2a[:, :]), inc=("dma", 16))
        P.on("sp", lambda e: e.dma_start(out=x3t[0:1, 0:DM], in_=q2b[:, :]), inc=("dma", 16))
        P.on("sp", lambda e: e.dma_start(out=xtile[0:1, 0:8], in_=taucr[:, :]), inc=("dma", 16))
        P.wait("dve", "dma")
        P.on("dve", lambda e: e.memset(onescol32[:, :], 1.0))
        P.on("dve", lambda e: e.memset(ones1b[:, :], 1.0))
        P.on("dve", lambda e: e.memset(ones1_32[:, :], 1.0), inc="dve")
        P.on("dve", lambda e: e.tensor_tensor(dot[:, :], dot[:, :], sqt[:, :], ALU.add))
        P.on("dve", lambda e: e.tensor_tensor(q2row[0:1, :], q2row[0:1, :], x3t[0:1, 0:DM],
                                              ALU.add), inc="dve")
        P.wait("dve", "dve")
        P.on("dve", lambda e: e.tensor_scalar_mul(q2row[0:1, :], q2row[0:1, :], -0.5), inc="q2h")
        P.wait("sp", "q2h")
        P.on("sp", lambda e: e.dma_start(out=dbounce.ap().rearrange("(one j) -> one j", one=1),
                                         in_=q2row[0:1, :]), inc=("dma", 16))
        P.wait("sp", "dma")
        P.on("sp", lambda e: e.dma_start(out=q2colh[:, :], in_=dbounce.ap().rearrange(
            "(c p) -> p c", p=128), allow_slow_non_contiguous=True), inc=("dma", 16))
        P.wait("pe", "q2h")
        P.on("dve", lambda e: e.tensor_copy(grow[0:1, :], q2row[0:1, :]), inc="nbc")
        P.wait("pe", "nbc")
        emit_bcast_row(P, nb, grow[0:1, :], ones1b, psA, DM, "gb_dve", "gb_pe")
        P.wait("pe", "dma")
        P.on("pe", lambda e: e.matmul(psB[:, 0:8], ones1_32[0:1, :], xtile[0:1, 0:8],
                                      start=True, stop=True), inc="tc_pe")
        P.wait("dve", "tc_pe")
        P.on("dve", lambda e: e.tensor_scalar_mul(taucb[:, :], psB[:, 0:8], 1.0 / 32.0), inc="tc_d")
        P.wait("pe", "tc_d")
        for idx, src_t in enumerate([ln2w, ln2b, g2r]):
            P.wait("pool", "gb_dve")
            P.on("pool", lambda e, s=src_t: e.dma_start(out=grow[0:1, :], in_=s[:, :]),
                 inc=("dmap", 16))
            P.wait("pe", "dmap")
            dst = lnwb[:, idx * DM:(idx + 1) * DM] if idx < 2 else g2b
            emit_bcast_row(P, dst, grow[0:1, :], ones1b, psA, DM, "gb_dve", "gb_pe")

        P.wait("sp", "tc_pe")
        for i in range(NT):
            P.on("sp", lambda e, i=i: e.dma_start(
                out=xtile[:, :], in_=xmid[i * 128:(i + 1) * 128, :]), inc=("dma", 16))
            P.wait("dve", "dma")
            emit_ln(P, 0, 0, xtile, None, t1t, st, "ln", n_tiles=1, sem_suffix="e1")
            P.on("dve", lambda e: e.tensor_tensor(t1t[:, :], t1t[:, :], lnwb[:, 0:DM], ALU.mult))
            P.on("dve", lambda e: e.tensor_tensor(t1t[:, :], t1t[:, :], lnwb[:, DM:2 * DM],
                                                  ALU.add), inc="e1t")
            P.wait("pe", "e1t")
            transpose_tm_to_fm(P, lambda ii, c: t1t[:, c * 128:(c + 1) * 128],
                               t2fm, lambda ii, c, i=i: c * T + i * 128, 1, NC, pt0, pt1,
                               pe_sem="e1tp", dve_sem="e1td")
            P.wait("sp", "e1td")

        P.wait("pe", "dmap")
        nn = 0
        for c in range(NC):
            for tc in range(4):
                pp = (psA, psB, psC, psD)[nn % 4]
                if nn >= 4:
                    P.wait_at("pe", "vce", P.val("vce") - 3)
                for k in range(NC):
                    P.on("pe", lambda e, pp=pp, c=c, tc=tc, k=k: e.matmul(
                        pp[:, :], wvcb[:, k * DM + c * 128:k * DM + (c + 1) * 128],
                        t2fm[:, k * T + tc * 512:k * T + (tc + 1) * 512],
                        start=(k == 0), stop=(k == NC - 1)),
                        inc="vcp" if k == NC - 1 else None)
                P.wait("dve", "vcp")
                P.on("dve", lambda e, pp=pp, c=c, tc=tc: e.tensor_copy(
                    vcfm[:, c * T + tc * 512:c * T + (tc + 1) * 512], pp[:, :]), inc="vce")
                nn += 1
        P.wait("pe", "vce")

        for g in range(NC):
            P.on("dve", lambda e, g=g: e.tensor_tensor(
                lg[:, :], dot[:, g * 128:(g + 1) * 128], nb[:, g * 128:(g + 1) * 128], ALU.add),
                inc="cd0")
            P.wait("dve", "cd0")
            P.on("dve", lambda e, g=g: e.tensor_scalar(
                lg[:, :], lg[:, :], q2colh[:, g:g + 1], None, ALU.add), inc="cd1")
            P.wait("dve", "cd1")
            P.on("dve", lambda e: e.reduce_max(st2[:, 0:1], lg[:, :], axis=AX.X), inc="cd2")
            P.wait("dve", "cd2")
            P.on("dve", lambda e, g=g: e.tensor_scalar(
                st2[:, 1:2], st2[:, 0:1], taucb[:, g:g + 1], -1.0, ALU.mult, ALU.mult), inc="cd3")
            if DEBUG:
                P.on("dve", lambda e, g=g: e.tensor_copy(t2fm.bitcast(F32)[:, 6 * 1024 + g * 128:6 * 1024 + (g + 1) * 128], lg[:, :]))
            P.wait("act", "cd3")
            P.wait("act", "cdtp")
            P.on("act", lambda e, g=g: e.activation(
                pc[:, :], lg[:, :], AF.Exp, bias=st2[:, 1:2], scale=taucb[:, g:g + 1],
                accum_out=st2[:, 2:3]), inc="cdexp")
            P.wait("act", "cdexp")
            act_pow(P, st2[:, 3:4], st2[:, 2:3], -1.0, "cdrc")
            P.wait("dve", "cdrc")
            if DEBUG:
                P.on("dve", lambda e, g=g: e.tensor_copy(
                    t2fm.bitcast(F32)[:, 7 * 1024 + g * 4:7 * 1024 + g * 4 + 4], st2[:, 0:4]),
                    inc="dbg")
                P.on("dve", lambda e, g=g: e.tensor_copy(
                    t2fm.bitcast(F32)[:, 7 * 1024 + 40 + g:7 * 1024 + 40 + g + 1],
                    taucb[:, g:g + 1]), inc="dbg")
            P.on("dve", lambda e: e.tensor_scalar_mul(pc[:, :], pc[:, :], st2[:, 3:4]), inc="cdn")
            P.wait("pe", "cdn")
            P.on("pe", lambda e: e.transpose(pt0[:, 0:128], pc[:, :], id_bf[:, :]), inc="cdtp")
            P.wait("dve", "cdtp")
            P.on("dve", lambda e: e.tensor_copy(pnt[:, :], pt0[:, 0:128]), inc="cdte")
            P.wait("pe", "cdte")
            for tc in range(4):
                pp = psE if tc % 2 == 0 else psF
                if tc >= 2:
                    P.wait_at("pe", "cdoe", P.val("cdoe") - 1)
                P.on("pe", lambda e, pp=pp, g=g, tc=tc: e.matmul(
                    pp[:, :], pnt[:, :], vcfm[:, g * T + tc * 512:g * T + (tc + 1) * 512],
                    start=True, stop=True), inc="cdo")
                P.wait("dve", "cdo")
                P.on("dve", lambda e, pp=pp, g=g, tc=tc: e.tensor_copy(
                    cdfm[:, g * T + tc * 512:g * T + (tc + 1) * 512], pp[:, :]), inc="cdoe")
            P.wait("pe", "cdoe")

        if DEBUG:
            P.wait("sp", "cdoe")
            P.on("sp", lambda e: e.dma_start(out=ddot.ap(), in_=dot[:, :]), inc=("dma", 16))
            P.on("sp", lambda e: e.dma_start(out=dq2c.ap(), in_=q2colh[:, :]), inc=("dma", 16))
            P.on("sp", lambda e: e.dma_start(out=dnb.ap(), in_=nb[:, :]), inc=("dma", 16))
            P.on("sp", lambda e: e.dma_start(out=dvc.ap(), in_=vcfm[:, :]), inc=("dma", 16))
            P.on("sp", lambda e: e.dma_start(out=dcd.ap(), in_=cdfm[:, :]), inc=("dma", 16))
            P.on("sp", lambda e: e.dma_start(out=dt2.ap(), in_=t2fm[:, :]), inc=("dma", 16))
            P.wait("sp", "dbg")
            P.on("sp", lambda e: e.dma_start(out=dlg.ap(), in_=t2fm.bitcast(F32)[:, 6 * 1024:7 * 1024]), inc=("dma", 16))
            P.on("sp", lambda e: e.dma_start(out=dst2.ap(), in_=t2fm.bitcast(F32)[:, 7 * 1024:7 * 1024 + 48]), inc=("dma", 16))
            P.wait("sp", "dma")

        for i in range(NT):
            P.on("sp", lambda e, i=i: e.dma_start(
                out=xtile[:, :], in_=x2own[i * 128:(i + 1) * 128, :]), inc=("dma", 16))
            P.on("sp", lambda e, i=i: e.dma_start(
                out=sqt[:, :], in_=xmid[i * 128:(i + 1) * 128, :]), inc=("dma", 16))
            P.wait("pe", "woe")
            for half in range(2):
                pp = psA if half == 0 else psB
                for k in range(NC):
                    P.on("pe", lambda e, pp=pp, i=i, k=k, half=half: e.matmul(
                        pp[:, :], cdfm[:, k * T + i * 128:k * T + (i + 1) * 128],
                        wocb[:, k * DM + half * 512:k * DM + half * 512 + 512],
                        start=(k == 0), stop=(k == NC - 1)),
                        inc="wom" if k == NC - 1 else None)
            P.wait("dve", "dma")
            P.wait("dve", "wom")
            P.on("dve", lambda e: e.tensor_tensor(x3t[:, 0:512], psA[:, :], g2b[:, 0:512],
                                                  ALU.mult))
            P.on("dve", lambda e: e.tensor_tensor(x3t[:, 512:DM], psB[:, :], g2b[:, 512:DM],
                                                  ALU.mult), inc="woe")
            P.wait("dve", "woe")
            P.on("dve", lambda e: e.tensor_tensor(x3t[:, :], x3t[:, :], sqt[:, :], ALU.add))
            P.on("dve", lambda e: e.tensor_tensor(x3t[:, :], x3t[:, :], xtile[:, :], ALU.add),
                 inc="x3d")
            P.wait("sp", "x3d")
            P.on("sp", lambda e, i=i: e.dma_start(
                out=x3o[i * 128:(i + 1) * 128, :], in_=x3t[:, :]), inc=("dma", 16))
            emit_ln(P, 0, 0, x3t, None, t1t, st, "ln", n_tiles=1, sem_suffix="e2")
            P.on("dve", lambda e: e.engine_nop(), inc="e2t")
            P.wait("pe", "e2t")
            transpose_tm_to_fm(P, lambda ii, c: t1t[:, c * 128:(c + 1) * 128],
                               t1f, lambda ii, c: c * 128, 1, NC, pt0, pt1,
                               pe_sem="e2tp", dve_sem="e2td")
            P.wait("pe", "g_exp")
            P.on("pe", lambda e: [e.matmul(psE[:, 0:NE], t1f[:, k * 128:(k + 1) * 128],
                                           wgb[:, k * NE:(k + 1) * NE],
                                           start=(k == 0), stop=(k == NC - 1))
                                  for k in range(NC)][-1], inc="g_pe")
            P.wait("dve", "g_pe")
            P.on("dve", lambda e: e.reduce_max(st2[:, 4:5], psE[:, 0:NE], axis=AX.X), inc="g_m")
            P.wait("dve", "g_m")
            P.on("dve", lambda e: e.tensor_scalar_mul(st2[:, 5:6], st2[:, 4:5], -1.0), inc="g_m2")
            P.wait("act", "g_m2")
            P.wait("act", "g_done")
            P.on("act", lambda e: e.activation(
                gsc[:, 0:NE], psE[:, 0:NE], AF.Exp, bias=st2[:, 5:6], scale=1.0,
                accum_out=st2[:, 6:7]), inc="g_exp")
            P.wait("act", "g_exp")
            act_pow(P, st2[:, 7:8], st2[:, 6:7], -1.0, "g_rc")
            P.wait("dve", "g_rc")
            P.on("dve", lambda e: e.tensor_scalar_mul(gsc[:, 0:NE], gsc[:, 0:NE], st2[:, 7:8]),
                 inc="g_1")
            P.wait("dve", "g_1")
            P.on("dve", lambda e: e.reduce_max(st2[:, 8:9], gsc[:, 0:NE], axis=AX.X), inc="g_2")
            P.wait("dve", "g_2")
            P.on("dve", lambda e, i=i: e.tensor_scalar(
                h1sb[:, i * NE:(i + 1) * NE], gsc[:, 0:NE], st2[:, 8:9], None, ALU.is_equal),
                 inc="g_3")
            P.wait("dve", "g_3")
            P.on("dve", lambda e, i=i: e.tensor_tensor(
                gsc[:, NE:2 * NE], gsc[:, 0:NE], h1sb[:, i * NE:(i + 1) * NE], ALU.mult),
                 inc="g_4")
            P.wait("dve", "g_4")
            P.on("dve", lambda e: e.tensor_tensor(
                gsc[:, 2 * NE:3 * NE], gsc[:, 0:NE], gsc[:, NE:2 * NE], ALU.subtract), inc="g_5")
            P.wait("dve", "g_5")
            P.on("dve", lambda e: e.reduce_max(st2[:, 9:10], gsc[:, 2 * NE:3 * NE], axis=AX.X),
                 inc="g_6")
            P.wait("dve", "g_6")
            P.on("dve", lambda e, i=i: e.tensor_scalar(
                h2sb[:, i * NE:(i + 1) * NE], gsc[:, 2 * NE:3 * NE], st2[:, 9:10], None,
                ALU.is_equal), inc="g_7")
            P.wait("dve", "g_7")
            P.on("dve", lambda e: e.tensor_scalar(st2[:, 10:11], st2[:, 8:9], st2[:, 9:10], None,
                                                  ALU.add), inc="g_8")
            P.wait("dve", "g_8")
            P.on("dve", lambda e: e.tensor_scalar_max(st2[:, 10:11], st2[:, 10:11], 1e-9),
                 inc="g_9")
            P.wait("act", "g_9")
            act_pow(P, st2[:, 11:12], st2[:, 10:11], -1.0, "g_10")
            P.wait("dve", "g_10")
            P.on("dve", lambda e, i=i: e.tensor_scalar(
                w12sb[:, i * 2:i * 2 + 1], st2[:, 8:9], st2[:, 11:12], None, ALU.mult))
            P.on("dve", lambda e, i=i: e.tensor_scalar(
                w12sb[:, i * 2 + 1:i * 2 + 2], st2[:, 9:10], st2[:, 11:12], None, ALU.mult),
                 inc="g_done")
            P.wait("pe", "g_done")
            P.on("pe", lambda e, i=i: e.matmul(
                psC[0:NE, 0:1], h1sb[:, i * NE:(i + 1) * NE], onescol32[:, 0:1],
                start=(i == 0), stop=(i == NT - 1)))
            P.on("pe", lambda e, i=i: e.matmul(
                psD[0:NE, 0:1], h2sb[:, i * NE:(i + 1) * NE], onescol32[:, 0:1],
                start=(i == 0), stop=(i == NT - 1)),
                 inc="g_cnt" if i == NT - 1 else None)
            P.wait("sp", "e2td")
        P.wait("dve", "g_cnt")
        P.on("dve", lambda e: e.tensor_copy(cntsb[0:NE, 0:1], psC[0:NE, 0:1]))
        P.on("dve", lambda e: e.tensor_copy(cntsb[0:NE, 1:2], psD[0:NE, 0:1]), inc="cnte")
        P.wait("sp", "cnte")
        P.wait("sp", "g_done")
        P.on("sp", lambda e: e.dma_start(out=cnto.ap(), in_=cntsb[:, :]), inc=("dma", 16))
        P.on("sp", lambda e: e.dma_start(
            out=h1o.ap().rearrange("(t p) e -> p t e", p=128),
            in_=h1sb[:, :].rearrange("p (t e) -> p t e", t=NT)), inc=("dma", 16))
        P.on("sp", lambda e: e.dma_start(
            out=h2o.ap().rearrange("(t p) e -> p t e", p=128),
            in_=h2sb[:, :].rearrange("p (t e) -> p t e", t=NT)), inc=("dma", 16))
        P.on("sp", lambda e: e.dma_start(
            out=w12o.ap().rearrange("(t p) e -> p t e", p=128),
            in_=w12sb[:, :].rearrange("p (t e) -> p t e", t=NT)), inc=("dma", 16))
        P.wait("sp", "dma")
        P.finalize()
    return nc



def build_G(stage="full"):
    nc = bass.Bass("TRN2", target_bir_lowering=False)
    es = contextlib.ExitStack()
    with es:
        P = Pr(nc, es)
        x3 = nc.dram_tensor("x3", [T, DM], F32, kind="ExternalInput")
        h1i = nc.dram_tensor("h1i", [T, NE], F32, kind="ExternalInput")
        h2i = nc.dram_tensor("h2i", [T, NE], F32, kind="ExternalInput")
        w12i = nc.dram_tensor("w12i", [T, 2], F32, kind="ExternalInput")
        cntall = nc.dram_tensor("cntall", [NCORES, 2 * NE], F32, kind="ExternalInput")
        masklt = nc.dram_tensor("masklt", [NCORES, 1], F32, kind="ExternalInput")
        tri = nc.dram_tensor("tri", [128, 128], BF16, kind="ExternalInput")
        w1 = nc.dram_tensor("w1", [NE, DM, HI], F32, kind="ExternalInput")
        b1 = nc.dram_tensor("b1", [NE, HI], F32, kind="ExternalInput")
        w2 = nc.dram_tensor("w2", [NE, HI, DM], F32, kind="ExternalInput")
        b2 = nc.dram_tensor("b2", [NE, DM], F32, kind="ExternalInput")
        outo = nc.dram_tensor("outo", [T, DM], F32, kind="ExternalOutput")

        x_res = P.sbuf("x_res", [128, NT * DM], F32)
        xn_fm = P.sbuf("xn_fm", [128, NC * T], BF16)
        hbuf = P.sbuf("hbuf", [128, 16 * 512], BF16)
        w1b = P.sbuf("w1b", [128, NC * 2048], BF16)
        w2b = P.sbuf("w2b", [128, 16 * DM], BF16)
        tri_sb = P.sbuf("tri_sb", [128, 128], BF16)
        h1sb = P.sbuf("h1sb", [128, NT * NE], F32)
        h2sb = P.sbuf("h2sb", [128, NT * NE], F32)
        h1b = P.sbuf("h1b", [128, NT * NE], BF16)
        h2b = P.sbuf("h2b", [128, NT * NE], BF16)
        w12sb = P.sbuf("w12sb", [128, NT * 2], F32)
        csb = P.sbuf("csb", [128, NT * NE], F32)
        cnt_sb = P.sbuf("cnt_sb", [NCORES, 2 * NE], F32)
        mlt = P.sbuf("mlt", [NCORES, 2], F32)
        runs = P.sbuf("runs", [1, 4 * NE], F32)
        b1t = P.sbuf("b1t", [128, 32 * NE], F32)
        b1sb = P.sbuf("b1sb", [NE, 128], F32)
        b2row = P.sbuf("b2row", [1, DM], F32)
        kp = P.sbuf("kp", [128, 4 * NE], F32)
        st = P.sbuf("st", [128, 8], F32)
        xn16t = P.sbuf("xn16t", [128, DM], BF16)
        ytmp = P.sbuf("ytmp", [128, DM], F32)
        sqt = P.sbuf("sqt", [128, DM], F32)
        onesr32 = P.sbuf("onesr32", [1, 128], F32)
        onescol32 = P.sbuf("onescol32", [128, 1], F32)
        id_bf = P.sbuf("id_bf", [128, 128], BF16)
        id_32 = P.sbuf("id_32", [128, 128], F32)
        P.bufs["ln_sq"] = sqt

        psA = P.psum("psA", [128, 512], F32)
        psB = P.psum("psB", [128, 512], F32)
        psC = P.psum("psC", [128, 512], F32)
        psD = P.psum("psD", [128, 512], F32)
        psE = P.psum("psE", [128, 512], F32)
        psF = P.psum("psF", [128, 512], F32)
        pt0 = P.psum("pt0", [128, 1024], BF16)
        pt1 = P.psum("pt1", [128, 1024], BF16)

        emit_identity(P, id_bf, id_32, scratch=xn_fm)

        P.on("sp", lambda e: e.dma_start(
            out=x_res[:, :].rearrange("p (t j) -> p t j", t=NT),
            in_=x3.ap().rearrange("(t p) j -> p t j", p=128)), inc=("dma", 16))
        P.on("sp", lambda e: e.dma_start(
            out=h1sb[:, :].rearrange("p (t e) -> p t e", t=NT),
            in_=h1i.ap().rearrange("(t p) e -> p t e", p=128)), inc=("dma", 16))
        P.on("sp", lambda e: e.dma_start(
            out=h2sb[:, :].rearrange("p (t e) -> p t e", t=NT),
            in_=h2i.ap().rearrange("(t p) e -> p t e", p=128)), inc=("dma", 16))
        P.on("sp", lambda e: e.dma_start(
            out=w12sb[:, :].rearrange("p (t e) -> p t e", t=NT),
            in_=w12i.ap().rearrange("(t p) e -> p t e", p=128)), inc=("dma", 16))
        P.on("sp", lambda e: e.dma_start(out=cnt_sb[:, :], in_=cntall[:, :]), inc=("dma", 16))
        P.on("sp", lambda e: e.dma_start(out=mlt[:, 0:1], in_=masklt[:, :]), inc=("dma", 16))
        P.on("sp", lambda e: e.dma_start(out=tri_sb[:, :], in_=tri[:, :]), inc=("dma", 16))
        P.wait("dve", "dma")
        P.on("dve", lambda e: e.memset(onesr32[:, :], 1.0))
        P.on("dve", lambda e: e.memset(onescol32[:, :], 1.0))
        P.on("dve", lambda e: e.memset(mlt[:, 1:2], 1.0))
        P.on("dve", lambda e: e.tensor_copy(h1b[:, :], h1sb[:, :]))
        P.on("dve", lambda e: e.tensor_copy(h2b[:, :], h2sb[:, :]), inc="dve")
        P.wait("pe", "dve")

        for hc in range(32):
            pp = psE if hc % 2 == 0 else psF
            P.wait("sp", "b1td")
            P.on("sp", lambda e, hc=hc: e.dma_start(
                out=b1sb[:, :], in_=b1[:, hc * 128:(hc + 1) * 128]), inc=("dma", 16))
            P.wait("pe", "dma")
            if hc >= 2:
                P.wait_at("pe", "b1td", P.val("b1td") - 1)
            P.on("pe", lambda e, hc=hc, pp=pp: e.transpose(
                pp[0:128, 0:NE], b1sb[0:NE, 0:128], id_32[0:NE, 0:NE]), inc="b1tp")
            P.wait("dve", "b1tp")
            P.on("dve", lambda e, hc=hc, pp=pp: e.tensor_copy(
                b1t[:, hc * NE:(hc + 1) * NE], pp[0:128, 0:NE]), inc="b1td")

        P.wait("pe", "b1td")
        P.on("pe", lambda e: e.matmul(psA[0:1, 0:NE], mlt[:, 0:1], cnt_sb[:, 0:NE],
                                      start=True, stop=True))
        P.on("pe", lambda e: e.matmul(psB[0:1, 0:NE], mlt[:, 1:2], cnt_sb[:, 0:NE],
                                      start=True, stop=False))
        P.on("pe", lambda e: e.matmul(psB[0:1, 0:NE], mlt[:, 0:1], cnt_sb[:, NE:2 * NE],
                                      start=False, stop=True), inc="bse")
        P.wait("dve", "bse")
        P.on("dve", lambda e: e.tensor_copy(runs[0:1, 2 * NE:3 * NE], psA[0:1, 0:NE]))
        P.on("dve", lambda e: e.tensor_copy(runs[0:1, 3 * NE:4 * NE], psB[0:1, 0:NE]))
        P.on("dve", lambda e: e.tensor_copy(runs[0:1, 0:NE], psA[0:1, 0:NE]))
        P.on("dve", lambda e: e.tensor_copy(runs[0:1, NE:2 * NE], psB[0:1, 0:NE]), inc="rn")
        P.wait("pe", "rn")

        if stage == "prep":
            P.wait("sp", "rn")
            P.on("sp", lambda e: e.dma_start(
                out=outo.ap().rearrange("(t p) j -> p t j", p=128),
                in_=x_res[:, :].rearrange("p (t j) -> p t j", t=NT)), inc=("dma", 16))
            P.wait("sp", "dma")
            P.finalize()
            return nc
        for i in range(NT):
            P.wait("dve", "g_tp")
            emit_ln(P, i * DM, 0, x_res, None, xn16t, st, "ln", n_tiles=1, sem_suffix="g1",
                    inc_last="g_ln")
            P.wait("pe", "g_ln")
            transpose_tm_to_fm(P, lambda ii, c: xn16t[:, c * 128:(c + 1) * 128],
                               xn_fm, lambda ii, c, i=i: c * T + i * 128, 1, NC, pt0, pt1,
                               pe_sem="g_tp2", dve_sem="g_tp")
            for kk in range(2):
                pp = psC if kk == 0 else psD
                hb = h1b if kk == 0 else h2b
                P.on("pe", lambda e, pp=pp, hb=hb, i=i: e.matmul(
                    pp[:, 0:NE], tri_sb[:, :], hb[:, i * NE:(i + 1) * NE],
                    start=True, stop=False))
                P.on("pe", lambda e, pp=pp, kk=kk: e.matmul(
                    pp[:, 0:NE], onesr32[0:1, :], runs[0:1, kk * NE:(kk + 1) * NE],
                    start=False, stop=True), inc="pos")
                pq = psE if kk == 0 else psF
                hs = h1sb if kk == 0 else h2sb
                P.on("pe", lambda e, pq=pq, hs=hs, i=i: e.matmul(
                    pq[0:1, 0:NE], onescol32[:, 0:1], hs[:, i * NE:(i + 1) * NE],
                    start=True, stop=True), inc="tot")
            P.wait("dve", "pos")
            for kk in range(2):
                pp = psC if kk == 0 else psD
                hsb = h1sb if kk == 0 else h2sb
                P.on("dve", lambda e, pp=pp, kk=kk: e.tensor_scalar(
                    kp[:, kk * NE:(kk + 1) * NE], pp[:, 0:NE], CAP + 0.5, None, ALU.is_lt))
            P.on("dve", lambda e: e.engine_nop(), inc="kp1")
            P.wait("dve", "kp1")
            for kk in range(2):
                hsb = h1sb if kk == 0 else h2sb
                P.on("dve", lambda e, hsb=hsb, kk=kk, i=i: e.tensor_tensor(
                    kp[:, (2 + kk) * NE:(3 + kk) * NE], kp[:, kk * NE:(kk + 1) * NE],
                    hsb[:, i * NE:(i + 1) * NE], ALU.mult))
            P.on("dve", lambda e: e.engine_nop(), inc="kp2")
            P.wait("dve", "kp2")
            P.on("dve", lambda e, i=i: e.tensor_scalar(
                kp[:, 2 * NE:3 * NE], kp[:, 2 * NE:3 * NE], w12sb[:, i * 2:i * 2 + 1], None,
                ALU.mult))
            P.on("dve", lambda e, i=i: e.tensor_scalar(
                kp[:, 3 * NE:4 * NE], kp[:, 3 * NE:4 * NE], w12sb[:, i * 2 + 1:i * 2 + 2], None,
                ALU.mult), inc="kp3")
            P.wait("dve", "kp3")
            P.on("dve", lambda e, i=i: e.tensor_tensor(
                csb[:, i * NE:(i + 1) * NE], kp[:, 2 * NE:3 * NE], kp[:, 3 * NE:4 * NE],
                ALU.add), inc="kp4")
            P.wait("dve", "tot")
            P.on("dve", lambda e: e.tensor_tensor(runs[0:1, 0:NE], runs[0:1, 0:NE],
                                                  psE[0:1, 0:NE], ALU.add))
            P.on("dve", lambda e: e.tensor_tensor(runs[0:1, NE:2 * NE], runs[0:1, NE:2 * NE],
                                                  psF[0:1, 0:NE], ALU.add), inc="rn2")
            P.wait("pe", "rn2")
        P.wait("pe", "kp4")
        P.wait("pe", "rn2")

        if stage == "route":
            P.wait("sp", "kp4")
            P.on("sp", lambda e: e.dma_start(
                out=outo[0:128, 0:NT * NE], in_=csb[:, :]), inc=("dma", 16))
            P.wait("sp", "dma")
            P.finalize()
            return nc
        hcnt = 0
        NEx = 1 if stage in ("e1", "e1nb2", "e1noy") else NE
        for e_ in range(NEx):
            P.wait("sp", "g_ym")
            P.on("sp", lambda e, e_=e_: e.dma_start(out=b2row[0:1, :], in_=b2[e_:e_ + 1, :]),
                 inc=("dma", 16))
            for half in range(2):
                P.wait("pool", "g_hm")
                P.on("pool", lambda e, e_=e_, half=half: e.dma_start(
                    out=w1b.rearrange("p (c j) -> p c j", c=NC),
                    in_=w1.ap()[e_].rearrange("(c p) j -> p c j", p=128)
                    [:, :, half * 2048:(half + 1) * 2048]), inc=("dmap", 16))
                P.wait("pool", "g_ym")
                P.on("pool", lambda e, e_=e_, half=half: e.dma_start(
                    out=w2b.rearrange("p (c j) -> p c j", c=16),
                    in_=w2.ap()[e_].rearrange("(c p) j -> p c j", p=128)
                    [:, half * 16:(half + 1) * 16, :]), inc=("dmap", 16))
                P.wait("pe", "dmap")
                P.wait("pe", "dma")
                for tc in range(4):
                    for hc in range(16):
                        pp = psA if hcnt % 2 == 0 else psB
                        if hcnt >= 2:
                            P.wait_at("pe", "g_ha", P.val("g_ha") - 1)
                        for k in range(NC):
                            P.on("pe", lambda e, pp=pp, tc=tc, hc=hc, k=k: e.matmul(
                                pp[:, :], w1b[:, k * 2048 + hc * 128:k * 2048 + (hc + 1) * 128],
                                xn_fm[:, k * T + tc * 512:k * T + tc * 512 + 512],
                                start=(k == 0), stop=(k == NC - 1)),
                                inc="g_hm" if k == NC - 1 else None)
                        P.wait("act", "g_hm")
                        P.on("act", lambda e, pp=pp, hc=hc, half=half, e_=e_: e.activation(
                            hbuf[:, hc * 512:(hc + 1) * 512], pp[:, :], AF.Gelu,
                            bias=b1t[:, (half * 16 + hc) * NE + e_:(half * 16 + hc) * NE + e_ + 1]),
                            inc="g_ha")
                        hcnt += 1
                    P.wait("pe", "g_ha")
                    if stage == "e1noy":
                        continue
                    for ts in range(4):
                        pp0, pp1 = (psC, psD) if ts % 2 == 0 else (psE, psF)
                        P.wait_at("pe", "g_ya", max(0, P.val("g_ya") - 1))
                        fold_b2 = (half == 1 and stage != "e1nb2")
                        for k in range(16):
                            last = (k == 15) and not fold_b2
                            P.on("pe", lambda e, pp0=pp0, ts=ts, k=k, last=last: e.matmul(
                                pp0[:, :], hbuf[:, k * 512 + ts * 128:k * 512 + ts * 128 + 128],
                                w2b[:, k * DM:k * DM + 512],
                                start=(k == 0), stop=(k == 15 and not fold_b2)))
                            P.on("pe", lambda e, pp1=pp1, ts=ts, k=k: e.matmul(
                                pp1[:, :], hbuf[:, k * 512 + ts * 128:k * 512 + ts * 128 + 128],
                                w2b[:, k * DM + 512:(k + 1) * DM],
                                start=(k == 0), stop=(k == 15 and not fold_b2)),
                                inc="g_ym" if last else None)
                        if fold_b2:
                            P.on("pe", lambda e, pp0=pp0: e.matmul(
                                pp0[:, :], onesr32[0:1, :],
                                b2row[0:1, 0:512], start=False, stop=True))
                            P.on("pe", lambda e, pp1=pp1: e.matmul(
                                pp1[:, :], onesr32[0:1, :], b2row[0:1, 512:DM],
                                start=False, stop=True), inc="g_ym")
                        P.wait("act", "g_ym")
                        t_g = tc * 4 + ts
                        P.wait("act", "g_yd")
                        P.on("act", lambda e, pp0=pp0, t_g=t_g, e_=e_: e.activation(
                            ytmp[:, 0:512], pp0[:, :], AF.Copy,
                            scale=csb[:, t_g * NE + e_:t_g * NE + e_ + 1]))
                        P.on("act", lambda e, pp1=pp1, t_g=t_g, e_=e_: e.activation(
                            ytmp[:, 512:DM], pp1[:, :], AF.Copy,
                            scale=csb[:, t_g * NE + e_:t_g * NE + e_ + 1]), inc="g_ya")
                        P.wait("dve", "g_ya")
                        P.on("dve", lambda e, t_g=t_g: e.tensor_tensor(
                            x_res[:, t_g * DM:(t_g + 1) * DM], x_res[:, t_g * DM:(t_g + 1) * DM],
                            ytmp[:, :], ALU.add), inc="g_yd")
        if stage == "e1noy":
            P.wait("sp", "g_ha")
        else:
            P.wait("sp", "g_yd")
        P.on("sp", lambda e: e.dma_start(
            out=outo.ap().rearrange("(t p) j -> p t j", p=128),
            in_=x_res[:, :].rearrange("p (t j) -> p t j", t=NT)), inc=("dma", 16))
        P.wait("sp", "dma")
        P.finalize()
    return nc



def _rope_tables():
    inv = 1.0 / 10000.0 ** (np.arange(0, DH, 2, dtype=np.float64) / DH)
    pos = np.arange(SEQ, dtype=np.float64)[:, None] * inv[None, :]
    cos = np.cos(pos).astype(np.float32)
    sin = np.sin(pos).astype(np.float32)
    A = np.zeros((SEQ, DM), np.float32)
    Bt = np.zeros((SEQ, DM), np.float32)
    for h in range(NH):
        A[:, h * 64:h * 64 + 32] = cos
        A[:, h * 64 + 32:h * 64 + 64] = cos
        Bt[:, h * 64:h * 64 + 32] = -sin
        Bt[:, h * 64 + 32:h * 64 + 64] = sin
    return A, Bt


def run_A(x_full, Wq, Wk, Wv):
    A, Bt = _rope_tables()
    xs = x_full.reshape(NCORES, T, DM)
    nc = build_A()
    in_maps = [{"x": np.ascontiguousarray(xs[c]), "wq": Wq, "wk": Wk, "wv": Wv,
                "ra": np.ascontiguousarray(A[(c % 2) * T:(c % 2 + 1) * T]),
                "rb": np.ascontiguousarray(Bt[(c % 2) * T:(c % 2 + 1) * T])}
               for c in range(NCORES)]
    res = run_bass_kernel_spmd(nc, in_maps, core_ids=list(range(NCORES)))
    q = np.stack([res.results[c]["qo"] for c in range(NCORES)]).reshape(B, SEQ, DM)
    k = np.stack([res.results[c]["ko"] for c in range(NCORES)]).reshape(B, SEQ, DM)
    v = np.stack([res.results[c]["vo"] for c in range(NCORES)]).reshape(B, SEQ, DM)
    return q, k, v


def run_BC(x_full, q, k, v, Wout, gamma, beta, router_w, W1, W2):
    xs = x_full.reshape(NCORES, T, DM)
    qs = np.asarray(q, dtype=BF).reshape(NCORES, T, DM)
    kk = np.asarray(k, dtype=BF)
    vv = np.asarray(v, dtype=BF)
    maskrow = np.zeros((1, SEQ), np.float32)
    maskrow[0, :NG] = -10000.0
    nc = build_BC()
    in_maps = []
    for c in range(NCORES):
        b = c // 2
        in_maps.append({
            "x": np.ascontiguousarray(xs[c]), "qin": np.ascontiguousarray(qs[c]),
            "kseq": np.ascontiguousarray(kk[b]), "vseq": np.ascontiguousarray(vv[b]),
            "wout": Wout, "maskrow": maskrow,
            "flagc": np.full((128, 1), 1.0 - (c % 2), np.float32),
            "gam": gamma.reshape(1, DM), "bet": beta.reshape(1, DM),
            "rw": router_w, "w1": W1, "w2": W2,
        })
    res = run_bass_kernel_spmd(nc, in_maps, core_ids=list(range(NCORES)))
    return np.stack([res.results[c]["x2o"] for c in range(NCORES)]).reshape(B, SEQ, DM)


def run_D(x2_full, Wqk, Wv, Wo, tau, ln1w, ln1b, ln2w, ln2b, Wqkc, g1):
    xs = x2_full.reshape(NCORES, T, DM)
    nc = build_D()
    in_maps = []
    for c in range(NCORES):
        pair = c ^ 1
        x2p = np.concatenate([xs[c], xs[pair]], axis=0)
        in_maps.append({
            "x2": np.ascontiguousarray(x2p),
            "ln1w": ln1w.reshape(1, DM), "ln1b": ln1b.reshape(1, DM),
            "ln2w": ln2w.reshape(1, DM), "ln2b": ln2b.reshape(1, DM),
            "wqk": Wqk, "wv": Wv, "wo": Wo, "wqkc": Wqkc,
            "taur": np.ascontiguousarray(tau.reshape(1, NH)),
            "g1r": g1.reshape(1, DM),
        })
    res = run_bass_kernel_spmd(nc, in_maps, core_ids=list(range(NCORES)))
    xmid = np.stack([res.results[c]["xmido"] for c in range(NCORES)])
    dots = np.stack([res.results[c]["dotso"] for c in range(NCORES)])
    q2c = np.stack([res.results[c]["q2co"] for c in range(NCORES)])
    return xmid, dots, q2c


def run_E(xmid, x2_full, dots, q2c, ln2w, ln2b, Wvc, Woc, tauc, g2, wg):
    xs = x2_full.reshape(NCORES, T, DM)
    nc = build_E()
    in_maps = []
    for c in range(NCORES):
        pair = c ^ 1
        in_maps.append({
            "xmid": np.ascontiguousarray(xmid[c]), "x2own": np.ascontiguousarray(xs[c]),
            "dotsa": np.ascontiguousarray(dots[c]), "dotsb": np.ascontiguousarray(dots[pair]),
            "q2a": np.ascontiguousarray(q2c[c]), "q2b": np.ascontiguousarray(q2c[pair]),
            "ln2w": ln2w.reshape(1, DM), "ln2b": ln2b.reshape(1, DM),
            "wvc": Wvc, "woc": Woc, "taucr": np.ascontiguousarray(tauc.reshape(1, 8)),
            "g2r": g2.reshape(1, DM), "wg": wg,
        })
    res = run_bass_kernel_spmd(nc, in_maps, core_ids=list(range(NCORES)))
    x3 = np.stack([res.results[c]["x3o"] for c in range(NCORES)])
    h1 = np.stack([res.results[c]["h1o"] for c in range(NCORES)])
    h2 = np.stack([res.results[c]["h2o"] for c in range(NCORES)])
    w12 = np.stack([res.results[c]["w12o"] for c in range(NCORES)])
    cnt = np.stack([res.results[c]["cnto"] for c in range(NCORES)])
    return x3, h1, h2, w12, cnt


def run_G(x3, h1, h2, w12, cnt, w1, b1, w2, b2, stage="full"):
    nc = build_G(stage)
    tri = np.tril(np.ones((128, 128), np.float32)).T.astype(BF)
    cntall = cnt.transpose(0, 2, 1).reshape(NCORES, 2 * NE)
    cntall = np.concatenate([cnt[:, :, 0], cnt[:, :, 1]], axis=1).astype(np.float32)
    in_maps = []
    for c in range(NCORES):
        mlt = (np.arange(NCORES) < c).astype(np.float32).reshape(NCORES, 1)
        in_maps.append({
            "x3": np.ascontiguousarray(x3[c]), "h1i": np.ascontiguousarray(h1[c]),
            "h2i": np.ascontiguousarray(h2[c]), "w12i": np.ascontiguousarray(w12[c]),
            "cntall": cntall, "masklt": mlt, "tri": tri,
            "w1": w1, "b1": b1, "w2": w2, "b2": b2,
        })
    res = run_bass_kernel_spmd(nc, in_maps, core_ids=list(range(NCORES)))
    return np.stack([res.results[c]["outo"] for c in range(NCORES)])


def kernel(**inputs):
    x = np.ascontiguousarray(inputs["L0_attn_x"], dtype=np.float32)
    q, k, v = run_A(x, inputs["L0_attn_Wq"], inputs["L0_attn_Wk"], inputs["L0_attn_Wv"])
    x2 = run_BC(x, q, k, v, inputs["L0_attn_Wout"], inputs["L0_ffn_gamma"],
                inputs["L0_ffn_beta"], inputs["L0_ffn_router_w"],
                inputs["L0_ffn_W1"], inputs["L0_ffn_W2"])
    xmid, dots, q2c = run_D(x2.reshape(-1, DM), inputs["L1_attn_Wqk_t"],
                            inputs["L1_attn_Wv_t"], inputs["L1_attn_Wo_t"],
                            inputs["L1_attn_tau_t"], inputs["L1_attn_ln1_w"],
                            inputs["L1_attn_ln1_b"], inputs["L1_attn_ln2_w"],
                            inputs["L1_attn_ln2_b"], inputs["L1_attn_Wqk_c"],
                            inputs["L1_attn_gamma1"])
    x3, h1, h2, w12, cnt = run_E(xmid, x2.reshape(-1, DM), dots, q2c,
                                 inputs["L1_attn_ln2_w"], inputs["L1_attn_ln2_b"],
                                 inputs["L1_attn_Wv_c"], inputs["L1_attn_Wo_c"],
                                 inputs["L1_attn_tau_c"], inputs["L1_attn_gamma2"],
                                 inputs["L1_ffn_wg"])
    out = run_G(x3, h1, h2, w12, cnt, inputs["L1_ffn_w1"], inputs["L1_ffn_b1"],
                inputs["L1_ffn_w2"], inputs["L1_ffn_b2"])
    return np.ascontiguousarray(out.reshape(B, SEQ, DM), dtype=np.float32)


if __name__ == "__main__":
    pass


# version 02
# speedup vs baseline: 1.1805x; 1.1805x over previous
import contextlib
import math
import numpy as np
import ml_dtypes
import concourse.bass as bass
import concourse.mybir as mybir
from concourse.bass_utils import run_bass_kernel_spmd

F32 = mybir.dt.float32
BF16 = mybir.dt.bfloat16
U32 = mybir.dt.uint32
AF = mybir.ActivationFunctionType
ALU = mybir.AluOpType
AX = mybir.AxisListType

NCORES = 8
B, SEQ, DM = 4, 4096, 1024
T = 2048
NT = T // 128
NC = DM // 128
NH, DH = 16, 64
NG = 409
NE, HI = 8, 4096
CAP = 4096
EPS = 1e-5
SCALE_G = DH ** -0.5

BF = ml_dtypes.bfloat16

DEBUG = False

_QMAP = dict(pe="tensor", dve="vector", act="scalar", pool="gpsimd", sp="sync")


class Pr:

    def __init__(self, nc, es):
        self.nc = nc
        self.es = es
        self.q = {k: [] for k in _QMAP}
        self.sems = {}
        self.cnt = {}
        self.bufs = {}

    def sem(self, name):
        if name not in self.sems:
            self.sems[name] = self.es.enter_context(self.nc.semaphore(name))
            self.cnt[name] = 0
        return self.sems[name]

    def set_arena(self, n_bf16_elts):
        self.arena = self.es.enter_context(
            self.nc.sbuf_tensor("arena", [128, n_bf16_elts], BF16))
        self.arena_elts = n_bf16_elts
        self.cursor = 0

    def begin_section(self, name=""):
        self.cursor = 0
        self.bufs = {}
        self.section = name

    def sbuf(self, name, shape, dtype):
        if getattr(self, "arena", None) is None:
            t = self.es.enter_context(self.nc.sbuf_tensor(name, shape, dtype))
            self.bufs[name] = t
            return t
        mult = {BF16: 1, F32: 2, U32: 2}[dtype]
        n = shape[1] * mult
        self.cursor = (self.cursor + 1) & ~1
        assert self.cursor + n <= self.arena_elts, (
            f"arena overflow at {self.section}/{name}: {self.cursor}+{n}")
        v = self.arena[0:shape[0], self.cursor:self.cursor + n]
        self.cursor += n
        if dtype != BF16:
            v = v.bitcast(dtype)
        self.bufs[name] = v
        return v

    def psum(self, name, shape, dtype):
        if name in getattr(self, "psums", {}):
            return self.psums[name]
        t = self.es.enter_context(self.nc.psum_tensor(name, shape, dtype))
        if not hasattr(self, "psums"):
            self.psums = {}
        self.psums[name] = t
        self.bufs[name] = t
        return t

    def barrier(self, tag):
        for qn in _QMAP:
            if qn in ("dve", "pool"):
                self.on(qn, lambda e: e.engine_nop(), inc=tag)
            else:
                self.on(qn, lambda e: e.nop(), inc=tag)
        for qn in _QMAP:
            self.wait(qn, tag)

    def on(self, qn, fn, inc=None):
        if inc is not None:
            s, n = inc if isinstance(inc, tuple) else (inc, 1)
            h = self.sem(s)
            self.cnt[s] += n

            def wrapped(e, fn=fn, h=h, n=n):
                r = fn(e)
                r.then_inc(h, n)
            self.q[qn].append(wrapped)
        else:
            self.q[qn].append(fn)

    def wait(self, qn, s, minus=0):
        h = self.sem(s)
        v = self.cnt[s] - minus
        self.q[qn].append(lambda e, h=h, v=v: e.wait_ge(h, v))

    def wait_at(self, qn, s, v):
        h = self.sem(s)
        self.q[qn].append(lambda e, h=h, v=v: e.wait_ge(h, v))

    def val(self, s):
        self.sem(s)
        return self.cnt[s]

    def finalize(self):
        with self.nc.Block() as b:
            for qn, attr in _QMAP.items():
                ops = self.q[qn]
                if not ops:
                    continue

                def runner(e, ops=ops):
                    for f in ops:
                        f(e)
                getattr(b, attr)(runner)



def load_w_chunks(P, name, src_ap, dout, sem="dmap"):
    wb = P.sbuf(name, [128, NC * dout], BF16)
    P.on("pool", lambda e, wb=wb, src=src_ap, dout=dout: e.dma_start(
        out=wb[:, :].rearrange("p (c j) -> p c j", c=NC),
        in_=src.rearrange("(c p) j -> p c j", p=128)), inc=(sem, 16))
    return wb


def emit_ln(P, src_col, dst_col, x_sb, o32_sb, o16_sb, st, pfx,
            w_b=None, b_b=None, n_tiles=NT, ncols=DM, sem_suffix="", inc_last=None):
    sq = P.bufs[pfx + "_sq"]
    dvesem = "dve" + sem_suffix
    actsem = "act" + sem_suffix
    for i in range(n_tiles):
        xs = x_sb[:, src_col + i * ncols: src_col + (i + 1) * ncols]
        P.on("dve", lambda e, xs=xs, st=st, i=i: e.reduce_sum(
            st[:, 4 * i:4 * i + 1], xs, axis=AX.X))
        P.on("dve", lambda e, xs=xs, sq=sq: e.tensor_tensor(sq[:, :ncols], xs, xs, ALU.mult))
        last = (i == n_tiles - 1)
        P.on("dve", lambda e, sq=sq, st=st, i=i: e.reduce_sum(
            st[:, 4 * i + 1:4 * i + 2], sq[:, :ncols], axis=AX.X),
            inc=dvesem if last else None)
    P.wait("dve", dvesem)
    sv = st.ap().rearrange("p (t f) -> p t f", f=4)

    def stats1(e, sv=sv):
        e.tensor_scalar(sv[:, :, 2:3], sv[:, :, 0:1], 1.0 / ncols, None, ALU.mult)
        return e.tensor_scalar(sv[:, :, 1:2], sv[:, :, 1:2], 1.0 / ncols, None, ALU.mult)
    P.on("dve", stats1, inc=dvesem)
    P.wait("dve", dvesem)

    def stats2(e, sv=sv):
        e.tensor_tensor(sv[:, :, 3:4], sv[:, :, 2:3], sv[:, :, 2:3], ALU.mult)
        return e.engine_nop()
    P.on("dve", stats2, inc=dvesem)
    P.wait("dve", dvesem)

    def stats3(e, sv=sv):
        e.tensor_tensor(sv[:, :, 3:4], sv[:, :, 1:2], sv[:, :, 3:4], ALU.subtract)
        return e.engine_nop()
    P.on("dve", stats3, inc=dvesem)
    P.wait("dve", dvesem)
    P.on("dve", lambda e, sv=sv: e.tensor_scalar(sv[:, :, 3:4], sv[:, :, 3:4], EPS, None, ALU.add),
         inc=dvesem)
    P.wait("act", dvesem)
    act_pow(P, sv[:, :, 3:4], sv[:, :, 3:4], -0.5, actsem)
    P.wait("dve", actsem)
    ops = []
    for i in range(n_tiles):
        xs = x_sb[:, src_col + i * ncols: src_col + (i + 1) * ncols]
        m_ = st[:, 4 * i + 2:4 * i + 3]
        r_ = st[:, 4 * i + 3:4 * i + 4]
        outs = []
        if o32_sb is not None:
            outs.append(o32_sb[:, dst_col + i * ncols: dst_col + (i + 1) * ncols])
        if o16_sb is not None:
            outs.append(o16_sb[:, dst_col + i * ncols: dst_col + (i + 1) * ncols])
        first = outs[0]
        ops.append(lambda e, xs=xs, m_=m_, r_=r_, o=first: e.tensor_scalar(
            o, xs, m_, r_, ALU.subtract, ALU.mult))
        if w_b is not None:
            ops.append(lambda e, o=first, w=w_b: e.tensor_tensor(o, o, w[:, :ncols], ALU.mult))
        if b_b is not None:
            ops.append(lambda e, o=first, bb=b_b: e.tensor_tensor(o, o, bb[:, :ncols], ALU.add))
        for o2 in outs[1:]:
            ops.append(lambda e, o=first, o2=o2: e.tensor_copy(o2, o))
    for j, op in enumerate(ops):
        P.on("dve", op, inc=inc_last if j == len(ops) - 1 else None)


def act_pow(P, dst, src, power, sem, rows=None):
    s = slice(0, rows) if rows is not None else slice(None)
    P.on("act", lambda e: e.activation(dst[s] if rows is not None else dst, 
                                       src[s] if rows is not None else src, AF.Ln), inc=sem)
    P.wait("act", sem)
    P.on("act", lambda e: e.activation(dst[s] if rows is not None else dst,
                                       dst[s] if rows is not None else dst, AF.Exp,
                                       scale=float(power)), inc=sem)


def emit_bcast_row(P, dst_sb, src_row_ap, ones1, ps, width, dve_sem="dve", pe_sem="pe"):
    nchunks = (width + 511) // 512
    for ci in range(nchunks):
        w = min(512, width - ci * 512)
        P.on("pe", lambda e, ps=ps, o=ones1, s=src_row_ap, ci=ci, w=w: e.matmul(
            ps[:, 0:w], o[0:1, 0:128], s[0:1, ci * 512:ci * 512 + w],
            start=True, stop=True), inc=pe_sem)
        P.wait("dve", pe_sem)
        P.on("dve", lambda e, ps=ps, d=dst_sb, ci=ci, w=w: e.tensor_copy(
            d[:, ci * 512:ci * 512 + w], ps[:, 0:w]), inc=dve_sem)
        P.wait("pe", dve_sem)


def transpose_tm_to_fm(P, src_fn, dst_sb, dst_col_fn, n_trow, n_tcol, pt0, pt1,
                       rows=128, pe_sem="tp_pe", dve_sem="tp_dve"):
    idt = P.bufs["id_bf"]
    n = 0
    for i in range(n_trow):
        for c in range(n_tcol):
            pp = pt0 if n % 2 == 0 else pt1
            if n >= 2:
                P.wait_at("pe", dve_sem, P.val(dve_sem) - 1)
            P.on("pe", lambda e, pp=pp, s=src_fn(i, c), idt=idt, rows=rows: e.transpose(
                pp[:, 0:rows], s, idt[0:rows, 0:rows]), inc=pe_sem)
            P.wait("dve", pe_sem)
            col = dst_col_fn(i, c)
            P.on("dve", lambda e, pp=pp, d=dst_sb, col=col, rows=rows: e.tensor_copy(
                d[:, col:col + rows], pp[:, 0:rows]), inc=dve_sem)
            n += 1
    P.wait("pe", dve_sem)



def emit_A(P, io, ntiles=NT):
    nc = P.nc
    x = io["x"]
    wq = io["wq"]
    wk = io["wk"]
    wv = io["wv"]
    ra = io["ra"]
    rb = io["rb"]
    qo = io["qo"]
    ko = io["ko"]
    vo = io["vo"]
    x_sb = P.sbuf("x_sb", [128, NT * DM], F32)
    xn16 = P.sbuf("xn16", [128, NT * DM], BF16)
    xfm = P.sbuf("xfm", [128, NC * T], BF16)
    st = P.sbuf("st", [128, 4 * NT], F32)
    P.sbuf("ln_sq", [128, DM], F32)
    id_bf = P.sbuf("id_bf", [128, 128], BF16)
    wqb = load_w_chunks(P, "wqb", wq.ap(), DM)
    wkb = load_w_chunks(P, "wkb", wk.ap(), DM)
    wvb = load_w_chunks(P, "wvb", wv.ap(), DM)
    rat = P.sbuf("rat", [128, 2 * DM], F32)
    rbt = P.sbuf("rbt", [128, 2 * DM], F32)
    otile = P.sbuf("otile", [128, 3 * DM], BF16)
    rsc = P.sbuf("rsc", [128, DM], F32)
    idram = nc.dram_tensor("id_dram", [128, 128], BF16)

    psA = P.psum("psA", [128, 512], F32)
    psB = P.psum("psB", [128, 512], F32)
    psC = P.psum("psC", [128, 512], F32)
    psD = P.psum("psD", [128, 512], F32)
    pt0 = P.psum("pt0", [128, 1024], BF16)
    pt1 = P.psum("pt1", [128, 1024], BF16)

    iot1 = P.sbuf("iot1", [128, 128], F32)
    iot2 = P.sbuf("iot2", [128, 1], F32)
    P.on("pool", lambda e: e.iota(iot1[:, :], [[1, 128]], base=0,
                                  channel_multiplier=0, allow_small_or_imprecise_dtypes=True))
    P.on("pool", lambda e: e.iota(iot2[:, :], [[1, 1]], base=0,
                                  channel_multiplier=1, allow_small_or_imprecise_dtypes=True),
         inc="pool")
    P.wait("dve", "pool")
    P.on("dve", lambda e: e.tensor_scalar(id_bf[:, :], iot1[:, :], iot2[:, 0:1],
                                          None, ALU.is_equal), inc="dve")

    P.on("sp", lambda e: e.dma_start(
        out=x_sb[:, :].rearrange("p (t j) -> p t j", t=NT),
        in_=x.ap().rearrange("(t p) j -> p t j", p=128)), inc=("dma", 16))
    P.wait("dve", "dma")
    P.wait("act", "dma")

    emit_ln(P, 0, 0, x_sb, None, xn16, st, "ln", inc_last="dve")
    P.wait("pe", "dve")

    transpose_tm_to_fm(P, lambda i, c: xn16[:, i * DM + c * 128: i * DM + (c + 1) * 128],
                       xfm, lambda i, c: c * T + i * 128, NT, NC, pt0, pt1)

    P.wait("pool", "dmap")
    P.on("pool", lambda e: e.engine_nop(), inc="pool")
    P.wait("pe", "pool")
    P.wait("sp", "dve")

    names = [("q", wqb, qo, True), ("k", wkb, ko, True), ("v", wvb, vo, False)]
    pspair = [(psA, psB), (psC, psD)]
    evac_cnt = 0
    for i in range(NT):
        rslot = i % 2
        P.on("sp", lambda e, i=i, rslot=rslot: e.dma_start(
            out=rat[:, rslot * DM:(rslot + 1) * DM], in_=ra[i * 128:(i + 1) * 128, :]),
            inc=("dma", 16))
        P.on("sp", lambda e, i=i, rslot=rslot: e.dma_start(
            out=rbt[:, rslot * DM:(rslot + 1) * DM], in_=rb[i * 128:(i + 1) * 128, :]),
            inc=("dma", 16))
        for tname, wb, outd, do_rope in names:
            psu = pspair[evac_cnt % 2]
            if evac_cnt >= 2:
                P.wait_at("pe", "evac", P.val("evac") - 1)
            for half in range(2):
                ps = psu[half]
                for k in range(NC):
                    P.on("pe", lambda e, ps=ps, xfm=xfm, wb=wb, i=i, k=k, half=half: e.matmul(
                        ps[:, :], xfm[:, k * T + i * 128: k * T + (i + 1) * 128],
                        wb[:, k * DM + half * 512: k * DM + half * 512 + 512],
                        start=(k == 0), stop=(k == NC - 1)),
                        inc="pe" if k == NC - 1 else None)
            P.wait("dve", "pe")
            P.wait("dve", "dma")
            if do_rope:
                oslot = {"q": 0, "k": 1}[tname]
                emit_rope_evac(P, psu, rat, rbt, rslot, rsc, otile, oslot, inc_last="evac")
            else:
                P.on("dve", lambda e, ps=psu[0]: e.tensor_copy(otile[:, 2 * DM: 2 * DM + 512], ps[:, :]))
                P.on("dve", lambda e, ps=psu[1]: e.tensor_copy(otile[:, 2 * DM + 512: 3 * DM], ps[:, :]),
                     inc="evac")
            P.wait("sp", "evac")
            oslot = {"q": 0, "k": 1, "v": 2}[tname]
            P.on("sp", lambda e, outd=outd, i=i, oslot=oslot: e.dma_start(
                out=outd[i * 128:(i + 1) * 128, :], in_=otile[:, oslot * DM:(oslot + 1) * DM]),
                inc=("dma", 16))
            evac_cnt += 1
        if i >= 1:
            pass
        if i < NT - 1:
            P.wait("sp", "evac")

    P.wait("sp", "dma")


def build_A():
    nc = bass.Bass("TRN2", target_bir_lowering=False)
    es = contextlib.ExitStack()
    with es:
        P = Pr(nc, es)
        io = {}
        x = nc.dram_tensor("x", [T, DM], F32, kind="ExternalInput")
        wq = nc.dram_tensor("wq", [DM, DM], F32, kind="ExternalInput")
        wk = nc.dram_tensor("wk", [DM, DM], F32, kind="ExternalInput")
        wv = nc.dram_tensor("wv", [DM, DM], F32, kind="ExternalInput")
        ra = nc.dram_tensor("ra", [T, DM], F32, kind="ExternalInput")
        rb = nc.dram_tensor("rb", [T, DM], F32, kind="ExternalInput")
        qo = nc.dram_tensor("qo", [T, DM], BF16, kind="ExternalOutput")
        ko = nc.dram_tensor("ko", [T, DM], BF16, kind="ExternalOutput")
        vo = nc.dram_tensor("vo", [T, DM], BF16, kind="ExternalOutput")

        io["x"] = x
        io["wq"] = wq
        io["wk"] = wk
        io["wv"] = wv
        io["ra"] = ra
        io["rb"] = rb
        io["qo"] = qo
        io["ko"] = ko
        io["vo"] = vo
        emit_A(P, io)
        P.finalize()
    return nc


def emit_rope_evac(P, psu, rat, rbt, rslot, rsc, otile, oslot, inc_last=None):
    for half in range(2):
        ps = psu[half]
        acol = rslot * DM + half * 512
        out0 = oslot * DM + half * 512
        psv = ps.ap().rearrange("p (h two c) -> p h two c", two=2, c=32)
        rsv = rsc[:, 0:512].rearrange("p (h two c) -> p h two c", two=2, c=32)
        tmp = rsc[:, 512:1024].rearrange("p (h two c) -> p h two c", two=2, c=32)
        rbv = rbt[:, acol:acol + 512].rearrange("p (h two c) -> p h two c", two=2, c=32)
        rav = rat[:, acol:acol + 512].rearrange("p (h two c) -> p h two c", two=2, c=32)
        ov = otile[:, out0:out0 + 512].rearrange("p (h two c) -> p h two c", two=2, c=32)

        P.on("dve", lambda e, psv=psv, rav=rav, rsv=rsv: e.tensor_tensor(
            rsv[:, :, :, :], psv[:, :, :, :], rav[:, :, :, :], ALU.mult))
        P.on("dve", lambda e, psv=psv, rbv=rbv, tmp=tmp: e.tensor_tensor(
            tmp[:, :, 0:1, :], psv[:, :, 1:2, :], rbv[:, :, 0:1, :], ALU.mult))
        P.on("dve", lambda e, psv=psv, rbv=rbv, tmp=tmp: e.tensor_tensor(
            tmp[:, :, 1:2, :], psv[:, :, 0:1, :], rbv[:, :, 1:2, :], ALU.mult))
        P.on("dve", lambda e, rsv=rsv, tmp=tmp, ov=ov: e.tensor_tensor(
            ov[:, :, :, :], rsv[:, :, :, :], tmp[:, :, :, :], ALU.add),
            inc=inc_last if half == 1 else None)




def emit_ln2x(P, i, x_res, xn16t, scratch, st, gb_b, bb_b, inc_last):
    xs = x_res[:, i * DM:(i + 1) * DM]
    P.on("dve", lambda e: e.reduce_sum(st[:, 0:1], xs, axis=AX.X))
    P.on("dve", lambda e: e.tensor_tensor(scratch[:, :], xs, xs, ALU.mult))
    P.on("dve", lambda e: e.reduce_sum(st[:, 1:2], scratch[:, :], axis=AX.X), inc="l2a")
    P.wait("dve", "l2a")

    def stats1(e):
        e.tensor_scalar(st[:, 2:3], st[:, 0:1], 1.0 / DM, None, ALU.mult)
        return e.tensor_scalar(st[:, 1:2], st[:, 1:2], 1.0 / DM, None, ALU.mult)
    P.on("dve", stats1, inc="l2a")
    P.wait("dve", "l2a")
    P.on("dve", lambda e: e.tensor_tensor(st[:, 3:4], st[:, 2:3], st[:, 2:3], ALU.mult), inc="l2a")
    P.wait("dve", "l2a")
    P.on("dve", lambda e: e.tensor_tensor(st[:, 3:4], st[:, 1:2], st[:, 3:4], ALU.subtract), inc="l2a")
    P.wait("dve", "l2a")
    P.on("dve", lambda e: e.tensor_scalar(st[:, 4:5], st[:, 3:4], EPS, None, ALU.add), inc="l2a")
    P.wait("act", "l2a")
    P.on("act", lambda e: e.activation(st[:, 6:7], st[:, 4:5], AF.Ln), inc="l2b")
    P.wait("act", "l2b")
    P.on("act", lambda e: e.activation(st[:, 5:6], st[:, 6:7], AF.Exp, scale=-0.5))
    P.on("act", lambda e: e.activation(st[:, 6:7], st[:, 6:7], AF.Exp, scale=-1.0), inc="l2b")
    P.wait("dve", "l2b")
    P.on("dve", lambda e: e.tensor_tensor(st[:, 6:7], st[:, 3:4], st[:, 6:7], ALU.mult), inc="l2c")
    P.wait("dve", "l2c")
    P.on("dve", lambda e: e.tensor_scalar(st[:, 6:7], st[:, 6:7], EPS, None, ALU.add), inc="l2c")
    P.wait("act", "l2c")
    act_pow(P, st[:, 7:8], st[:, 6:7], -0.5, "l2cc")
    P.wait("dve", "l2cc")
    P.on("dve", lambda e: e.tensor_scalar(st[:, 7:8], st[:, 5:6], st[:, 7:8], None, ALU.mult),
         inc="l2d")
    P.wait("dve", "l2d")
    P.on("dve", lambda e: e.tensor_scalar(scratch[:, :], xs, st[:, 2:3], st[:, 5:6],
                                          ALU.subtract, ALU.mult))
    P.on("dve", lambda e: e.tensor_scalar(xn16t[:, :], xs, st[:, 2:3], st[:, 7:8],
                                          ALU.subtract, ALU.mult))
    P.on("dve", lambda e: e.tensor_tensor(xs, xs, scratch[:, :], ALU.add))
    P.on("dve", lambda e: e.tensor_tensor(xn16t[:, :], xn16t[:, :], gb_b[:, :], ALU.mult))
    P.on("dve", lambda e: e.tensor_tensor(xn16t[:, :], xn16t[:, :], bb_b[:, :], ALU.add),
         inc=inc_last)



def emit_BC(P, io):
    nc = P.nc
    x = io["x"]
    qin = io["qin"]
    kseq = io["kseq"]
    vseq = io["vseq"]
    wout = io["wout"]
    flagc = io["flagc"]
    gam = io["gam"]
    bet = io["bet"]
    rw = io["rw"]
    w1 = io["w1"]
    w2 = io["w2"]
    x2o = io["x2o"]
    x1o = io["x1o"]
    attno = io["attno"]
    kvo = io["kvo"]
    csbo = io["csbo"]
    xno = io["xno"]
    x_res = P.sbuf("x_res", [128, NT * DM], F32)
    fmbuf = P.sbuf("fmbuf", [128, NC * T], BF16)
    lobuf = P.sbuf("lobuf", [128, NC * T], BF16)
    ubuf = P.sbuf("ubuf", [128, 32768], BF16)
    kch = ubuf[:, 0:4096]
    pch = ubuf[:, 4096:8192]
    pstrip = ubuf[:, 8192:12288]
    vch = ubuf[:, 12288:16384]
    OKG, OGL, OVG, OPG, OPGT = 16384, 19656, 22928, 27024, 27440
    kg_fm = ubuf[:, OKG:OKG + NC * NG]
    gl_fm = ubuf[:, OGL:OGL + NC * NG]
    vg_tm = ubuf[:, OVG:OVG + 4 * DM]
    pg = ubuf[:, OPG:OPG + NG]
    pgt = ubuf[:, OPGT:OPGT + 4 * NG]
    w1b = ubuf[:, 0:16384]
    w2b = ubuf[:, 16384:32768]
    hbuf = lobuf[:, 0:16 * 512]

    kv_sb = ubuf[:, 29184:29696]
    sqt32v = ubuf[:, 29696:30208]
    woutb = ubuf[:, 0:NC * DM]
    st = P.sbuf("st", [128, 8], F32)
    st2 = P.sbuf("st2", [128, 16], F32)
    id_bf = P.sbuf("id_bf", [128, 128], BF16)
    flag_sb = P.sbuf("flag_sb", [128, 4], F32)
    gb_b = P.sbuf("gb_b", [128, DM], BF16)
    bb_b = P.sbuf("bb_b", [128, DM], BF16)
    grow = P.sbuf("grow", [1, DM], F32)
    rw_sb = P.sbuf("rw_sb", [8, DM], BF16)
    rwT = P.sbuf("rwT", [128, NC * NE], BF16)
    csb = P.sbuf("csb", [128, NT * NE], F32)
    gsc = P.sbuf("gsc", [128, 4 * NE], F32)
    ytmp = P.sbuf("ytmp", [128, DM], F32)
    P.bufs["ln_sq"] = ytmp
    xn16t = lobuf[:, 15 * DM:16 * DM]
    ones1_32 = P.sbuf("ones1_32", [1, 128], F32)

    psA = P.psum("psA", [128, 512], F32)
    psB = P.psum("psB", [128, 512], F32)
    psC = P.psum("psC", [128, 512], F32)
    psD = P.psum("psD", [128, 512], F32)
    pt0 = P.psum("pt0", [128, 1024], BF16)
    pt1 = P.psum("pt1", [128, 1024], BF16)
    psE = P.psum("psE", [128, 512], F32)
    psF = P.psum("psF", [128, 512], F32)

    emit_identity(P, id_bf, None, scratch=lobuf)

    P.on("sp", lambda e: e.dma_start(
        out=x_res[:, :].rearrange("p (t j) -> p t j", t=NT),
        in_=x[0:T, :].rearrange("(t p) j -> p t j", p=128)), inc=("dma", 16))
    for c in range(NC):
        P.on("sp", lambda e, c=c: e.dma_start(
            out=fmbuf[:, c * T:(c + 1) * T], in_=qin[0:T, c * 128:(c + 1) * 128],
            transpose=True), inc=("dma", 16))
    P.on("sp", lambda e: e.dma_start(out=flag_sb[:, 0:1], in_=flagc[:, :]), inc=("dma", 16))
    P.on("sp", lambda e: e.dma_start(out=grow[0:1, 0:DM], in_=gam[:, :]), inc=("dma", 16))
    P.on("pool", lambda e: e.dma_start(out=rw_sb[:, :], in_=rw[:, :]), inc=("dmap", 16))
    P.wait("dve", "dma")
    P.on("dve", lambda e: e.tensor_scalar(
        flag_sb[:, 1:2], flag_sb[:, 0:1], -1.0, 1.0, ALU.mult, ALU.add), inc="dve")
    P.wait("dve", "dve")
    P.on("dve", lambda e: e.tensor_scalar_mul(flag_sb[:, 2:3], flag_sb[:, 0:1], -10000.0))
    P.on("dve", lambda e: e.tensor_scalar_mul(flag_sb[:, 3:4], flag_sb[:, 1:2], -10000.0),
         inc="dve")
    P.on("dve", lambda e: e.memset(ones1_32[:, :], 1.0), inc="dve")
    P.wait("pe", "dve")

    emit_bcast_row(P, gb_b, grow[0:1, 0:DM], ones1_32, psA, DM, "gb_dve", "gb_pe")
    P.wait("sp", "gb_dve")
    P.on("sp", lambda e: e.dma_start(out=grow[0:1, 0:DM], in_=bet[:, :]), inc=("dma", 16))
    P.wait("pe", "dma")
    emit_bcast_row(P, bb_b, grow[0:1, 0:DM], ones1_32, psA, DM, "gb_dve", "gb_pe")

    P.wait("pe", "dmap")
    for c in range(NC):
        P.on("pe", lambda e, c=c: e.transpose(
            pt0[0:128, 0:8],
            rw_sb[0:8, c * 128:(c + 1) * 128], id_bf[0:8, 0:8]), inc="rt_pe")
        P.wait("dve", "rt_pe")
        P.on("dve", lambda e, c=c: e.tensor_copy(
            rwT[:, c * NE:(c + 1) * NE], pt0[0:128, 0:8]), inc="rt_dve")
        P.wait_at("pe", "rt_dve", P.val("rt_dve"))

    for c in range(NC):
        P.on("sp", lambda e, c=c: e.dma_start(
            out=kch, in_=kseq[:, c * 128:(c + 1) * 128], transpose=True),
            inc=("dma", 16))
        P.on("sp", lambda e, c=c: e.dma_start(
            out=vch.rearrange("p (t j) -> p t j", t=32),
            in_=vseq.ap().rearrange("(t p) j -> p t j", p=128)[:, :, c * 128:(c + 1) * 128]),
            inc=("dma", 16))
        P.wait("dve", "dma")
        P.wait("dve", "lexp")
        P.on("dve", lambda e, c=c: e.tensor_scalar(
            kg_fm[:, c * NG:(c + 1) * NG], kch[:, 0:NG], flag_sb[:, 0:1], None, ALU.mult))
        P.on("dve", lambda e, c=c: e.tensor_scalar(
            sqt32v[:, 0:NG], kch[:, T:T + NG], flag_sb[:, 1:2], None, ALU.mult), inc="kgx")
        P.wait("dve", "kgx")
        P.on("dve", lambda e, c=c: e.tensor_tensor(
            kg_fm[:, c * NG:(c + 1) * NG], kg_fm[:, c * NG:(c + 1) * NG], sqt32v[:, 0:NG],
            ALU.add), inc="kgc")
        P.on("dve", lambda e: e.tensor_scalar(
            kch[:, 0:NG], kch[:, 0:NG], flag_sb[:, 2:3], None, ALU.add))
        P.on("dve", lambda e: e.tensor_scalar(
            kch[:, T:T + NG], kch[:, T:T + NG], flag_sb[:, 3:4], None, ALU.add))
        P.on("dve", lambda e: e.reduce_max(st2[:, 8:9], kch[:, :], axis=AX.X), inc="lst")
        P.wait("dve", "lst")
        P.on("dve", lambda e: e.tensor_scalar_mul(st2[:, 9:10], st2[:, 8:9], -1.0), inc="lst")
        P.wait("act", "lst")
        P.wait("act", "ptp")
        P.on("act", lambda e: e.activation(
            pch, kch[:, :], AF.Exp, bias=st2[:, 9:10], scale=1.0,
            accum_out=st2[:, 10:11]), inc="lexp")
        P.wait("act", "lexp")
        act_pow(P, st2[:, 11:12], st2[:, 10:11], -1.0, "lrc")
        P.wait("dve", "lrc")
        P.on("dve", lambda e: e.tensor_scalar_mul(pch, pch, st2[:, 11:12]), inc="lnorm")
        for kt in range(4):
            rows = 128 if kt < 3 else NG - 384
            P.on("dve", lambda e, c=c, kt=kt, rows=rows: e.tensor_scalar(
                vg_tm[0:rows, kt * DM + c * 128: kt * DM + (c + 1) * 128],
                vch[0:rows, kt * 128:(kt + 1) * 128], flag_sb[0:rows, 0:1], None, ALU.mult))
            P.on("dve", lambda e, c=c, kt=kt, rows=rows: e.tensor_scalar(
                sqt32v[0:rows, 0:128], vch[0:rows, (16 + kt) * 128:(17 + kt) * 128],
                flag_sb[0:rows, 1:2], None, ALU.mult), inc="vgx")
            P.wait("dve", "vgx")
            P.on("dve", lambda e, c=c, kt=kt, rows=rows: e.tensor_tensor(
                vg_tm[0:rows, kt * DM + c * 128: kt * DM + (c + 1) * 128],
                vg_tm[0:rows, kt * DM + c * 128: kt * DM + (c + 1) * 128],
                sqt32v[0:rows, 0:128], ALU.add), inc="vgc")
        P.wait("pe", "lnorm")
        transpose_tm_to_fm(P, lambda i, cc: pch[:, i * 128:(i + 1) * 128],
                           pstrip, lambda i, cc: i * 128, 32, 1, pt0, pt1,
                           pe_sem="ptp", dve_sem="ptd")
        for hh in range(2):
            base = 64 * hh
            kvp = psE if hh == 0 else psF
            for t_ in range(32):
                P.on("pe", lambda e, kvp=kvp, hh=hh, t_=t_, base=base: e.matmul(
                    kvp[base:base + 64, 0:64],
                    vch[:, t_ * 128 + hh * 64: t_ * 128 + hh * 64 + 64],
                    pstrip[:, t_ * 128 + hh * 64: t_ * 128 + hh * 64 + 64],
                    start=(t_ == 0), stop=(t_ == 31), tile_position=(0, base)),
                    inc="kvpe" if t_ == 31 else None)
        P.wait("dve", "kvpe")
        for hh in range(2):
            h = 2 * c + hh
            kvp = psE if hh == 0 else psF
            base = 64 * hh
            P.on("dve", lambda e, kvp=kvp, h=h, base=base: e.tensor_copy(
                kv_sb[base:base + 64, (h // 2) * 64:(h // 2) * 64 + 64],
                kvp[base:base + 64, 0:64]), inc="kvd")
        P.wait("pe", "kvd")
        P.wait("sp", "kvd")

    P.wait("pe", "dma")
    nn = 0
    for h in range(NH):
        base = 64 * (h % 2)
        for qc in range(4):
            pp = (psA, psB, psC, psD)[nn % 4]
            if nn >= 4:
                P.wait_at("pe", "lod", P.val("lod") - 3)
            P.on("pe", lambda e, pp=pp, h=h, qc=qc, base=base: e.matmul(
                pp[base:base + 64, 0:512],
                kv_sb[base:base + 64, (h // 2) * 64:(h // 2) * 64 + 64],
                fmbuf[base:base + 64, (h // 2) * T + qc * 512:(h // 2) * T + (qc + 1) * 512],
                start=True, stop=True, tile_position=(base, base)), inc="lope")
            P.wait("dve", "lope")
            P.on("dve", lambda e, pp=pp, h=h, qc=qc, base=base: e.tensor_copy(
                lobuf[base:base + 64, (h // 2) * T + qc * 512:(h // 2) * T + (qc + 1) * 512],
                pp[base:base + 64, 0:512]), inc="lod")
            nn += 1
    P.wait("pe", "lod")

    P.wait("pe", "vgc")
    P.wait("pe", "kgc")
    QT = [(0, 128), (128, 128), (256, 128), (384, NG - 384)]
    for h in range(NH):
        base = 64 * (h % 2)
        for qt, (q0, qn) in enumerate(QT):
            P.wait("pe", "gexp")
            P.on("pe", lambda e, h=h, q0=q0, qn=qn, base=base: e.matmul(
                psE[0:qn, 0:NG],
                fmbuf[base:base + 64, (h // 2) * T + q0:(h // 2) * T + q0 + qn],
                kg_fm[base:base + 64, (h // 2) * NG:(h // 2 + 1) * NG],
                start=True, stop=True, tile_position=(base, 0)), inc="gsc_pe")
            P.wait("dve", "gsc_pe")
            P.on("dve", lambda e, qn=qn: e.reduce_max(st2[0:qn, 12:13], psE[0:qn, 0:NG], axis=AX.X),
                 inc="gst")
            P.wait("dve", "gst")
            P.on("dve", lambda e, qn=qn: e.tensor_scalar_mul(
                st2[0:qn, 13:14], st2[0:qn, 12:13], -SCALE_G), inc="gst")
            P.wait("act", "gst")
            P.wait("act", "gtp")
            P.on("act", lambda e, qn=qn: e.activation(
                pg[0:qn, :], psE[0:qn, 0:NG], AF.Exp, bias=st2[0:qn, 13:14],
                scale=SCALE_G, accum_out=st2[0:qn, 14:15]), inc="gexp")
            P.wait("act", "gexp")
            act_pow(P, st2[:, 15:16], st2[:, 14:15], -1.0, "grc")
            P.wait("dve", "grc")
            P.on("dve", lambda e, qn=qn: e.tensor_scalar_mul(
                pg[0:qn, :], pg[0:qn, :], st2[0:qn, 15:16]), inc="gnorm")
            P.wait("pe", "gnorm")
            for kt, (k0, kn) in enumerate(QT):
                ppt = pt0 if kt % 2 == 0 else pt1
                if kt >= 2:
                    P.wait_at("pe", "gtd", P.val("gtd") - 1)
                P.on("pe", lambda e, ppt=ppt, qn=qn, k0=k0, kn=kn: e.transpose(
                    ppt[0:kn, 0:qn], pg[0:qn, k0:k0 + kn], id_bf[0:qn, 0:qn]), inc="gtp")
                P.wait("dve", "gtp")
                P.on("dve", lambda e, ppt=ppt, kt=kt, q0=q0, qn=qn, kn=kn: e.tensor_copy(
                    pgt[0:kn, kt * NG + q0:kt * NG + q0 + qn], ppt[0:kn, 0:qn]), inc="gtd")
            P.wait("pe", "gtd")
        for kt, (k0, kn) in enumerate(QT):
            P.on("pe", lambda e, h=h, kt=kt, k0=k0, kn=kn, base=base: e.matmul(
                psF[base:base + 64, 0:NG],
                vg_tm[0:kn, kt * DM + (h // 2) * 128 + (h % 2) * 64:
                      kt * DM + (h // 2) * 128 + (h % 2) * 64 + 64],
                pgt[0:kn, kt * NG:(kt + 1) * NG],
                start=(kt == 0), stop=(kt == 3), tile_position=(0, base)),
                inc="gav" if kt == 3 else None)
        P.wait("dve", "gav")
        P.on("dve", lambda e, h=h, base=base: e.tensor_copy(
            gl_fm[base:base + 64, (h // 2) * NG:(h // 2 + 1) * NG],
            psF[base:base + 64, 0:NG]), inc="gld")
        P.wait("pe", "gld")

    for c in range(NC):
        P.on("dve", lambda e, c=c: e.tensor_scalar(
            gl_fm[:, c * NG:(c + 1) * NG], gl_fm[:, c * NG:(c + 1) * NG],
            flag_sb[:, 0:1], None, ALU.mult))
        P.on("dve", lambda e, c=c: e.tensor_scalar(
            lobuf[:, c * T:c * T + NG], lobuf[:, c * T:c * T + NG],
            flag_sb[:, 1:2], None, ALU.mult))
        P.on("dve", lambda e, c=c: e.tensor_tensor(
            lobuf[:, c * T:c * T + NG], lobuf[:, c * T:c * T + NG],
            gl_fm[:, c * NG:(c + 1) * NG], ALU.add), inc="blend")

    P.wait("pool", "kvd")
    P.wait("pool", "ptd")
    P.on("pool", lambda e: e.dma_start(
        out=woutb.rearrange("p (c j) -> p c j", c=NC),
        in_=wout.ap().rearrange("(c p) j -> p c j", p=128)), inc=("dmap", 16))
    P.wait("pool", "dmap")
    P.on("pool", lambda e: e.engine_nop(), inc="pool")
    P.wait("pe", "pool")
    P.wait("pe", "blend")
    nn = 0
    for i in range(NT):
        for half in range(2):
            pp = (psA, psB, psC, psD)[nn % 4]
            if nn >= 4:
                P.wait_at("pe", "wod", P.val("wod") - 3)
            for k in range(NC):
                P.on("pe", lambda e, pp=pp, i=i, k=k, half=half: e.matmul(
                    pp[:, :], lobuf[:, k * T + i * 128:k * T + (i + 1) * 128],
                    woutb[:, k * DM + half * 512:k * DM + half * 512 + 512],
                    start=(k == 0), stop=(k == NC - 1)),
                    inc="wop" if k == NC - 1 else None)
            P.wait("dve", "wop")
            P.on("dve", lambda e, pp=pp, i=i, half=half: e.tensor_tensor(
                x_res[:, i * DM + half * 512:i * DM + half * 512 + 512],
                x_res[:, i * DM + half * 512:i * DM + half * 512 + 512],
                pp[:, :], ALU.add), inc="wod")
            nn += 1

    if DEBUG:
        P.wait("sp", "wod")
        P.on("sp", lambda e: e.dma_start(
            out=x1o.ap().rearrange("(t p) j -> p t j", p=128),
            in_=x_res[:, :].rearrange("p (t j) -> p t j", t=NT)), inc=("dma", 16))
        P.on("sp", lambda e: e.dma_start(out=attno.ap(), in_=lobuf[:, :]), inc=("dma", 16))
        P.on("sp", lambda e: e.dma_start(out=kvo.ap(), in_=kv_sb), inc=("dma", 16))
        P.wait("dve", "dma")
        P.wait("act", "dma")

    emit_moe_l0(P, x_res, fmbuf, hbuf, csb, gsc, st, st2, gb_b, bb_b,
                rwT, w1, w2, w1b, w2b, ytmp, xn16t,
                (psA, psB, psC, psD, psE, psF), (pt0, pt1))

    if DEBUG:
        P.wait("sp", "m_gate")
        P.on("sp", lambda e: e.dma_start(out=csbo.ap(), in_=csb[:, :]), inc=("dma", 16))
        P.on("sp", lambda e: e.dma_start(out=xno.ap(), in_=fmbuf[:, :]), inc=("dma", 16))
    P.wait("sp", "moe_done")
    P.on("sp", lambda e: e.dma_start(
        out=x2o.ap().rearrange("(t p) j -> p t j", p=128),
        in_=x_res[:, :].rearrange("p (t j) -> p t j", t=NT)), inc=("dma", 16))
    P.wait("sp", "dma")


def build_BC():
    nc = bass.Bass("TRN2", target_bir_lowering=False)
    es = contextlib.ExitStack()
    with es:
        P = Pr(nc, es)
        io = {}
        x = nc.dram_tensor("x", [T, DM], F32, kind="ExternalInput")
        qin = nc.dram_tensor("qin", [T, DM], BF16, kind="ExternalInput")
        maskrow = nc.dram_tensor("maskrow", [1, SEQ], F32, kind="ExternalInput")
        kseq = nc.dram_tensor("kseq", [SEQ, DM], BF16, kind="ExternalInput")
        vseq = nc.dram_tensor("vseq", [SEQ, DM], BF16, kind="ExternalInput")
        wout = nc.dram_tensor("wout", [DM, DM], F32, kind="ExternalInput")
        flagc = nc.dram_tensor("flagc", [128, 1], F32, kind="ExternalInput")
        gam = nc.dram_tensor("gam", [1, DM], F32, kind="ExternalInput")
        bet = nc.dram_tensor("bet", [1, DM], F32, kind="ExternalInput")
        rw = nc.dram_tensor("rw", [NE, DM], F32, kind="ExternalInput")
        w1 = nc.dram_tensor("w1", [NE, DM, HI], F32, kind="ExternalInput")
        w2 = nc.dram_tensor("w2", [NE, HI, DM], F32, kind="ExternalInput")
        x2o = nc.dram_tensor("x2o", [T, DM], F32, kind="ExternalOutput")
        if DEBUG:
            x1o = nc.dram_tensor("x1o", [T, DM], F32, kind="ExternalOutput")
            attno = nc.dram_tensor("attno", [128, NC * T], BF16, kind="ExternalOutput")
            kvo = nc.dram_tensor("kvo", [128, 512], BF16, kind="ExternalOutput")
            csbo = nc.dram_tensor("csbo", [128, NT * NE], F32, kind="ExternalOutput")
            xno = nc.dram_tensor("xno", [128, NC * T], BF16, kind="ExternalOutput")

        io["x"] = x
        io["qin"] = qin
        io["maskrow"] = maskrow
        io["kseq"] = kseq
        io["vseq"] = vseq
        io["wout"] = wout
        io["flagc"] = flagc
        io["gam"] = gam
        io["bet"] = bet
        io["rw"] = rw
        io["w1"] = w1
        io["w2"] = w2
        io["x2o"] = x2o
        io["x1o"] = x1o if DEBUG else None
        io["attno"] = attno if DEBUG else None
        io["kvo"] = kvo if DEBUG else None
        io["csbo"] = csbo if DEBUG else None
        io["xno"] = xno if DEBUG else None
        emit_BC(P, io)
        P.finalize()
    return nc


def emit_identity(P, id_bf, id_32, scratch=None):
    if scratch is None:
        iot1 = P.sbuf("iot1", [128, 128], BF16)
        iot2 = P.sbuf("iot2", [128, 1], F32)
    else:
        iot1 = scratch[:, 0:128]
        iot2 = P.sbuf("iot2", [128, 1], F32)
    P.on("pool", lambda e: e.iota(iot1[:, :], [[1, 128]], base=0,
                                  channel_multiplier=0, allow_small_or_imprecise_dtypes=True))
    P.on("pool", lambda e: e.iota(iot2[:, :], [[1, 1]], base=0,
                                  channel_multiplier=1, allow_small_or_imprecise_dtypes=True),
         inc="pool")
    P.wait("dve", "pool")
    P.on("dve", lambda e: e.tensor_scalar(id_bf[:, :], iot1[:, :], iot2[:, 0:1],
                                          None, ALU.is_equal), inc="dve")
    if id_32 is not None:
        P.on("dve", lambda e: e.tensor_scalar(id_32[:, :], iot1[:, :], iot2[:, 0:1],
                                              None, ALU.is_equal), inc="dve")


def emit_moe_l0(P, x_res, xn_fm, hbuf, csb, gsc, st, st2, gb_b, bb_b,
                rwT, w1, w2, w1b, w2b, ytmp, xn16t, ps6, pts):
    psA, psB, psC, psD, psE, psF = ps6
    pt0, pt1 = pts

    for i in range(NT):
        P.wait("dve", "m_tp")
        emit_ln2x(P, i, x_res, xn16t, ytmp, st, gb_b, bb_b, "m_lnt")
        P.wait("pe", "m_lnt")
        transpose_tm_to_fm(P, lambda ii, c: xn16t[:, c * 128:(c + 1) * 128],
                           xn_fm, lambda ii, c, i=i: c * T + i * 128, 1, NC, pt0, pt1,
                           pe_sem="m_tp2", dve_sem="m_tp")

    for i in range(NT):
        P.wait("pe", "m_gexp")
        P.on("pe", lambda e, i=i: [
            e.matmul(psE[:, 0:NE], xn_fm[:, k * T + i * 128:k * T + (i + 1) * 128],
                     rwT[:, k * NE:(k + 1) * NE], start=(k == 0), stop=(k == NC - 1))
            for k in range(NC)][-1], inc="m_gpe")
        P.wait("dve", "m_gpe")
        P.on("dve", lambda e: e.reduce_max(st2[:, 0:1], psE[:, 0:NE], axis=AX.X), inc="m_gst")
        P.wait("dve", "m_gst")
        P.on("dve", lambda e: e.tensor_scalar_mul(st2[:, 1:2], st2[:, 0:1], -1.0 / 1.5), inc="m_gst")
        P.wait("act", "m_gst")
        P.wait("act", "m_gate")
        P.on("act", lambda e: e.activation(
            gsc[:, 0:NE], psE[:, 0:NE], AF.Exp, bias=st2[:, 1:2], scale=1.0 / 1.5,
            accum_out=st2[:, 2:3]), inc="m_gexp")
        P.wait("act", "m_gexp")
        act_pow(P, st2[:, 3:4], st2[:, 2:3], -1.0, "m_grc")
        P.wait("dve", "m_grc")
        P.on("dve", lambda e: e.tensor_scalar_mul(gsc[:, 0:NE], gsc[:, 0:NE], st2[:, 3:4]), inc="m_g2")
        P.wait("dve", "m_g2")
        P.on("dve", lambda e: e.reduce_max(st2[:, 4:5], gsc[:, 0:NE], axis=AX.X), inc="m_g3")
        P.wait("dve", "m_g3")
        P.on("dve", lambda e: e.tensor_scalar(
            gsc[:, NE:2 * NE], gsc[:, 0:NE], st2[:, 4:5], None, ALU.is_equal), inc="m_g3")
        P.wait("dve", "m_g3")
        P.on("dve", lambda e: e.tensor_tensor(
            gsc[:, 2 * NE:3 * NE], gsc[:, 0:NE], gsc[:, NE:2 * NE], ALU.mult), inc="m_g3")
        P.wait("dve", "m_g3")
        P.on("dve", lambda e: e.tensor_tensor(
            gsc[:, 2 * NE:3 * NE], gsc[:, 0:NE], gsc[:, 2 * NE:3 * NE], ALU.subtract), inc="m_g3")
        P.wait("dve", "m_g3")
        P.on("dve", lambda e: e.reduce_max(st2[:, 5:6], gsc[:, 2 * NE:3 * NE], axis=AX.X), inc="m_g4")
        P.wait("dve", "m_g4")
        P.on("dve", lambda e: e.tensor_scalar(
            gsc[:, 3 * NE:4 * NE], gsc[:, 2 * NE:3 * NE], st2[:, 5:6], None, ALU.is_equal), inc="m_g4")
        P.wait("dve", "m_g4")
        P.on("dve", lambda e: e.tensor_tensor(
            gsc[:, NE:2 * NE], gsc[:, NE:2 * NE], gsc[:, 3 * NE:4 * NE], ALU.add), inc="m_g4")
        P.wait("dve", "m_g4")
        P.on("dve", lambda e, i=i: e.tensor_tensor(
            csb[:, i * NE:(i + 1) * NE], gsc[:, 0:NE], gsc[:, NE:2 * NE], ALU.mult),
            inc="m_gate")
    P.wait("pe", "m_gate")

    hcnt = 0
    P.wait("pool", "wop")
    for e_ in range(NE):
        for half in range(2):
            P.wait("pool", "m_hpe")
            P.on("pool", lambda e, e_=e_, half=half: e.dma_start(
                out=w1b.rearrange("p (c j) -> p c j", c=NC),
                in_=w1.ap()[e_].rearrange("(c p) j -> p c j", p=128)
                [:, :, half * 2048:(half + 1) * 2048]), inc=("dmap", 16))
            P.wait("pool", "m_ype")
            P.on("pool", lambda e, e_=e_, half=half: e.dma_start(
                out=w2b.rearrange("p (c j) -> p c j", c=16),
                in_=w2.ap()[e_].rearrange("(c p) j -> p c j", p=128)
                [:, half * 16:(half + 1) * 16, :]), inc=("dmap", 16))
            P.wait("pe", "dmap")
            for tc in range(4):
                for hc in range(16):
                    pp = psA if hcnt % 2 == 0 else psB
                    if hcnt >= 2:
                        P.wait_at("pe", "m_hact", P.val("m_hact") - 1)
                    for k in range(NC):
                        P.on("pe", lambda e, pp=pp, tc=tc, hc=hc, k=k: e.matmul(
                            pp[:, :], w1b[:, k * 2048 + hc * 128:k * 2048 + (hc + 1) * 128],
                            xn_fm[:, k * T + tc * 512:k * T + tc * 512 + 512],
                            start=(k == 0), stop=(k == NC - 1)),
                            inc="m_hpe" if k == NC - 1 else None)
                    P.wait("act", "m_hpe")
                    P.on("act", lambda e, pp=pp, hc=hc: e.activation(
                        hbuf[:, hc * 512:(hc + 1) * 512], pp[:, :], AF.Silu), inc="m_hact")
                    hcnt += 1
                P.wait("pe", "m_hact")
                for ts in range(4):
                    pp0, pp1 = (psC, psD) if ts % 2 == 0 else (psE, psF)
                    P.wait_at("pe", "m_yact", max(0, P.val("m_yact") - 1))
                    for k in range(16):
                        P.on("pe", lambda e, pp0=pp0, ts=ts, k=k: e.matmul(
                            pp0[:, :], hbuf[:, k * 512 + ts * 128:k * 512 + ts * 128 + 128],
                            w2b[:, k * DM:k * DM + 512],
                            start=(k == 0), stop=(k == 15)))
                        P.on("pe", lambda e, pp1=pp1, ts=ts, k=k: e.matmul(
                            pp1[:, :], hbuf[:, k * 512 + ts * 128:k * 512 + ts * 128 + 128],
                            w2b[:, k * DM + 512:(k + 1) * DM],
                            start=(k == 0), stop=(k == 15)),
                            inc="m_ype" if k == 15 else None)
                    P.wait("act", "m_ype")
                    t_g = tc * 4 + ts
                    P.wait("act", "m_yd")
                    P.on("act", lambda e, pp0=pp0, t_g=t_g, e_=e_: e.activation(
                        ytmp[:, 0:512], pp0[:, :], AF.Copy,
                        scale=csb[:, t_g * NE + e_:t_g * NE + e_ + 1]))
                    P.on("act", lambda e, pp1=pp1, t_g=t_g, e_=e_: e.activation(
                        ytmp[:, 512:DM], pp1[:, :], AF.Copy,
                        scale=csb[:, t_g * NE + e_:t_g * NE + e_ + 1]), inc="m_yact")
                    P.wait("dve", "m_yact")
                    P.on("dve", lambda e, t_g=t_g: e.tensor_tensor(
                        x_res[:, t_g * DM:(t_g + 1) * DM], x_res[:, t_g * DM:(t_g + 1) * DM],
                        ytmp[:, :], ALU.add), inc="m_yd")
    P.wait("dve", "m_yd")
    P.on("dve", lambda e: e.engine_nop(), inc="moe_done")



def emit_D(P, io, fused=False):
    nc = P.nc
    x2 = io["x2"]
    ln1w = io["ln1w"]
    ln1b = io["ln1b"]
    ln2w = io["ln2w"]
    ln2b = io["ln2b"]
    wqk = io["wqk"]
    wv = io["wv"]
    wo = io["wo"]
    wqkc = io["wqkc"]
    taur = io["taur"]
    g1r = io["g1r"]
    xmido = io["xmido"]
    dotso = io["dotso"]
    q2co = io["q2co"]
    qk_fm = P.sbuf("qk_fm", [128, NC * SEQ], BF16)
    v65 = P.sbuf("v65", [128, 32 * NH * 65], BF16)
    wbuf = P.sbuf("wbuf", [128, 2 * NC * DM], BF16)
    wqkb = wbuf[:, 0:NC * DM]
    wvb = wbuf[:, NC * DM:2 * NC * DM]
    attn_fm = wbuf
    ptb = P.sbuf("ptb", [128, 2 * 1024], BF16)
    xtile = P.sbuf("xtile", [128, DM], F32)
    t1t = P.sbuf("t1t", [128, DM], BF16)
    t1f = P.sbuf("t1f", [128, DM], BF16)
    qktt = t1t
    sqt = P.sbuf("sqt", [128, DM], F32)
    st = P.sbuf("st", [128, 8], F32)
    q2tm = P.sbuf("q2tm", [128, 32 * NH], F32)
    biasall = P.sbuf("biasall", [128, 32 * NH], F32)
    taub = P.sbuf("taub", [128, NH], F32)
    taub8 = P.sbuf("taub8", [128, NH], F32)
    mb = P.sbuf("mb", [128, NH], F32)
    m16 = P.sbuf("m16", [16, 2], F32)
    mrow = P.sbuf("mrow", [1, NH], F32)
    lnwb = P.sbuf("lnwb", [128, 4 * DM], BF16)
    g1b = P.sbuf("g1b", [128, DM], BF16)
    rssb = P.sbuf("rssb", [1, 2048], BF16)
    rsbb = P.sbuf("rsbb", [128, 2048], BF16)
    ones1_32 = P.sbuf("ones1_32", [1, 128], F32)
    ones1b = P.sbuf("ones1b", [1, 128], BF16)
    onescol = P.sbuf("onescol", [128, 1], BF16)
    grow = P.sbuf("grow", [1, DM], BF16)
    id_bf = P.sbuf("id_bf", [128, 128], BF16)
    id_32 = P.sbuf("id_32", [128, 128], F32)
    dots_sb = rsbb.bitcast(F32)
    q2crow = ptb[0:1, 0:2048].bitcast(F32)
    tmpf = P.sbuf("tmpf", [128, DM], F32)
    P.bufs["ln_sq"] = sqt

    psA = P.psum("psA", [128, 512], F32)
    psB = P.psum("psB", [128, 512], F32)
    psC = P.psum("psC", [128, 512], F32)
    psD = P.psum("psD", [128, 512], F32)
    psE = P.psum("psE", [128, 512], F32)
    psF = P.psum("psF", [128, 512], F32)
    pt0 = P.psum("pt0", [128, 1024], BF16)
    pt1 = P.psum("pt1", [128, 1024], BF16)

    emit_identity(P, id_bf, id_32, scratch=qk_fm)

    P.on("pool", lambda e: e.dma_start(
        out=wqkb.rearrange("p (c j) -> p c j", c=NC),
        in_=wqk.ap().rearrange("(c p) j -> p c j", p=128)), inc=("dmap", 16))
    P.on("pool", lambda e: e.dma_start(
        out=wvb.rearrange("p (c j) -> p c j", c=NC),
        in_=wv.ap().rearrange("(c p) j -> p c j", p=128)), inc=("dmap", 16))

    P.on("sp", lambda e: e.dma_start(out=mrow[0:1, :], in_=taur[:, :]), inc=("dma", 16))
    P.on("dve", lambda e: e.memset(ones1_32[:, :], 1.0))
    P.on("dve", lambda e: e.memset(ones1b[:, :], 1.0))
    P.on("dve", lambda e: e.memset(onescol[:, :], 1.0), inc="dve")
    P.wait("pe", "dve")
    P.wait("pe", "dma")
    P.on("pe", lambda e: e.matmul(psA[:, 0:NH], ones1_32[0:1, :], mrow[0:1, :],
                                  start=True, stop=True), inc="tb_pe")
    P.wait("dve", "tb_pe")
    P.on("dve", lambda e: e.tensor_scalar_mul(taub[:, :], psA[:, 0:NH], 0.25))
    P.on("dve", lambda e: e.tensor_scalar_mul(taub8[:, :], psA[:, 0:NH], -0.125), inc="tb_dve")
    P.wait("pe", "tb_dve")
    for idx, src_t in enumerate([ln1w, ln1b, ln2w, ln2b, g1r]):
        P.wait("pool", "gb_dve")
        P.on("pool", lambda e, s=src_t: e.dma_start(out=grow[0:1, :], in_=s[:, :]),
             inc=("dmap", 16))
        P.wait("pe", "dmap")
        dst = lnwb[:, idx * DM:(idx + 1) * DM] if idx < 4 else g1b
        emit_bcast_row(P, dst, grow[0:1, :], ones1b, psA, DM, "gb_dve", "gb_pe")
    P.on("dve", lambda e: e.memset(
        v65.ap().rearrange("p (t h c) -> p t h c", t=32, h=NH)[:, :, :, 64:65], 1.0),
        inc="v1s")
    P.on("dve", lambda e: e.memset(m16[:, :], -1e30), inc="v1s")

    P.wait("pe", "dmap")
    for j in range(32):
        P.on("sp", lambda e, j=j: e.dma_start(
            out=xtile[:, :], in_=x2[j * 128:(j + 1) * 128, :]), inc=("dma", 16))
        P.wait("dve", "dma")
        emit_ln(P, 0, 0, xtile, None, t1t, st, "ln", n_tiles=1, sem_suffix="d1")
        P.on("dve", lambda e: e.tensor_tensor(t1t[:, :], t1t[:, :], lnwb[:, 0:DM], ALU.mult))
        P.on("dve", lambda e: e.tensor_tensor(t1t[:, :], t1t[:, :], lnwb[:, DM:2 * DM], ALU.add),
             inc="p1t1")
        P.wait("pe", "p1t1")
        transpose_tm_to_fm(P, lambda ii, c: t1t[:, c * 128:(c + 1) * 128],
                           t1f, lambda ii, c: c * 128, 1, NC, pt0, pt1,
                           pe_sem="p1tp", dve_sem="p1td")
        P.wait("pe", "p1ve")
        for half in range(2):
            pp = psC if half == 0 else psD
            for k in range(NC):
                P.on("pe", lambda e, pp=pp, k=k, half=half: e.matmul(
                    pp[:, :], t1f[:, k * 128:(k + 1) * 128],
                    wvb[:, k * DM + half * 512:k * DM + half * 512 + 512],
                    start=(k == 0), stop=(k == NC - 1)),
                    inc="p1v" if k == NC - 1 else None)
        P.wait("dve", "p1v")
        P.wait("dve", "v1s")
        for half in range(2):
            pp = psC if half == 0 else psD
            src = pp.ap().rearrange("p (h c) -> p h c", h=8)
            dst = v65[:, j * NH * 65 + half * 520:j * NH * 65 + (half + 1) * 520].rearrange(
                "p (h c) -> p h c", c=65)[:, :, 0:64]
            P.on("dve", lambda e, src=src, dst=dst: e.tensor_copy(dst, src),
                 inc="p1ve" if half == 1 else None)
        P.wait("pe", "p1sq")
        for half in range(2):
            pp = psA if half == 0 else psB
            for k in range(NC):
                P.on("pe", lambda e, pp=pp, k=k, half=half: e.matmul(
                    pp[:, :], t1f[:, k * 128:(k + 1) * 128],
                    wqkb[:, k * DM + half * 512:k * DM + half * 512 + 512],
                    start=(k == 0), stop=(k == NC - 1)),
                    inc="p1qk" if k == NC - 1 else None)
        P.wait("dve", "p1qk")
        P.on("dve", lambda e: e.tensor_copy(qktt[:, 0:512], psA[:, :]))
        P.on("dve", lambda e: e.tensor_copy(qktt[:, 512:1024], psB[:, :]), inc="p1qe")
        P.wait("dve", "p1qe")
        P.on("dve", lambda e: e.tensor_tensor(sqt[:, :], qktt[:, :], qktt[:, :], ALU.mult),
             inc="p1sq")
        P.wait("dve", "p1sq")
        for h in range(NH):
            P.on("dve", lambda e, j=j, h=h: e.reduce_sum(
                q2tm[:, j * NH + h:j * NH + h + 1], sqt[:, h * 64:(h + 1) * 64], axis=AX.X),
                inc="p1q2" if h == NH - 1 else None)
        transpose_tm_to_fm(P, lambda ii, c: qktt[:, c * 128:(c + 1) * 128],
                           qk_fm, lambda ii, c, j=j: c * SEQ + j * 128, 1, NC, pt0, pt1,
                           pe_sem="p1tp", dve_sem="p1td")
        P.wait("sp", "p1sq")

    P.wait("pe", "p1q2")
    for j in range(32):
        pp = psE if j % 2 == 0 else psF
        if j >= 2:
            P.wait_at("pe", "q2td", P.val("q2td") - 1)
        P.on("pe", lambda e, j=j, pp=pp: e.transpose(
            pp[0:16, 0:128], q2tm[:, j * NH:(j + 1) * NH], id_32[:, :]), inc="q2tp")
        P.wait("dve", "q2tp")
        P.on("dve", lambda e, pp=pp: e.reduce_max(m16[0:16, 1:2], pp[0:16, 0:128], axis=AX.X),
             inc="q2td")
        P.wait("dve", "q2td")
        P.on("dve", lambda e: e.tensor_scalar(m16[0:16, 0:1], m16[0:16, 1:2],
                                              m16[0:16, 0:1], None, ALU.max), inc="q2td")
        P.wait("dve", "q2td")
    P.wait("pe", "q2td")
    P.on("pe", lambda e: e.transpose(psE[0:1, 0:16], m16[0:16, 0:1], id_32[0:16, 0:16]),
         inc="q2mt")
    P.wait("dve", "q2mt")
    P.on("dve", lambda e: e.tensor_copy(mrow[0:1, 0:NH], psE[0:1, 0:16]), inc="q2mr")
    P.wait("pe", "q2mr")
    P.on("pe", lambda e: e.matmul(psF[:, 0:NH], ones1_32[0:1, :], mrow[0:1, 0:NH],
                                  start=True, stop=True), inc="q2mb")
    P.wait("dve", "q2mb")
    P.on("dve", lambda e: e.tensor_copy(mb[:, 0:NH], psF[:, 0:NH]), inc="q2mb2")
    P.wait("dve", "q2mb2")
    for j in range(32):
        P.on("dve", lambda e, j=j: e.tensor_tensor(
            biasall[:, j * NH:(j + 1) * NH], q2tm[:, j * NH:(j + 1) * NH], mb[:, 0:NH],
            ALU.add), inc="ba1" if j == 31 else None)
    P.wait("dve", "ba1")
    P.wait("pool", "ba1")
    for j in range(32):
        P.on("pool", lambda e, j=j: e.tensor_tensor(
            biasall[:, j * NH:(j + 1) * NH], biasall[:, j * NH:(j + 1) * NH], taub8[:, :],
            ALU.mult), inc="ba2" if j == 31 else None)
    P.wait("act", "ba2")
    P.wait("act", "p1ve")
    P.wait("act", "p1td")

    for h in range(NH):
        base = 64 * (h % 2)
        P.wait("pe", "tnorm")
        for qh in range(2):
            P.wait("pe", "tev")
            P.wait("dve", "trc")
            dcnt = 0
            for jt in range(32):
                dps = (psA, psB) if dcnt % 2 == 0 else (psC, psD)
                if dcnt >= 2:
                    P.wait_at("pe", "texp", P.val("texp") - 1)
                for qq in range(2):
                    P.on("pe", lambda e, dps=dps, h=h, base=base, jt=jt, qh=qh, qq=qq: e.matmul(
                        dps[qq][:, :],
                        qk_fm[base:base + 64,
                              (h // 2) * SEQ + jt * 128:(h // 2) * SEQ + (jt + 1) * 128],
                        qk_fm[base:base + 64,
                              (h // 2) * SEQ + qh * 1024 + qq * 512:
                              (h // 2) * SEQ + qh * 1024 + (qq + 1) * 512],
                        start=True, stop=True, tile_position=(base, 0)),
                        inc="tdot" if qq == 1 else None)
                P.wait("act", "tdot")
                pb = ptb[:, (dcnt % 2) * 1024:(dcnt % 2 + 1) * 1024]
                P.on("act", lambda e, dps=dps, pb=pb, h=h, jt=jt: e.activation(
                    pb[:, 0:512], dps[0][:, :], AF.Exp,
                    bias=biasall[:, jt * NH + h:jt * NH + h + 1], scale=taub[:, h:h + 1]))
                P.on("act", lambda e, dps=dps, pb=pb, h=h, jt=jt: e.activation(
                    pb[:, 512:1024], dps[1][:, :], AF.Exp,
                    bias=biasall[:, jt * NH + h:jt * NH + h + 1], scale=taub[:, h:h + 1]),
                    inc="texp")
                P.wait("pe", "texp")
                for qq in range(2):
                    pv = psE if qq == 0 else psF
                    P.on("pe", lambda e, pv=pv, pb=pb, h=h, jt=jt, qq=qq: e.matmul(
                        pv[0:65, :],
                        v65[:, jt * NH * 65 + h * 65:jt * NH * 65 + (h + 1) * 65],
                        pb[:, qq * 512:(qq + 1) * 512],
                        start=(jt == 0), stop=(jt == 31)),
                        inc="tpv" if (jt == 31 and qq == 1) else None)
                dcnt += 1
            P.wait("dve", "tpv")
            P.on("dve", lambda e, h=h, qh=qh, base=base: e.tensor_copy(
                attn_fm[base:base + 64,
                        (h // 2) * T + qh * 1024:(h // 2) * T + qh * 1024 + 512],
                psE[0:64, :]))
            P.on("dve", lambda e, h=h, qh=qh, base=base: e.tensor_copy(
                attn_fm[base:base + 64,
                        (h // 2) * T + qh * 1024 + 512:(h // 2) * T + (qh + 1) * 1024],
                psF[0:64, :]))
            P.on("dve", lambda e, qh=qh: e.tensor_copy(
                rssb[0:1, qh * 1024:qh * 1024 + 512], psE[64:65, :]))
            P.on("dve", lambda e, qh=qh: e.tensor_copy(
                rssb[0:1, qh * 1024 + 512:(qh + 1) * 1024], psF[64:65, :]), inc="tev")
        P.wait("act", "tev")
        act_pow(P, rssb[0:1, :], rssb[0:1, :], -1.0, "trc")
        P.wait("pe", "trc")
        for qq in range(4):
            pp = (psA, psB, psC, psD)[qq]
            P.on("pe", lambda e, pp=pp, qq=qq: e.matmul(
                pp[:, :], ones1b[0:1, :], rssb[0:1, qq * 512:(qq + 1) * 512],
                start=True, stop=True), inc="tnb")
        P.wait("dve", "tnb")
        for qq in range(4):
            pp = (psA, psB, psC, psD)[qq]
            P.on("dve", lambda e, pp=pp, qq=qq: e.tensor_copy(
                rsbb[:, qq * 512:(qq + 1) * 512], pp[:, :]), inc="tnc" if qq == 3 else None)
        P.wait("dve", "tnc")
        for qq in range(4):
            P.on("dve", lambda e, h=h, base=base, qq=qq: e.tensor_tensor(
                attn_fm[base:base + 64, (h // 2) * T + qq * 512:(h // 2) * T + (qq + 1) * 512],
                attn_fm[base:base + 64, (h // 2) * T + qq * 512:(h // 2) * T + (qq + 1) * 512],
                rsbb[base:base + 64, qq * 512:(qq + 1) * 512], ALU.mult),
                inc="tnorm" if qq == 3 else None)

    wob = v65[:, 0:NC * DM]
    t2fm = v65[:, NC * DM:NC * DM + NC * T]
    P.wait("pool", "tnorm")
    P.on("pool", lambda e: e.dma_start(
        out=wob.rearrange("p (c j) -> p c j", c=NC),
        in_=wo.ap().rearrange("(c p) j -> p c j", p=128)), inc=("dmap", 16))
    P.wait("pe", "dmap")
    for i in range(NT):
        P.on("sp", lambda e, i=i: e.dma_start(
            out=xtile[:, :], in_=x2[i * 128:(i + 1) * 128, :]), inc=("dma", 16))
        P.wait("pe", "wog")
        for half in range(2):
            pp = psE if half == 0 else psF
            for k in range(NC):
                P.on("pe", lambda e, pp=pp, i=i, k=k, half=half: e.matmul(
                    pp[:, :], attn_fm[:, k * T + i * 128:k * T + (i + 1) * 128],
                    wob[:, k * DM + half * 512:k * DM + half * 512 + 512],
                    start=(k == 0), stop=(k == NC - 1)),
                    inc="wop" if k == NC - 1 else None)
        P.wait("dve", "dma")
        emit_ln(P, 0, 0, xtile, tmpf, None, st, "ln", n_tiles=1, sem_suffix="d2")
        P.wait("dve", "wop")
        P.on("dve", lambda e: e.tensor_tensor(sqt[:, 0:512], psE[:, :], g1b[:, 0:512], ALU.mult))
        P.on("dve", lambda e: e.tensor_tensor(sqt[:, 512:1024], psF[:, :], g1b[:, 512:1024],
                                              ALU.mult), inc="wog")
        P.wait("dve", "wog")
        P.on("dve", lambda e: e.tensor_tensor(tmpf[:, :], tmpf[:, :], sqt[:, :], ALU.add),
             inc="xmid")
        P.wait("sp", "xmid")
        P.on("sp", lambda e, i=i: e.dma_start(
            out=xmido[i * 128:(i + 1) * 128, :], in_=tmpf[:, :]), inc=("dma", 16))
        emit_ln(P, 0, 0, tmpf, None, t1t, st, "ln", n_tiles=1, sem_suffix="d3")
        P.on("dve", lambda e: e.tensor_tensor(t1t[:, :], t1t[:, :], lnwb[:, 2 * DM:3 * DM],
                                              ALU.mult))
        P.on("dve", lambda e: e.tensor_tensor(t1t[:, :], t1t[:, :], lnwb[:, 3 * DM:4 * DM],
                                              ALU.add), inc="p3t2")
        P.wait("pe", "p3t2")
        transpose_tm_to_fm(P, lambda ii, c: t1t[:, c * 128:(c + 1) * 128],
                           t2fm, lambda ii, c, i=i: c * T + i * 128, 1, NC, pt0, pt1,
                           pe_sem="p3tp", dve_sem="p3td")
        P.wait("sp", "p3td")

    wqkcb = qk_fm[:, 0:NC * DM]
    qkctm = qk_fm[:, NC * DM:NC * DM + NT * DM]
    P.wait("pool", "p3td")
    P.on("pool", lambda e: e.dma_start(
        out=wqkcb.rearrange("p (c j) -> p c j", c=NC),
        in_=wqkc.ap().rearrange("(c p) j -> p c j", p=128)), inc=("dmap", 16))
    P.wait("pe", "dmap")
    nn = 0
    for i in range(NT):
        for half in range(2):
            pp = psA if nn % 2 == 0 else psB
            if nn >= 2:
                P.wait_at("pe", "p4e", P.val("p4e") - 1)
            for k in range(NC):
                P.on("pe", lambda e, pp=pp, i=i, k=k, half=half: e.matmul(
                    pp[:, :], t2fm[:, k * T + i * 128:k * T + (i + 1) * 128],
                    wqkcb[:, k * DM + half * 512:k * DM + half * 512 + 512],
                    start=(k == 0), stop=(k == NC - 1)),
                    inc="p4pe" if k == NC - 1 else None)
            P.wait("dve", "p4pe")
            P.on("dve", lambda e, pp=pp, i=i, half=half: e.tensor_copy(
                qkctm[:, i * DM + half * 512:i * DM + half * 512 + 512], pp[:, :]),
                inc="p4e")
            nn += 1
    P.wait("dve", "p4e")
    sq16 = t1f
    for i in range(NT):
        if i > 0:
            P.wait("dve", "p4q2m")
        P.on("dve", lambda e, i=i: e.tensor_tensor(
            sq16[:, :], qkctm[:, i * DM:(i + 1) * DM], qkctm[:, i * DM:(i + 1) * DM],
            ALU.mult), inc="p4sq")
        P.wait("pe", "p4sq")
        for half in range(2):
            pp = psC if half == 0 else psD
            P.on("pe", lambda e, pp=pp, i=i, half=half: e.matmul(
                pp[0:1, :], onescol[:, 0:1], sq16[:, half * 512:(half + 1) * 512],
                start=(i == 0), stop=(i == NT - 1)),
                inc="p4q2m" if half == 1 else None)
    P.wait("dve", "p4q2m")
    P.on("dve", lambda e: e.tensor_copy(q2crow[0:1, 0:512], psC[0:1, :]))
    P.on("dve", lambda e: e.tensor_copy(q2crow[0:1, 512:1024], psD[0:1, :]), inc="p4q2e")
    for g in range(NC):
        pp = psE if g % 2 == 0 else psF
        if g >= 2:
            P.wait_at("pe", "p4de", P.val("p4de") - 1)
        for i in range(NT):
            P.on("pe", lambda e, pp=pp, g=g, i=i: e.matmul(
                pp[:, 0:128], qkctm[:, i * DM + g * 128:i * DM + (g + 1) * 128],
                qkctm[:, i * DM + g * 128:i * DM + (g + 1) * 128],
                start=(i == 0), stop=(i == NT - 1)),
                inc="p4d" if i == NT - 1 else None)
        P.wait("dve", "p4d")
        P.on("dve", lambda e, pp=pp, g=g: e.tensor_copy(
            dots_sb[:, g * 128:(g + 1) * 128], pp[:, 0:128]), inc="p4de")
    P.wait("sp", "p4de")
    P.wait("sp", "p4q2e")
    P.on("sp", lambda e: e.dma_start(out=dotso.ap(), in_=dots_sb[:, :]), inc=("dma", 16))
    P.on("sp", lambda e: e.dma_start(out=q2co.ap(), in_=q2crow[0:1, :]), inc=("dma", 16))
    P.wait("sp", "dma")


def build_D():
    nc = bass.Bass("TRN2", target_bir_lowering=False)
    es = contextlib.ExitStack()
    with es:
        P = Pr(nc, es)
        io = {}
        x2 = nc.dram_tensor("x2", [SEQ, DM], F32, kind="ExternalInput")
        ln1w = nc.dram_tensor("ln1w", [1, DM], F32, kind="ExternalInput")
        ln1b = nc.dram_tensor("ln1b", [1, DM], F32, kind="ExternalInput")
        ln2w = nc.dram_tensor("ln2w", [1, DM], F32, kind="ExternalInput")
        ln2b = nc.dram_tensor("ln2b", [1, DM], F32, kind="ExternalInput")
        wqk = nc.dram_tensor("wqk", [DM, DM], F32, kind="ExternalInput")
        wv = nc.dram_tensor("wv", [DM, DM], F32, kind="ExternalInput")
        wo = nc.dram_tensor("wo", [DM, DM], F32, kind="ExternalInput")
        wqkc = nc.dram_tensor("wqkc", [DM, DM], F32, kind="ExternalInput")
        taur = nc.dram_tensor("taur", [1, NH], F32, kind="ExternalInput")
        g1r = nc.dram_tensor("g1r", [1, DM], F32, kind="ExternalInput")
        xmido = nc.dram_tensor("xmido", [T, DM], F32, kind="ExternalOutput")
        dotso = nc.dram_tensor("dotso", [128, NC * 128], F32, kind="ExternalOutput")
        q2co = nc.dram_tensor("q2co", [1, DM], F32, kind="ExternalOutput")

        io["x2"] = x2
        io["ln1w"] = ln1w
        io["ln1b"] = ln1b
        io["ln2w"] = ln2w
        io["ln2b"] = ln2b
        io["wqk"] = wqk
        io["wv"] = wv
        io["wo"] = wo
        io["wqkc"] = wqkc
        io["taur"] = taur
        io["g1r"] = g1r
        io["xmido"] = xmido
        io["dotso"] = dotso
        io["q2co"] = q2co
        emit_D(P, io)
        P.finalize()
    return nc



def emit_E(P, io, fused=False):
    nc = P.nc
    xmid = io["xmid"]
    x2own = io["x2own"]
    dotsa = io["dotsa"]
    dotsb = io["dotsb"]
    q2a = io["q2a"]
    q2b = io["q2b"]
    ln2w = io["ln2w"]
    ln2b = io["ln2b"]
    wvc = io["wvc"]
    woc = io["woc"]
    taucr = io["taucr"]
    g2r = io["g2r"]
    wg = io["wg"]
    x3o = io["x3o"]
    h1o = io["h1o"]
    h2o = io["h2o"]
    w12o = io["w12o"]
    cnto = io["cnto"]
    dbounce = io["dbounce"]
    ddot = io["ddot"]
    dq2c = io["dq2c"]
    dnb = io["dnb"]
    dvc = io["dvc"]
    dcd = io["dcd"]
    dt2 = io["dt2"]
    dlg = io["dlg"]
    dst2 = io["dst2"]
    t2fm = P.sbuf("t2fm", [128, NC * T], BF16)
    vcfm = P.sbuf("vcfm", [128, NC * T], BF16)
    cdfm = P.sbuf("cdfm", [128, NC * T], BF16)
    wvcb = P.sbuf("wvcb", [128, NC * DM], BF16)
    wocb = P.sbuf("wocb", [128, NC * DM], BF16)
    wgb = P.sbuf("wgb", [128, NC * NE], BF16)
    dot = P.sbuf("dot", [128, DM], F32)
    nb = P.sbuf("nb", [128, DM], BF16)
    q2colh = P.sbuf("q2colh", [128, NC], F32)
    q2row = P.sbuf("q2row", [1, DM], F32)
    taucb = P.sbuf("taucb", [128, NC], F32)
    lnwb = P.sbuf("lnwb", [128, 2 * DM], BF16)
    g2b = P.sbuf("g2b", [128, DM], BF16)
    grow = P.sbuf("grow", [1, DM], BF16)
    xtile = P.sbuf("xtile", [128, DM], F32)
    x3t = P.sbuf("x3t", [128, DM], F32)
    t1t = P.sbuf("t1t", [128, DM], BF16)
    t1f = P.sbuf("t1f", [128, DM], BF16)
    sqt = P.sbuf("sqt", [128, DM], F32)
    st = P.sbuf("st", [128, 8], F32)
    st2 = P.sbuf("st2", [128, 16], F32)
    lg = P.sbuf("lg", [128, 128], F32)
    pc = P.sbuf("pc", [128, 128], BF16)
    pnt = P.sbuf("pnt", [128, 128], BF16)
    gsc = P.sbuf("gsc", [128, 4 * NE], F32)
    h1sb = P.sbuf("h1sb", [128, NT * NE], F32)
    h2sb = P.sbuf("h2sb", [128, NT * NE], F32)
    w12sb = P.sbuf("w12sb", [128, NT * 2], F32)
    cntsb = P.sbuf("cntsb", [NE, 2], F32)
    onescol32 = P.sbuf("onescol32", [128, 1], F32)
    ones1b = P.sbuf("ones1b", [1, 128], BF16)
    ones1_32 = P.sbuf("ones1_32", [1, 128], F32)
    id_bf = P.sbuf("id_bf", [128, 128], BF16)
    P.bufs["ln_sq"] = sqt

    psA = P.psum("psA", [128, 512], F32)
    psB = P.psum("psB", [128, 512], F32)
    psC = P.psum("psC", [128, 512], F32)
    psD = P.psum("psD", [128, 512], F32)
    psE = P.psum("psE", [128, 512], F32)
    psF = P.psum("psF", [128, 512], F32)
    pt0 = P.psum("pt0", [128, 1024], BF16)
    pt1 = P.psum("pt1", [128, 1024], BF16)

    emit_identity(P, id_bf, None, scratch=t2fm)

    P.on("pool", lambda e: e.dma_start(
        out=wvcb.rearrange("p (c j) -> p c j", c=NC),
        in_=wvc.ap().rearrange("(c p) j -> p c j", p=128)), inc=("dmap", 16))
    P.on("pool", lambda e: e.dma_start(
        out=wocb.rearrange("p (c j) -> p c j", c=NC),
        in_=woc.ap().rearrange("(c p) j -> p c j", p=128)), inc=("dmap", 16))
    P.on("pool", lambda e: e.dma_start(
        out=wgb.rearrange("p (c j) -> p c j", c=NC),
        in_=wg.ap().rearrange("(c p) j -> p c j", p=128)), inc=("dmap", 16))
    P.on("sp", lambda e: e.dma_start(out=dot[:, :].rearrange("p (c j) -> p c j", c=NC) if False
                                     else dot[:, :], in_=dotsa[:, :]), inc=("dma", 16))
    P.on("sp", lambda e: e.dma_start(out=sqt[:, :], in_=dotsb[:, :]), inc=("dma", 16))
    P.on("sp", lambda e: e.dma_start(out=q2row[0:1, :], in_=q2a[:, :]), inc=("dma", 16))
    P.on("sp", lambda e: e.dma_start(out=x3t[0:1, 0:DM], in_=q2b[:, :]), inc=("dma", 16))
    P.on("sp", lambda e: e.dma_start(out=xtile[0:1, 0:8], in_=taucr[:, :]), inc=("dma", 16))
    P.wait("dve", "dma")
    P.on("dve", lambda e: e.memset(onescol32[:, :], 1.0))
    P.on("dve", lambda e: e.memset(ones1b[:, :], 1.0))
    P.on("dve", lambda e: e.memset(ones1_32[:, :], 1.0), inc="dve")
    P.on("dve", lambda e: e.tensor_tensor(dot[:, :], dot[:, :], sqt[:, :], ALU.add))
    P.on("dve", lambda e: e.tensor_tensor(q2row[0:1, :], q2row[0:1, :], x3t[0:1, 0:DM],
                                          ALU.add), inc="dve")
    P.wait("dve", "dve")
    P.on("dve", lambda e: e.tensor_scalar_mul(q2row[0:1, :], q2row[0:1, :], -0.5), inc="q2h")
    P.wait("sp", "q2h")
    P.on("sp", lambda e: e.dma_start(out=dbounce.ap().rearrange("(one j) -> one j", one=1),
                                     in_=q2row[0:1, :]), inc=("dma", 16))
    P.wait("sp", "dma")
    P.on("sp", lambda e: e.dma_start(out=q2colh[:, :], in_=dbounce.ap().rearrange(
        "(c p) -> p c", p=128), allow_slow_non_contiguous=True), inc=("dma", 16))
    P.wait("pe", "q2h")
    P.on("dve", lambda e: e.tensor_copy(grow[0:1, :], q2row[0:1, :]), inc="nbc")
    P.wait("pe", "nbc")
    emit_bcast_row(P, nb, grow[0:1, :], ones1b, psA, DM, "gb_dve", "gb_pe")
    P.wait("pe", "dma")
    P.on("pe", lambda e: e.matmul(psB[:, 0:8], ones1_32[0:1, :], xtile[0:1, 0:8],
                                  start=True, stop=True), inc="tc_pe")
    P.wait("dve", "tc_pe")
    P.on("dve", lambda e: e.tensor_scalar_mul(taucb[:, :], psB[:, 0:8], 1.0 / 32.0), inc="tc_d")
    P.wait("pe", "tc_d")
    for idx, src_t in enumerate([ln2w, ln2b, g2r]):
        P.wait("pool", "gb_dve")
        P.on("pool", lambda e, s=src_t: e.dma_start(out=grow[0:1, :], in_=s[:, :]),
             inc=("dmap", 16))
        P.wait("pe", "dmap")
        dst = lnwb[:, idx * DM:(idx + 1) * DM] if idx < 2 else g2b
        emit_bcast_row(P, dst, grow[0:1, :], ones1b, psA, DM, "gb_dve", "gb_pe")

    P.wait("sp", "tc_pe")
    for i in range(NT):
        P.on("sp", lambda e, i=i: e.dma_start(
            out=xtile[:, :], in_=xmid[i * 128:(i + 1) * 128, :]), inc=("dma", 16))
        P.wait("dve", "dma")
        emit_ln(P, 0, 0, xtile, None, t1t, st, "ln", n_tiles=1, sem_suffix="e1")
        P.on("dve", lambda e: e.tensor_tensor(t1t[:, :], t1t[:, :], lnwb[:, 0:DM], ALU.mult))
        P.on("dve", lambda e: e.tensor_tensor(t1t[:, :], t1t[:, :], lnwb[:, DM:2 * DM],
                                              ALU.add), inc="e1t")
        P.wait("pe", "e1t")
        transpose_tm_to_fm(P, lambda ii, c: t1t[:, c * 128:(c + 1) * 128],
                           t2fm, lambda ii, c, i=i: c * T + i * 128, 1, NC, pt0, pt1,
                           pe_sem="e1tp", dve_sem="e1td")
        P.wait("sp", "e1td")

    P.wait("pe", "dmap")
    nn = 0
    for c in range(NC):
        for tc in range(4):
            pp = (psA, psB, psC, psD)[nn % 4]
            if nn >= 4:
                P.wait_at("pe", "vce", P.val("vce") - 3)
            for k in range(NC):
                P.on("pe", lambda e, pp=pp, c=c, tc=tc, k=k: e.matmul(
                    pp[:, :], wvcb[:, k * DM + c * 128:k * DM + (c + 1) * 128],
                    t2fm[:, k * T + tc * 512:k * T + (tc + 1) * 512],
                    start=(k == 0), stop=(k == NC - 1)),
                    inc="vcp" if k == NC - 1 else None)
            P.wait("dve", "vcp")
            P.on("dve", lambda e, pp=pp, c=c, tc=tc: e.tensor_copy(
                vcfm[:, c * T + tc * 512:c * T + (tc + 1) * 512], pp[:, :]), inc="vce")
            nn += 1
    P.wait("pe", "vce")

    for g in range(NC):
        P.on("dve", lambda e, g=g: e.tensor_tensor(
            lg[:, :], dot[:, g * 128:(g + 1) * 128], nb[:, g * 128:(g + 1) * 128], ALU.add),
            inc="cd0")
        P.wait("dve", "cd0")
        P.on("dve", lambda e, g=g: e.tensor_scalar(
            lg[:, :], lg[:, :], q2colh[:, g:g + 1], None, ALU.add), inc="cd1")
        P.wait("dve", "cd1")
        P.on("dve", lambda e: e.reduce_max(st2[:, 0:1], lg[:, :], axis=AX.X), inc="cd2")
        P.wait("dve", "cd2")
        P.on("dve", lambda e, g=g: e.tensor_scalar(
            st2[:, 1:2], st2[:, 0:1], taucb[:, g:g + 1], -1.0, ALU.mult, ALU.mult), inc="cd3")
        if DEBUG:
            P.on("dve", lambda e, g=g: e.tensor_copy(t2fm.bitcast(F32)[:, 6 * 1024 + g * 128:6 * 1024 + (g + 1) * 128], lg[:, :]))
        P.wait("act", "cd3")
        P.wait("act", "cdtp")
        P.on("act", lambda e, g=g: e.activation(
            pc[:, :], lg[:, :], AF.Exp, bias=st2[:, 1:2], scale=taucb[:, g:g + 1],
            accum_out=st2[:, 2:3]), inc="cdexp")
        P.wait("act", "cdexp")
        act_pow(P, st2[:, 3:4], st2[:, 2:3], -1.0, "cdrc")
        P.wait("dve", "cdrc")
        if DEBUG:
            P.on("dve", lambda e, g=g: e.tensor_copy(
                t2fm.bitcast(F32)[:, 7 * 1024 + g * 4:7 * 1024 + g * 4 + 4], st2[:, 0:4]),
                inc="dbg")
            P.on("dve", lambda e, g=g: e.tensor_copy(
                t2fm.bitcast(F32)[:, 7 * 1024 + 40 + g:7 * 1024 + 40 + g + 1],
                taucb[:, g:g + 1]), inc="dbg")
        P.on("dve", lambda e: e.tensor_scalar_mul(pc[:, :], pc[:, :], st2[:, 3:4]), inc="cdn")
        P.wait("pe", "cdn")
        P.on("pe", lambda e: e.transpose(pt0[:, 0:128], pc[:, :], id_bf[:, :]), inc="cdtp")
        P.wait("dve", "cdtp")
        P.on("dve", lambda e: e.tensor_copy(pnt[:, :], pt0[:, 0:128]), inc="cdte")
        P.wait("pe", "cdte")
        for tc in range(4):
            pp = psE if tc % 2 == 0 else psF
            if tc >= 2:
                P.wait_at("pe", "cdoe", P.val("cdoe") - 1)
            P.on("pe", lambda e, pp=pp, g=g, tc=tc: e.matmul(
                pp[:, :], pnt[:, :], vcfm[:, g * T + tc * 512:g * T + (tc + 1) * 512],
                start=True, stop=True), inc="cdo")
            P.wait("dve", "cdo")
            P.on("dve", lambda e, pp=pp, g=g, tc=tc: e.tensor_copy(
                cdfm[:, g * T + tc * 512:g * T + (tc + 1) * 512], pp[:, :]), inc="cdoe")
        P.wait("pe", "cdoe")

    if DEBUG:
        P.wait("sp", "cdoe")
        P.on("sp", lambda e: e.dma_start(out=ddot.ap(), in_=dot[:, :]), inc=("dma", 16))
        P.on("sp", lambda e: e.dma_start(out=dq2c.ap(), in_=q2colh[:, :]), inc=("dma", 16))
        P.on("sp", lambda e: e.dma_start(out=dnb.ap(), in_=nb[:, :]), inc=("dma", 16))
        P.on("sp", lambda e: e.dma_start(out=dvc.ap(), in_=vcfm[:, :]), inc=("dma", 16))
        P.on("sp", lambda e: e.dma_start(out=dcd.ap(), in_=cdfm[:, :]), inc=("dma", 16))
        P.on("sp", lambda e: e.dma_start(out=dt2.ap(), in_=t2fm[:, :]), inc=("dma", 16))
        P.wait("sp", "dbg")
        P.on("sp", lambda e: e.dma_start(out=dlg.ap(), in_=t2fm.bitcast(F32)[:, 6 * 1024:7 * 1024]), inc=("dma", 16))
        P.on("sp", lambda e: e.dma_start(out=dst2.ap(), in_=t2fm.bitcast(F32)[:, 7 * 1024:7 * 1024 + 48]), inc=("dma", 16))
        P.wait("sp", "dma")

    for i in range(NT):
        P.on("sp", lambda e, i=i: e.dma_start(
            out=xtile[:, :], in_=x2own[i * 128:(i + 1) * 128, :]), inc=("dma", 16))
        P.on("sp", lambda e, i=i: e.dma_start(
            out=sqt[:, :], in_=xmid[i * 128:(i + 1) * 128, :]), inc=("dma", 16))
        P.wait("pe", "woe")
        for half in range(2):
            pp = psA if half == 0 else psB
            for k in range(NC):
                P.on("pe", lambda e, pp=pp, i=i, k=k, half=half: e.matmul(
                    pp[:, :], cdfm[:, k * T + i * 128:k * T + (i + 1) * 128],
                    wocb[:, k * DM + half * 512:k * DM + half * 512 + 512],
                    start=(k == 0), stop=(k == NC - 1)),
                    inc="wom" if k == NC - 1 else None)
        P.wait("dve", "dma")
        P.wait("dve", "wom")
        P.on("dve", lambda e: e.tensor_tensor(x3t[:, 0:512], psA[:, :], g2b[:, 0:512],
                                              ALU.mult))
        P.on("dve", lambda e: e.tensor_tensor(x3t[:, 512:DM], psB[:, :], g2b[:, 512:DM],
                                              ALU.mult), inc="woe")
        P.wait("dve", "woe")
        P.on("dve", lambda e: e.tensor_tensor(x3t[:, :], x3t[:, :], sqt[:, :], ALU.add))
        P.on("dve", lambda e: e.tensor_tensor(x3t[:, :], x3t[:, :], xtile[:, :], ALU.add),
             inc="x3d")
        P.wait("sp", "x3d")
        P.on("sp", lambda e, i=i: e.dma_start(
            out=x3o[i * 128:(i + 1) * 128, :], in_=x3t[:, :]), inc=("dma", 16))
        emit_ln(P, 0, 0, x3t, None, t1t, st, "ln", n_tiles=1, sem_suffix="e2")
        P.on("dve", lambda e: e.engine_nop(), inc="e2t")
        P.wait("pe", "e2t")
        transpose_tm_to_fm(P, lambda ii, c: t1t[:, c * 128:(c + 1) * 128],
                           t1f, lambda ii, c: c * 128, 1, NC, pt0, pt1,
                           pe_sem="e2tp", dve_sem="e2td")
        P.wait("pe", "g_exp")
        P.on("pe", lambda e: [e.matmul(psE[:, 0:NE], t1f[:, k * 128:(k + 1) * 128],
                                       wgb[:, k * NE:(k + 1) * NE],
                                       start=(k == 0), stop=(k == NC - 1))
                              for k in range(NC)][-1], inc="g_pe")
        P.wait("dve", "g_pe")
        P.on("dve", lambda e: e.reduce_max(st2[:, 4:5], psE[:, 0:NE], axis=AX.X), inc="g_m")
        P.wait("dve", "g_m")
        P.on("dve", lambda e: e.tensor_scalar_mul(st2[:, 5:6], st2[:, 4:5], -1.0), inc="g_m2")
        P.wait("act", "g_m2")
        P.wait("act", "g_done")
        P.on("act", lambda e: e.activation(
            gsc[:, 0:NE], psE[:, 0:NE], AF.Exp, bias=st2[:, 5:6], scale=1.0,
            accum_out=st2[:, 6:7]), inc="g_exp")
        P.wait("act", "g_exp")
        act_pow(P, st2[:, 7:8], st2[:, 6:7], -1.0, "g_rc")
        P.wait("dve", "g_rc")
        P.on("dve", lambda e: e.tensor_scalar_mul(gsc[:, 0:NE], gsc[:, 0:NE], st2[:, 7:8]),
             inc="g_1")
        P.wait("dve", "g_1")
        P.on("dve", lambda e: e.reduce_max(st2[:, 8:9], gsc[:, 0:NE], axis=AX.X), inc="g_2")
        P.wait("dve", "g_2")
        P.on("dve", lambda e, i=i: e.tensor_scalar(
            h1sb[:, i * NE:(i + 1) * NE], gsc[:, 0:NE], st2[:, 8:9], None, ALU.is_equal),
             inc="g_3")
        P.wait("dve", "g_3")
        P.on("dve", lambda e, i=i: e.tensor_tensor(
            gsc[:, NE:2 * NE], gsc[:, 0:NE], h1sb[:, i * NE:(i + 1) * NE], ALU.mult),
             inc="g_4")
        P.wait("dve", "g_4")
        P.on("dve", lambda e: e.tensor_tensor(
            gsc[:, 2 * NE:3 * NE], gsc[:, 0:NE], gsc[:, NE:2 * NE], ALU.subtract), inc="g_5")
        P.wait("dve", "g_5")
        P.on("dve", lambda e: e.reduce_max(st2[:, 9:10], gsc[:, 2 * NE:3 * NE], axis=AX.X),
             inc="g_6")
        P.wait("dve", "g_6")
        P.on("dve", lambda e, i=i: e.tensor_scalar(
            h2sb[:, i * NE:(i + 1) * NE], gsc[:, 2 * NE:3 * NE], st2[:, 9:10], None,
            ALU.is_equal), inc="g_7")
        P.wait("dve", "g_7")
        P.on("dve", lambda e: e.tensor_scalar(st2[:, 10:11], st2[:, 8:9], st2[:, 9:10], None,
                                              ALU.add), inc="g_8")
        P.wait("dve", "g_8")
        P.on("dve", lambda e: e.tensor_scalar_max(st2[:, 10:11], st2[:, 10:11], 1e-9),
             inc="g_9")
        P.wait("act", "g_9")
        act_pow(P, st2[:, 11:12], st2[:, 10:11], -1.0, "g_10")
        P.wait("dve", "g_10")
        P.on("dve", lambda e, i=i: e.tensor_scalar(
            w12sb[:, i * 2:i * 2 + 1], st2[:, 8:9], st2[:, 11:12], None, ALU.mult))
        P.on("dve", lambda e, i=i: e.tensor_scalar(
            w12sb[:, i * 2 + 1:i * 2 + 2], st2[:, 9:10], st2[:, 11:12], None, ALU.mult),
             inc="g_done")
        P.wait("pe", "g_done")
        P.on("pe", lambda e, i=i: e.matmul(
            psC[0:NE, 0:1], h1sb[:, i * NE:(i + 1) * NE], onescol32[:, 0:1],
            start=(i == 0), stop=(i == NT - 1)))
        P.on("pe", lambda e, i=i: e.matmul(
            psD[0:NE, 0:1], h2sb[:, i * NE:(i + 1) * NE], onescol32[:, 0:1],
            start=(i == 0), stop=(i == NT - 1)),
             inc="g_cnt" if i == NT - 1 else None)
        P.wait("sp", "e2td")
    P.wait("dve", "g_cnt")
    P.on("dve", lambda e: e.tensor_copy(cntsb[0:NE, 0:1], psC[0:NE, 0:1]))
    P.on("dve", lambda e: e.tensor_copy(cntsb[0:NE, 1:2], psD[0:NE, 0:1]), inc="cnte")
    P.wait("sp", "cnte")
    P.wait("sp", "g_done")
    P.on("sp", lambda e: e.dma_start(out=cnto.ap(), in_=cntsb[:, :]), inc=("dma", 16))
    P.on("sp", lambda e: e.dma_start(
        out=h1o.ap().rearrange("(t p) e -> p t e", p=128),
        in_=h1sb[:, :].rearrange("p (t e) -> p t e", t=NT)), inc=("dma", 16))
    P.on("sp", lambda e: e.dma_start(
        out=h2o.ap().rearrange("(t p) e -> p t e", p=128),
        in_=h2sb[:, :].rearrange("p (t e) -> p t e", t=NT)), inc=("dma", 16))
    P.on("sp", lambda e: e.dma_start(
        out=w12o.ap().rearrange("(t p) e -> p t e", p=128),
        in_=w12sb[:, :].rearrange("p (t e) -> p t e", t=NT)), inc=("dma", 16))
    P.wait("sp", "dma")


def build_E():
    nc = bass.Bass("TRN2", target_bir_lowering=False)
    es = contextlib.ExitStack()
    with es:
        P = Pr(nc, es)
        io = {}
        xmid = nc.dram_tensor("xmid", [T, DM], F32, kind="ExternalInput")
        x2own = nc.dram_tensor("x2own", [T, DM], F32, kind="ExternalInput")
        dotsa = nc.dram_tensor("dotsa", [128, NC * 128], F32, kind="ExternalInput")
        dotsb = nc.dram_tensor("dotsb", [128, NC * 128], F32, kind="ExternalInput")
        q2a = nc.dram_tensor("q2a", [1, DM], F32, kind="ExternalInput")
        q2b = nc.dram_tensor("q2b", [1, DM], F32, kind="ExternalInput")
        ln2w = nc.dram_tensor("ln2w", [1, DM], F32, kind="ExternalInput")
        ln2b = nc.dram_tensor("ln2b", [1, DM], F32, kind="ExternalInput")
        wvc = nc.dram_tensor("wvc", [DM, DM], F32, kind="ExternalInput")
        woc = nc.dram_tensor("woc", [DM, DM], F32, kind="ExternalInput")
        taucr = nc.dram_tensor("taucr", [1, 8], F32, kind="ExternalInput")
        g2r = nc.dram_tensor("g2r", [1, DM], F32, kind="ExternalInput")
        wg = nc.dram_tensor("wg", [DM, NE], F32, kind="ExternalInput")
        x3o = nc.dram_tensor("x3o", [T, DM], F32, kind="ExternalOutput")
        h1o = nc.dram_tensor("h1o", [T, NE], F32, kind="ExternalOutput")
        h2o = nc.dram_tensor("h2o", [T, NE], F32, kind="ExternalOutput")
        w12o = nc.dram_tensor("w12o", [T, 2], F32, kind="ExternalOutput")
        cnto = nc.dram_tensor("cnto", [NE, 2], F32, kind="ExternalOutput")
        dbounce = nc.dram_tensor("dbounce", [DM], F32)
        if DEBUG:
            ddot = nc.dram_tensor("ddot", [128, DM], F32, kind="ExternalOutput")
            dq2c = nc.dram_tensor("dq2c", [128, NC], F32, kind="ExternalOutput")
            dnb = nc.dram_tensor("dnb", [128, DM], BF16, kind="ExternalOutput")
            dvc = nc.dram_tensor("dvc", [128, NC * T], BF16, kind="ExternalOutput")
            dcd = nc.dram_tensor("dcd", [128, NC * T], BF16, kind="ExternalOutput")
            dt2 = nc.dram_tensor("dt2", [128, NC * T], BF16, kind="ExternalOutput")
            dlg = nc.dram_tensor("dlg", [128, NC * 128], F32, kind="ExternalOutput")
            dst2 = nc.dram_tensor("dst2", [128, 48], F32, kind="ExternalOutput")

        io["xmid"] = xmid
        io["x2own"] = x2own
        io["dotsa"] = dotsa
        io["dotsb"] = dotsb
        io["q2a"] = q2a
        io["q2b"] = q2b
        io["ln2w"] = ln2w
        io["ln2b"] = ln2b
        io["wvc"] = wvc
        io["woc"] = woc
        io["taucr"] = taucr
        io["g2r"] = g2r
        io["wg"] = wg
        io["x3o"] = x3o
        io["h1o"] = h1o
        io["h2o"] = h2o
        io["w12o"] = w12o
        io["cnto"] = cnto
        io["dbounce"] = dbounce
        io["ddot"] = ddot if DEBUG else None
        io["dq2c"] = dq2c if DEBUG else None
        io["dnb"] = dnb if DEBUG else None
        io["dvc"] = dvc if DEBUG else None
        io["dcd"] = dcd if DEBUG else None
        io["dt2"] = dt2 if DEBUG else None
        io["dlg"] = dlg if DEBUG else None
        io["dst2"] = dst2 if DEBUG else None
        emit_E(P, io)
        P.finalize()
    return nc



def emit_G(P, io, stage="full", fused=False):
    nc = P.nc
    x3 = io["x3"]
    h1i = io["h1i"]
    h2i = io["h2i"]
    w12i = io["w12i"]
    cnt64 = io["cnt64"]
    mlt64 = io["mlt64"]
    mall64 = io["mall64"]
    tri = io["tri"]
    w1 = io["w1"]
    b1 = io["b1"]
    w2 = io["w2"]
    b2 = io["b2"]
    outo = io["outo"]
    x_res = P.sbuf("x_res", [128, NT * DM], F32)
    xn_fm = P.sbuf("xn_fm", [128, NC * T], BF16)
    hbuf = P.sbuf("hbuf", [128, 16 * 512], BF16)
    w1b = P.sbuf("w1b", [128, NC * 2048], BF16)
    w2b = P.sbuf("w2b", [128, 16 * DM], BF16)
    tri_sb = P.sbuf("tri_sb", [128, 128], BF16)
    h1sb = P.sbuf("h1sb", [128, NT * NE], F32)
    h2sb = P.sbuf("h2sb", [128, NT * NE], F32)
    h1b = P.sbuf("h1b", [128, NT * NE], BF16)
    h2b = P.sbuf("h2b", [128, NT * NE], BF16)
    w12sb = P.sbuf("w12sb", [128, NT * 2], F32)
    csb = P.sbuf("csb", [128, NT * NE], F32)
    cnt_sb = P.sbuf("cnt_sb", [64, 2], F32)
    mltsb = P.sbuf("mltsb", [64, NE], F32)
    mallsb = P.sbuf("mallsb", [64, NE], F32)
    runs = P.sbuf("runs", [1, 4 * NE], F32)
    b1t = P.sbuf("b1t", [128, 32 * NE], F32)
    b1sb = P.sbuf("b1sb", [NE, 128], F32)
    b2row = P.sbuf("b2row", [1, DM], F32)
    kp = P.sbuf("kp", [128, 4 * NE], F32)
    st = P.sbuf("st", [128, 8], F32)
    xn16t = P.sbuf("xn16t", [128, DM], BF16)
    ytmp = P.sbuf("ytmp", [128, DM], F32)
    sqt = P.sbuf("sqt", [128, DM], F32)
    onesr32 = P.sbuf("onesr32", [1, 128], F32)
    onescol32 = P.sbuf("onescol32", [128, 1], F32)
    id_bf = P.sbuf("id_bf", [128, 128], BF16)
    id_32 = P.sbuf("id_32", [128, 128], F32)
    P.bufs["ln_sq"] = sqt

    psA = P.psum("psA", [128, 512], F32)
    psB = P.psum("psB", [128, 512], F32)
    psC = P.psum("psC", [128, 512], F32)
    psD = P.psum("psD", [128, 512], F32)
    psE = P.psum("psE", [128, 512], F32)
    psF = P.psum("psF", [128, 512], F32)
    pt0 = P.psum("pt0", [128, 1024], BF16)
    pt1 = P.psum("pt1", [128, 1024], BF16)

    emit_identity(P, id_bf, id_32, scratch=xn_fm)

    P.on("sp", lambda e: e.dma_start(
        out=x_res[:, :].rearrange("p (t j) -> p t j", t=NT),
        in_=x3.ap().rearrange("(t p) j -> p t j", p=128)), inc=("dma", 16))
    P.on("sp", lambda e: e.dma_start(
        out=h1sb[:, :].rearrange("p (t e) -> p t e", t=NT),
        in_=h1i.ap().rearrange("(t p) e -> p t e", p=128)), inc=("dma", 16))
    P.on("sp", lambda e: e.dma_start(
        out=h2sb[:, :].rearrange("p (t e) -> p t e", t=NT),
        in_=h2i.ap().rearrange("(t p) e -> p t e", p=128)), inc=("dma", 16))
    P.on("sp", lambda e: e.dma_start(
        out=w12sb[:, :].rearrange("p (t e) -> p t e", t=NT),
        in_=w12i.ap().rearrange("(t p) e -> p t e", p=128)), inc=("dma", 16))
    P.on("sp", lambda e: e.dma_start(out=cnt_sb[:, :], in_=cnt64.ap() if hasattr(cnt64, "ap") else cnt64), inc=("dma", 16))
    P.on("sp", lambda e: e.dma_start(out=mltsb[:, :], in_=mlt64[:, :]), inc=("dma", 16))
    P.on("sp", lambda e: e.dma_start(out=mallsb[:, :], in_=mall64[:, :]), inc=("dma", 16))
    P.on("sp", lambda e: e.dma_start(out=tri_sb[:, :], in_=tri[:, :]), inc=("dma", 16))
    P.wait("dve", "dma")
    P.on("dve", lambda e: e.memset(onesr32[:, :], 1.0))
    P.on("dve", lambda e: e.memset(onescol32[:, :], 1.0))
    P.on("dve", lambda e: e.tensor_copy(h1b[:, :], h1sb[:, :]))
    P.on("dve", lambda e: e.tensor_copy(h2b[:, :], h2sb[:, :]), inc="dve")
    P.wait("pe", "dve")

    for hc in range(32):
        pp = psE if hc % 2 == 0 else psF
        P.wait("sp", "b1td")
        P.on("sp", lambda e, hc=hc: e.dma_start(
            out=b1sb[:, :], in_=b1[:, hc * 128:(hc + 1) * 128]), inc=("dma", 16))
        P.wait("pe", "dma")
        if hc >= 2:
            P.wait_at("pe", "b1td", P.val("b1td") - 1)
        P.on("pe", lambda e, hc=hc, pp=pp: e.transpose(
            pp[0:128, 0:NE], b1sb[0:NE, 0:128], id_32[0:NE, 0:NE]), inc="b1tp")
        P.wait("dve", "b1tp")
        P.on("dve", lambda e, hc=hc, pp=pp: e.tensor_copy(
            b1t[:, hc * NE:(hc + 1) * NE], pp[0:128, 0:NE]), inc="b1td")

    P.wait("pe", "b1td")
    P.on("pe", lambda e: e.matmul(psA[0:1, 0:NE], cnt_sb[:, 0:1], mltsb[:, :],
                                  start=True, stop=True))
    P.on("pe", lambda e: e.matmul(psB[0:1, 0:NE], cnt_sb[:, 0:1], mallsb[:, :],
                                  start=True, stop=False))
    P.on("pe", lambda e: e.matmul(psB[0:1, 0:NE], cnt_sb[:, 1:2], mltsb[:, :],
                                  start=False, stop=True), inc="bse")
    P.wait("dve", "bse")
    P.on("dve", lambda e: e.tensor_copy(runs[0:1, 2 * NE:3 * NE], psA[0:1, 0:NE]))
    P.on("dve", lambda e: e.tensor_copy(runs[0:1, 3 * NE:4 * NE], psB[0:1, 0:NE]))
    P.on("dve", lambda e: e.tensor_copy(runs[0:1, 0:NE], psA[0:1, 0:NE]))
    P.on("dve", lambda e: e.tensor_copy(runs[0:1, NE:2 * NE], psB[0:1, 0:NE]), inc="rn")
    P.wait("pe", "rn")

    if stage == "prep":
        P.wait("sp", "rn")
        P.on("sp", lambda e: e.dma_start(
            out=outo.ap().rearrange("(t p) j -> p t j", p=128),
            in_=x_res[:, :].rearrange("p (t j) -> p t j", t=NT)), inc=("dma", 16))
        P.wait("sp", "dma")
    for i in range(NT):
        P.wait("dve", "g_tp")
        emit_ln(P, i * DM, 0, x_res, None, xn16t, st, "ln", n_tiles=1, sem_suffix="g1",
                inc_last="g_ln")
        P.wait("pe", "g_ln")
        transpose_tm_to_fm(P, lambda ii, c: xn16t[:, c * 128:(c + 1) * 128],
                           xn_fm, lambda ii, c, i=i: c * T + i * 128, 1, NC, pt0, pt1,
                           pe_sem="g_tp2", dve_sem="g_tp")
        for kk in range(2):
            pp = psC if kk == 0 else psD
            hb = h1b if kk == 0 else h2b
            P.on("pe", lambda e, pp=pp, hb=hb, i=i: e.matmul(
                pp[:, 0:NE], tri_sb[:, :], hb[:, i * NE:(i + 1) * NE],
                start=True, stop=False))
            P.on("pe", lambda e, pp=pp, kk=kk: e.matmul(
                pp[:, 0:NE], onesr32[0:1, :], runs[0:1, kk * NE:(kk + 1) * NE],
                start=False, stop=True), inc="pos")
            pq = psE if kk == 0 else psF
            hs = h1sb if kk == 0 else h2sb
            P.on("pe", lambda e, pq=pq, hs=hs, i=i: e.matmul(
                pq[0:1, 0:NE], onescol32[:, 0:1], hs[:, i * NE:(i + 1) * NE],
                start=True, stop=True), inc="tot")
        P.wait("dve", "pos")
        for kk in range(2):
            pp = psC if kk == 0 else psD
            hsb = h1sb if kk == 0 else h2sb
            P.on("dve", lambda e, pp=pp, kk=kk: e.tensor_scalar(
                kp[:, kk * NE:(kk + 1) * NE], pp[:, 0:NE], CAP + 0.5, None, ALU.is_lt))
        P.on("dve", lambda e: e.engine_nop(), inc="kp1")
        P.wait("dve", "kp1")
        for kk in range(2):
            hsb = h1sb if kk == 0 else h2sb
            P.on("dve", lambda e, hsb=hsb, kk=kk, i=i: e.tensor_tensor(
                kp[:, (2 + kk) * NE:(3 + kk) * NE], kp[:, kk * NE:(kk + 1) * NE],
                hsb[:, i * NE:(i + 1) * NE], ALU.mult))
        P.on("dve", lambda e: e.engine_nop(), inc="kp2")
        P.wait("dve", "kp2")
        P.on("dve", lambda e, i=i: e.tensor_scalar(
            kp[:, 2 * NE:3 * NE], kp[:, 2 * NE:3 * NE], w12sb[:, i * 2:i * 2 + 1], None,
            ALU.mult))
        P.on("dve", lambda e, i=i: e.tensor_scalar(
            kp[:, 3 * NE:4 * NE], kp[:, 3 * NE:4 * NE], w12sb[:, i * 2 + 1:i * 2 + 2], None,
            ALU.mult), inc="kp3")
        P.wait("dve", "kp3")
        P.on("dve", lambda e, i=i: e.tensor_tensor(
            csb[:, i * NE:(i + 1) * NE], kp[:, 2 * NE:3 * NE], kp[:, 3 * NE:4 * NE],
            ALU.add), inc="kp4")
        P.wait("dve", "tot")
        P.on("dve", lambda e: e.tensor_tensor(runs[0:1, 0:NE], runs[0:1, 0:NE],
                                              psE[0:1, 0:NE], ALU.add))
        P.on("dve", lambda e: e.tensor_tensor(runs[0:1, NE:2 * NE], runs[0:1, NE:2 * NE],
                                              psF[0:1, 0:NE], ALU.add), inc="rn2")
        P.wait("pe", "rn2")
    P.wait("pe", "kp4")
    P.wait("pe", "rn2")

    if stage == "route":
        P.wait("sp", "kp4")
        P.on("sp", lambda e: e.dma_start(
            out=outo[0:128, 0:NT * NE], in_=csb[:, :]), inc=("dma", 16))
        P.wait("sp", "dma")
    hcnt = 0
    NEx = 1 if stage in ("e1", "e1nb2", "e1noy") else NE
    for e_ in range(NEx):
        P.wait("sp", "g_ym")
        P.on("sp", lambda e, e_=e_: e.dma_start(out=b2row[0:1, :], in_=b2[e_:e_ + 1, :]),
             inc=("dma", 16))
        for half in range(2):
            P.wait("pool", "g_hm")
            P.on("pool", lambda e, e_=e_, half=half: e.dma_start(
                out=w1b.rearrange("p (c j) -> p c j", c=NC),
                in_=w1.ap()[e_].rearrange("(c p) j -> p c j", p=128)
                [:, :, half * 2048:(half + 1) * 2048]), inc=("dmap", 16))
            P.wait("pool", "g_ym")
            P.on("pool", lambda e, e_=e_, half=half: e.dma_start(
                out=w2b.rearrange("p (c j) -> p c j", c=16),
                in_=w2.ap()[e_].rearrange("(c p) j -> p c j", p=128)
                [:, half * 16:(half + 1) * 16, :]), inc=("dmap", 16))
            P.wait("pe", "dmap")
            P.wait("pe", "dma")
            for tc in range(4):
                for hc in range(16):
                    pp = psA if hcnt % 2 == 0 else psB
                    if hcnt >= 2:
                        P.wait_at("pe", "g_ha", P.val("g_ha") - 1)
                    for k in range(NC):
                        P.on("pe", lambda e, pp=pp, tc=tc, hc=hc, k=k: e.matmul(
                            pp[:, :], w1b[:, k * 2048 + hc * 128:k * 2048 + (hc + 1) * 128],
                            xn_fm[:, k * T + tc * 512:k * T + tc * 512 + 512],
                            start=(k == 0), stop=(k == NC - 1)),
                            inc="g_hm" if k == NC - 1 else None)
                    P.wait("act", "g_hm")
                    P.on("act", lambda e, pp=pp, hc=hc, half=half, e_=e_: e.activation(
                        hbuf[:, hc * 512:(hc + 1) * 512], pp[:, :], AF.Gelu,
                        bias=b1t[:, (half * 16 + hc) * NE + e_:(half * 16 + hc) * NE + e_ + 1]),
                        inc="g_ha")
                    hcnt += 1
                P.wait("pe", "g_ha")
                if stage == "e1noy":
                    continue
                for ts in range(4):
                    pp0, pp1 = (psC, psD) if ts % 2 == 0 else (psE, psF)
                    P.wait_at("pe", "g_ya", max(0, P.val("g_ya") - 1))
                    fold_b2 = (half == 1 and stage != "e1nb2")
                    for k in range(16):
                        last = (k == 15) and not fold_b2
                        P.on("pe", lambda e, pp0=pp0, ts=ts, k=k, last=last: e.matmul(
                            pp0[:, :], hbuf[:, k * 512 + ts * 128:k * 512 + ts * 128 + 128],
                            w2b[:, k * DM:k * DM + 512],
                            start=(k == 0), stop=(k == 15 and not fold_b2)))
                        P.on("pe", lambda e, pp1=pp1, ts=ts, k=k: e.matmul(
                            pp1[:, :], hbuf[:, k * 512 + ts * 128:k * 512 + ts * 128 + 128],
                            w2b[:, k * DM + 512:(k + 1) * DM],
                            start=(k == 0), stop=(k == 15 and not fold_b2)),
                            inc="g_ym" if last else None)
                    if fold_b2:
                        P.on("pe", lambda e, pp0=pp0: e.matmul(
                            pp0[:, :], onesr32[0:1, :],
                            b2row[0:1, 0:512], start=False, stop=True))
                        P.on("pe", lambda e, pp1=pp1: e.matmul(
                            pp1[:, :], onesr32[0:1, :], b2row[0:1, 512:DM],
                            start=False, stop=True), inc="g_ym")
                    P.wait("act", "g_ym")
                    t_g = tc * 4 + ts
                    P.wait("act", "g_yd")
                    P.on("act", lambda e, pp0=pp0, t_g=t_g, e_=e_: e.activation(
                        ytmp[:, 0:512], pp0[:, :], AF.Copy,
                        scale=csb[:, t_g * NE + e_:t_g * NE + e_ + 1]))
                    P.on("act", lambda e, pp1=pp1, t_g=t_g, e_=e_: e.activation(
                        ytmp[:, 512:DM], pp1[:, :], AF.Copy,
                        scale=csb[:, t_g * NE + e_:t_g * NE + e_ + 1]), inc="g_ya")
                    P.wait("dve", "g_ya")
                    P.on("dve", lambda e, t_g=t_g: e.tensor_tensor(
                        x_res[:, t_g * DM:(t_g + 1) * DM], x_res[:, t_g * DM:(t_g + 1) * DM],
                        ytmp[:, :], ALU.add), inc="g_yd")
    if stage == "e1noy":
        P.wait("sp", "g_ha")
    else:
        P.wait("sp", "g_yd")
    P.on("sp", lambda e: e.dma_start(
        out=outo.ap().rearrange("(t p) j -> p t j", p=128),
        in_=x_res[:, :].rearrange("p (t j) -> p t j", t=NT)), inc=("dma", 16))
    P.wait("sp", "dma")


def build_G(stage="full"):
    nc = bass.Bass("TRN2", target_bir_lowering=False)
    es = contextlib.ExitStack()
    with es:
        P = Pr(nc, es)
        io = {}
        x3 = nc.dram_tensor("x3", [T, DM], F32, kind="ExternalInput")
        h1i = nc.dram_tensor("h1i", [T, NE], F32, kind="ExternalInput")
        h2i = nc.dram_tensor("h2i", [T, NE], F32, kind="ExternalInput")
        w12i = nc.dram_tensor("w12i", [T, 2], F32, kind="ExternalInput")
        cnt64 = nc.dram_tensor("cnt64", [64, 2], F32, kind="ExternalInput")
        mlt64 = nc.dram_tensor("mlt64", [64, NE], F32, kind="ExternalInput")
        mall64 = nc.dram_tensor("mall64", [64, NE], F32, kind="ExternalInput")
        tri = nc.dram_tensor("tri", [128, 128], BF16, kind="ExternalInput")
        w1 = nc.dram_tensor("w1", [NE, DM, HI], F32, kind="ExternalInput")
        b1 = nc.dram_tensor("b1", [NE, HI], F32, kind="ExternalInput")
        w2 = nc.dram_tensor("w2", [NE, HI, DM], F32, kind="ExternalInput")
        b2 = nc.dram_tensor("b2", [NE, DM], F32, kind="ExternalInput")
        outo = nc.dram_tensor("outo", [T, DM], F32, kind="ExternalOutput")

        io["x3"] = x3
        io["h1i"] = h1i
        io["h2i"] = h2i
        io["w12i"] = w12i
        io["cnt64"] = cnt64
        io["mlt64"] = mlt64
        io["mall64"] = mall64
        io["tri"] = tri
        io["w1"] = w1
        io["b1"] = b1
        io["w2"] = w2
        io["b2"] = b2
        io["outo"] = outo
        emit_G(P, io, stage=stage)
        P.finalize()
    return nc



def _rope_tables():
    inv = 1.0 / 10000.0 ** (np.arange(0, DH, 2, dtype=np.float64) / DH)
    pos = np.arange(SEQ, dtype=np.float64)[:, None] * inv[None, :]
    cos = np.cos(pos).astype(np.float32)
    sin = np.sin(pos).astype(np.float32)
    A = np.zeros((SEQ, DM), np.float32)
    Bt = np.zeros((SEQ, DM), np.float32)
    for h in range(NH):
        A[:, h * 64:h * 64 + 32] = cos
        A[:, h * 64 + 32:h * 64 + 64] = cos
        Bt[:, h * 64:h * 64 + 32] = -sin
        Bt[:, h * 64 + 32:h * 64 + 64] = sin
    return A, Bt


def run_A(x_full, Wq, Wk, Wv):
    A, Bt = _rope_tables()
    xs = x_full.reshape(NCORES, T, DM)
    nc = build_A()
    in_maps = [{"x": np.ascontiguousarray(xs[c]), "wq": Wq, "wk": Wk, "wv": Wv,
                "ra": np.ascontiguousarray(A[(c % 2) * T:(c % 2 + 1) * T]),
                "rb": np.ascontiguousarray(Bt[(c % 2) * T:(c % 2 + 1) * T])}
               for c in range(NCORES)]
    res = run_bass_kernel_spmd(nc, in_maps, core_ids=list(range(NCORES)))
    q = np.stack([res.results[c]["qo"] for c in range(NCORES)]).reshape(B, SEQ, DM)
    k = np.stack([res.results[c]["ko"] for c in range(NCORES)]).reshape(B, SEQ, DM)
    v = np.stack([res.results[c]["vo"] for c in range(NCORES)]).reshape(B, SEQ, DM)
    return q, k, v


def run_BC(x_full, q, k, v, Wout, gamma, beta, router_w, W1, W2):
    xs = x_full.reshape(NCORES, T, DM)
    qs = np.asarray(q, dtype=BF).reshape(NCORES, T, DM)
    kk = np.asarray(k, dtype=BF)
    vv = np.asarray(v, dtype=BF)
    nc = build_BC()
    in_maps = []
    for c in range(NCORES):
        b = c // 2
        pair = c ^ 1
        maskrow = np.zeros((1, SEQ), np.float32)
        if c % 2 == 0:
            maskrow[0, :NG] = -10000.0
        else:
            maskrow[0, T:T + NG] = -10000.0
        kk2 = kk.reshape(NCORES, T, DM)
        vv2 = vv.reshape(NCORES, T, DM)
        kp = np.concatenate([kk2[c], kk2[pair]], axis=0)
        vp = np.concatenate([vv2[c], vv2[pair]], axis=0)
        in_maps.append({
            "x": np.ascontiguousarray(xs[c]), "qin": np.ascontiguousarray(qs[c]),
            "kseq": np.ascontiguousarray(kp), "vseq": np.ascontiguousarray(vp),
            "wout": Wout, "maskrow": maskrow,
            "flagc": np.full((128, 1), 1.0 - (c % 2), np.float32),
            "gam": gamma.reshape(1, DM), "bet": beta.reshape(1, DM),
            "rw": router_w, "w1": W1, "w2": W2,
        })
    res = run_bass_kernel_spmd(nc, in_maps, core_ids=list(range(NCORES)))
    return np.stack([res.results[c]["x2o"] for c in range(NCORES)]).reshape(B, SEQ, DM)


def run_D(x2_full, Wqk, Wv, Wo, tau, ln1w, ln1b, ln2w, ln2b, Wqkc, g1):
    xs = x2_full.reshape(NCORES, T, DM)
    nc = build_D()
    in_maps = []
    for c in range(NCORES):
        pair = c ^ 1
        x2p = np.concatenate([xs[c], xs[pair]], axis=0)
        in_maps.append({
            "x2": np.ascontiguousarray(x2p),
            "ln1w": ln1w.reshape(1, DM), "ln1b": ln1b.reshape(1, DM),
            "ln2w": ln2w.reshape(1, DM), "ln2b": ln2b.reshape(1, DM),
            "wqk": Wqk, "wv": Wv, "wo": Wo, "wqkc": Wqkc,
            "taur": np.ascontiguousarray(tau.reshape(1, NH)),
            "g1r": g1.reshape(1, DM),
        })
    res = run_bass_kernel_spmd(nc, in_maps, core_ids=list(range(NCORES)))
    xmid = np.stack([res.results[c]["xmido"] for c in range(NCORES)])
    dots = np.stack([res.results[c]["dotso"] for c in range(NCORES)])
    q2c = np.stack([res.results[c]["q2co"] for c in range(NCORES)])
    return xmid, dots, q2c


def run_E(xmid, x2_full, dots, q2c, ln2w, ln2b, Wvc, Woc, tauc, g2, wg):
    xs = x2_full.reshape(NCORES, T, DM)
    nc = build_E()
    in_maps = []
    for c in range(NCORES):
        pair = c ^ 1
        in_maps.append({
            "xmid": np.ascontiguousarray(xmid[c]), "x2own": np.ascontiguousarray(xs[c]),
            "dotsa": np.ascontiguousarray(dots[c]), "dotsb": np.ascontiguousarray(dots[pair]),
            "q2a": np.ascontiguousarray(q2c[c]), "q2b": np.ascontiguousarray(q2c[pair]),
            "ln2w": ln2w.reshape(1, DM), "ln2b": ln2b.reshape(1, DM),
            "wvc": Wvc, "woc": Woc, "taucr": np.ascontiguousarray(tauc.reshape(1, 8)),
            "g2r": g2.reshape(1, DM), "wg": wg,
        })
    res = run_bass_kernel_spmd(nc, in_maps, core_ids=list(range(NCORES)))
    x3 = np.stack([res.results[c]["x3o"] for c in range(NCORES)])
    h1 = np.stack([res.results[c]["h1o"] for c in range(NCORES)])
    h2 = np.stack([res.results[c]["h2o"] for c in range(NCORES)])
    w12 = np.stack([res.results[c]["w12o"] for c in range(NCORES)])
    cnt = np.stack([res.results[c]["cnto"] for c in range(NCORES)])
    return x3, h1, h2, w12, cnt


def run_G(x3, h1, h2, w12, cnt, w1, b1, w2, b2, stage="full"):
    nc = build_G(stage)
    tri = np.tril(np.ones((128, 128), np.float32)).T.astype(BF)
    cnt64 = np.ascontiguousarray(cnt.reshape(64, 2).astype(np.float32))
    eye = np.eye(NE, dtype=np.float32)
    in_maps = []
    for c in range(NCORES):
        lt = (np.arange(NCORES) < c).astype(np.float32)
        mlt64 = (lt[:, None, None] * eye[None, :, :]).reshape(64, NE)
        mall64 = np.tile(eye, (NCORES, 1)).reshape(64, NE)
        in_maps.append({
            "x3": np.ascontiguousarray(x3[c]), "h1i": np.ascontiguousarray(h1[c]),
            "h2i": np.ascontiguousarray(h2[c]), "w12i": np.ascontiguousarray(w12[c]),
            "cnt64": cnt64, "mlt64": np.ascontiguousarray(mlt64),
            "mall64": np.ascontiguousarray(mall64), "tri": tri,
            "w1": w1, "b1": b1, "w2": w2, "b2": b2,
        })
    res = run_bass_kernel_spmd(nc, in_maps, core_ids=list(range(NCORES)))
    return np.stack([res.results[c]["outo"] for c in range(NCORES)])


def kernel(**inputs):
    x = np.ascontiguousarray(inputs["L0_attn_x"], dtype=np.float32)
    q, k, v = run_A(x, inputs["L0_attn_Wq"], inputs["L0_attn_Wk"], inputs["L0_attn_Wv"])
    x2 = run_BC(x, q, k, v, inputs["L0_attn_Wout"], inputs["L0_ffn_gamma"],
                inputs["L0_ffn_beta"], inputs["L0_ffn_router_w"],
                inputs["L0_ffn_W1"], inputs["L0_ffn_W2"])
    xmid, dots, q2c = run_D(x2.reshape(-1, DM), inputs["L1_attn_Wqk_t"],
                            inputs["L1_attn_Wv_t"], inputs["L1_attn_Wo_t"],
                            inputs["L1_attn_tau_t"], inputs["L1_attn_ln1_w"],
                            inputs["L1_attn_ln1_b"], inputs["L1_attn_ln2_w"],
                            inputs["L1_attn_ln2_b"], inputs["L1_attn_Wqk_c"],
                            inputs["L1_attn_gamma1"])
    x3, h1, h2, w12, cnt = run_E(xmid, x2.reshape(-1, DM), dots, q2c,
                                 inputs["L1_attn_ln2_w"], inputs["L1_attn_ln2_b"],
                                 inputs["L1_attn_Wv_c"], inputs["L1_attn_Wo_c"],
                                 inputs["L1_attn_tau_c"], inputs["L1_attn_gamma2"],
                                 inputs["L1_ffn_wg"])
    out = run_G(x3, h1, h2, w12, cnt, inputs["L1_ffn_w1"], inputs["L1_ffn_b1"],
                inputs["L1_ffn_w2"], inputs["L1_ffn_b2"])
    return np.ascontiguousarray(out.reshape(B, SEQ, DM), dtype=np.float32)


if __name__ == "__main__":
    pass
